# version 11
# baseline (speedup 1.0000x reference)
import contextlib
import numpy as np
import concourse.bass as bass
import concourse.mybir as mybir
from concourse.bass_utils import run_bass_kernel_spmd

F32 = mybir.dt.float32
BF16 = mybir.dt.bfloat16
FP8 = mybir.dt.float8e4
AF = mybir.ActivationFunctionType
ALU = mybir.AluOpType
AX = mybir.AxisListType

MAGIC = 12582912.0
ENGINES = ("sync", "vector", "scalar", "tensor", "gpsimd")
D = 1024
NCH = D // 128


class Ctr:
    def __init__(self, es, nc, name):
        self.h = es.enter_context(nc.semaphore(name))
        self.n = 0

    def inc(self, amt=1):
        self.n += amt
        return self.n


class Prog:
    def __init__(self, nc):
        self.nc = nc
        self.steps = {k: [] for k in ENGINES}

    def barrier_step(self, waits):
        waits = [w for w in waits if w[1]]
        for k in ENGINES:
            def fn(e, waits=tuple(waits)):
                for (h, n) in waits:
                    e.wait_ge(h, n)
            self.steps[k].append(fn)

    def sy(self, fn): self.steps["sync"].append(fn)
    def v(self, fn): self.steps["vector"].append(fn)
    def s(self, fn): self.steps["scalar"].append(fn)
    def t(self, fn): self.steps["tensor"].append(fn)
    def g(self, fn): self.steps["gpsimd"].append(fn)

    def emit(self, block):
        for key in ENGINES:
            fns = self.steps[key]
            if not fns:
                continue
            dec = getattr(block, key)

            def make(fns):
                def _(eng):
                    for f in fns:
                        f(eng)
                return _
            dec(make(fns))


def wge(e, h, n):
    if n:
        e.wait_ge(h, n)


class Arena:
    def __init__(self, nc, es, name, dtype, n_elems):
        self.t = es.enter_context(nc.sbuf_tensor(name, [128, n_elems], dtype))
        self.n = n_elems
        self.off = 0
        self.high = n_elems

    def carve(self, n):
        n_al = (n + 31) & ~31
        assert self.off + n_al <= self.high, \
            f"arena overflow {self.off}+{n_al}>{self.high}"
        ap = self.t[:, self.off:self.off + n]
        self.off += n_al
        return ap

    def carve_high(self, n):
        n_al = (n + 31) & ~31
        assert self.off <= self.high - n_al, "arena high overflow"
        self.high -= n_al
        return self.t[:, self.high:self.high + n]

    def reset(self):
        self.off = 0
        self.high = self.n


def build_consts(nc, es, P, gch):
    ident_f32 = es.enter_context(nc.sbuf_tensor("ident_f32", [128, 128], F32))
    ident_bf = es.enter_context(nc.sbuf_tensor("ident_bf", [128, 128], BF16))
    ones_bf = es.enter_context(nc.sbuf_tensor("ones_bf", [128, 1], BF16))
    ones_row = es.enter_context(nc.sbuf_tensor("ones_row", [1, 128], F32))

    def g_fn(g):
        g.memset(ident_bf[:], 1.0).then_inc(gch.h, 1)
        g.memset(ident_f32[:], 1.0).then_inc(gch.h, 1)
        g.memset(ones_bf[:], 1.0)
        g.memset(ones_row[:], 1.0)
        g.wait_ge(gch.h, 2)
        g.affine_select(out=ident_bf[:], in_=ident_bf[:], pattern=[[-1, 128]],
                        compare_op=ALU.is_equal, fill=0.0,
                        base=0, channel_multiplier=1)
        g.affine_select(out=ident_f32[:], in_=ident_f32[:], pattern=[[-1, 128]],
                        compare_op=ALU.is_equal, fill=0.0,
                        base=0, channel_multiplier=1).then_inc(gch.h, 1)
    P.g(g_fn)
    gch.inc(3)
    return dict(ident_bf=ident_bf, ident_f32=ident_f32, ones_bf=ones_bf,
                ones_row=ones_row, ready=gch.n)


def load_weights_bf(P, stage, names, ldw, vch):
    out = []
    prev_cast = 0
    for (wbf, ap, c_out) in names:
        hn = NCH // 2
        for hf in range(2):
            src = ap[hf * (D // 2):(hf + 1) * (D // 2), :].rearrange(
                "(n p) m -> p n m", p=128)
            t_ld = ldw.inc(16)

            def sy_fn(e, src=src, c_out=c_out, prev_cast=prev_cast, hn=hn):
                wge(e, vch.h, prev_cast)
                e.dma_start(
                    out=stage[:, 0:hn * c_out].rearrange("p (n m) -> p n m", n=hn),
                    in_=src).then_inc(ldw.h, 16)
            P.sy(sy_fn)
            t_cast = vch.inc(1)

            def v_fn(e, wbf=wbf, c_out=c_out, t_ld=t_ld, hf=hf, hn=hn):
                e.wait_ge(ldw.h, t_ld)
                e.tensor_copy(out=wbf[:, hf * hn * c_out:(hf + 1) * hn * c_out],
                              in_=stage[:, 0:hn * c_out]).then_inc(vch.h, 1)
            P.v(v_fn)
            prev_cast = t_cast
        out.append(prev_cast)
    return out


def emit_ln_blocks(P, bufs, vch, sch, *, src, dst, blocks, war_s=None):
    stats, mv, veps, rstd, negmr = (bufs[k] for k in
                                    ("stats", "mv", "veps", "rstd", "negmr"))
    nblk = len(blocks)

    def v_stats(e):
        for j, (sc, dc, r) in enumerate(blocks):
            e.bn_stats(out=stats[0:r, 12 * j:12 * j + 6],
                       in_=src[0:r, sc:sc + 512])
            ins = e.bn_stats(out=stats[0:r, 12 * j + 6:12 * j + 12],
                             in_=src[0:r, sc + 512:sc + D])
        ins.then_inc(vch.h, 1)
    P.v(v_stats)
    t_stats = vch.inc(1)

    def v_aggr(e):
        e.wait_ge(vch.h, t_stats)
        for j, (sc, dc, r) in enumerate(blocks):
            ins = e.bn_aggr(out=mv[0:r, 2 * j:2 * j + 2],
                            in_=stats[0:r, 12 * j:12 * j + 12])
        ins.then_inc(vch.h, 1)
    P.v(v_aggr)
    t_aggr = vch.inc(1)

    def v_veps(e):
        e.wait_ge(vch.h, t_aggr)
        for j, (sc, dc, r) in enumerate(blocks):
            ins = e.tensor_scalar_add(out=veps[0:r, j:j + 1],
                                      in0=mv[0:r, 2 * j + 1:2 * j + 2], scalar1=1e-5)
        ins.then_inc(vch.h, 1)
    P.v(v_veps)
    t_veps = vch.inc(1)

    maxr = max(r for (_, _, r) in blocks)

    def s_ln(e):
        e.wait_ge(vch.h, t_veps)
        e.activation(out=rstd[0:maxr, 0:nblk], in_=veps[0:maxr, 0:nblk],
                     func=AF.Ln).then_inc(sch.h, 1)
    P.s(s_ln)
    t_ln = sch.inc(1)

    def s_exp(e):
        e.wait_ge(sch.h, t_ln)
        e.activation(out=rstd[0:maxr, 0:nblk], in_=rstd[0:maxr, 0:nblk],
                     func=AF.Exp, scale=-0.5).then_inc(sch.h, 1)
    P.s(s_exp)
    t_rstd = sch.inc(1)

    def v_negmr(e):
        e.wait_ge(sch.h, t_rstd)
        for j, (sc, dc, r) in enumerate(blocks):
            ins = e.scalar_tensor_tensor(out=negmr[0:r, j:j + 1],
                                         in0=mv[0:r, 2 * j:2 * j + 1], scalar=-1.0,
                                         in1=rstd[0:r, j:j + 1],
                                         op0=ALU.mult, op1=ALU.mult)
        ins.then_inc(vch.h, 1)
    P.v(v_negmr)
    t_negmr = vch.inc(1)

    def s_apply(e):
        e.wait_ge(vch.h, t_negmr)
        if war_s is not None:
            h, n = war_s
            wge(e, h, n)
        for j, (sc, dc, r) in enumerate(blocks):
            ins = e.activation(out=dst[0:r, dc:dc + D],
                               in_=src[0:r, sc:sc + D],
                               func=AF.Identity, bias=negmr[0:r, j:j + 1],
                               scale=rstd[0:r, j:j + 1])
        ins.then_inc(sch.h, 1)
    P.s(s_apply)
    return sch.inc(1)


def emit_ln(P, bufs, vch, sch, *, nblk, src, srcol, dst, dstcol, war_s=None):
    blocks = [(srcol(j), dstcol(j), 128) for j in range(nblk)]
    return emit_ln_blocks(P, bufs, vch, sch, src=src, dst=dst, blocks=blocks,
                          war_s=war_s)


def mk_ln_bufs(nc, es, maxblk):
    return dict(
        stats=es.enter_context(nc.sbuf_tensor("lnstats", [128, 12 * maxblk], F32)),
        mv=es.enter_context(nc.sbuf_tensor("lnmv", [128, 2 * maxblk], F32)),
        veps=es.enter_context(nc.sbuf_tensor("lnveps", [128, maxblk], F32)),
        rstd=es.enter_context(nc.sbuf_tensor("lnrstd", [128, maxblk], F32)),
        negmr=es.enter_context(nc.sbuf_tensor("lnnegmr", [128, maxblk], F32)),
    )



def build_l0attn(T_own, T_full, QG=512, KSC=512, G=None, io=None):
    standalone = G is None
    if standalone:
        nc = bass.Bass("TRN2", target_bir_lowering=False)
        pre = ""
    else:
        nc = G["nc"]
        pre = "a_"
    io = io or {}

    def dram(name, shape, dt, kind):
        if name in io:
            return io[name]
        return nc.dram_tensor(name, shape, dt, kind=kind).ap()

    x_full = dram("x_full", [T_full, D], F32, "ExternalInput")
    x_own = dram("x_own", [T_own, D], F32, "ExternalInput")
    wqT = dram("wqT", [D, D], F32, "ExternalInput")
    wkT = dram("wkT", [D, D], F32, "ExternalInput")
    wvT = dram("wvT", [D, D], F32, "ExternalInput")
    s1 = dram("s1", [T_own, D], F32, "ExternalOutput")

    kfm_d = nc.dram_tensor("kfm_d", [NCH, 128, T_full], BF16, kind="Internal")
    vtm_d = nc.dram_tensor("vtm_d", [T_full, 1025], BF16, kind="Internal")
    qfm_d = nc.dram_tensor("qfm_d", [NCH, 128, T_own], BF16, kind="Internal")

    TILE = 512
    NBLK = TILE // 128
    nt_full = T_full // TILE
    nt_own = T_own // TILE
    n_ksc = T_full // KSC
    n_kc = KSC // 128
    n_groups = T_own // QG
    NSUB = QG // 128
    NCHU_F = T_full // 128

    if standalone:
        es = contextlib.ExitStack()
        P = Prog(nc)
    else:
        es, P = G["es"], G["P"]
        P.barrier_step(G["entry"])

    vch = Ctr(es, nc, pre + "vch")
    sch = Ctr(es, nc, pre + "sch")
    tch = Ctr(es, nc, pre + "tch")
    gch = Ctr(es, nc, pre + "gch")
    ldw = Ctr(es, nc, pre + "ldw")
    ldx = [Ctr(es, nc, f"{pre}ldx{p}") for p in range(2)]
    splp = [Ctr(es, nc, f"{pre}splp{p}") for p in range(2)]
    splv = [Ctr(es, nc, f"{pre}splv{p}") for p in range(2)]
    ldqg = Ctr(es, nc, pre + "ldqg")
    ldkv = [Ctr(es, nc, f"{pre}ldkv{p}") for p in range(2)]
    ldxr = Ctr(es, nc, pre + "ldxr")
    st = Ctr(es, nc, pre + "st")

    if standalone:
        consts = build_consts(nc, es, P, gch)
        t_consts_h, t_consts = gch.h, consts["ready"]
    else:
        consts = G["consts"]
        t_consts_h, t_consts = G["consts_sem"], G["consts_ready"]
    ident_f32 = consts["ident_f32"]
    ident_bf = consts["ident_bf"]
    ones_bf = consts["ones_bf"]
    ones_row = consts["ones_row"]

    if standalone:
        abf = Arena(nc, es, "abf", BF16, 66000)
        af32 = Arena(nc, es, "af32", F32, 19200)
    else:
        abf, af32 = G["abf"], G["af32"]
        abf.reset()
        af32.reset()

    wq_bf = abf.carve(NCH * D)
    wk_bf = abf.carve(NCH * D)
    wv_bf = abf.carve(NCH * D)
    xfm = [abf.carve(NCH * TILE) for _ in range(2)]
    xbf1 = abf.carve(NBLK * D)
    xbf = [xbf1, xbf1]
    prj = [abf.carve(NCH * TILE) for _ in range(2)]
    vsb = [abf.carve(NBLK * 1025) for _ in range(2)]
    a1_bf_end = abf.off
    xst = [af32.carve(NBLK * D) for _ in range(2)]
    wstage = abf.carve(NCH * D).bitcast(F32)
    a1_f32_end = af32.off

    ksq = [abf.carve_high(512) for _ in range(2)]
    lnb = (G["lnb"] if not standalone else mk_ln_bufs(nc, es, 4))
    normq = af32.carve_high(max(T_own, 512))[0:1, :]
    _misc = af32.carve_high(96)
    nkv = _misc[0:1, 0:512] if False else af32.carve_high(512)[0:1, :]
    nk1 = _misc[0:1, 0:1]
    mkrun = _misc[0:1, 1:2]
    mksc = _misc[0:1, 2:3]
    m_q = normq
    tsb0 = af32.carve_high(QG)
    tsb = [tsb0, tsb0]
    m_bc = af32.carve_high(QG)
    rden = _misc[:, 32:36]

    W = load_weights_bf(P, wstage, [(wq_bf, wqT, D), (wk_bf, wkT, D),
                                    (wv_bf, wvT, D)], ldw, vch)
    t_wq, t_wk, t_wv = W

    if standalone:
        psA = es.enter_context(nc.psum_tensor("psA", [128, 512], F32))[:]
        psB = es.enter_context(nc.psum_tensor("psB", [128, 512], F32))[:]
        psO0 = es.enter_context(nc.psum_tensor("psO0", [128, 1536], F32))[:]
        psO1 = es.enter_context(nc.psum_tensor("psO1", [128, 1536], F32))[:]
    else:
        psA, psB = G["psA"], G["psB"]
        psO0 = G["psO0"]
        psO1 = G["psO1"]
    psAB = [psA, psB]
    psO = [psO0[:, 0:1024], psO1[:, 0:1024]]
    psDen = psO1[:, 1024:1536]

    def v_init(e):
        e.memset(vsb[0][:, 1024::1025], 1.0)
        e.memset(vsb[1][:, 1024::1025], 1.0)
        e.memset(mkrun[:], 0.0).then_inc(vch.h, 1)
    P.v(v_init)
    t_vinit = vch.inc(1)

    bank_free = {0: 0, 1: 0}
    state = {}
    apply_done = {}

    def emit_tile(i, gi, src_ap, pass_kind):
        p = gi % 2
        t_apfree = apply_done.get(gi - 2, 0)
        t_tp_prev = state.get(f"tp_{gi - 1}", 0)
        src = src_ap[i * TILE:(i + 1) * TILE, :].rearrange("(j p) c -> p j c", p=128)

        def sy_load(e, src=src, p=p, t_apfree=t_apfree, t_tp_prev=t_tp_prev):
            wge(e, sch.h, t_apfree)
            wge(e, tch.h, t_tp_prev)
            e.dma_start(out=xst[p].rearrange("p (j c) -> p j c", j=NBLK),
                        in_=src).then_inc(ldx[p].h, 16)
        P.sy(sy_load)
        t_ldx = ldx[p].inc(16)

        def v_predma(e, p=p, t_ldx=t_ldx):
            e.wait_ge(ldx[p].h, t_ldx)
        P.v(v_predma)
        t_apply = emit_ln(P, lnb, vch, sch, nblk=NBLK,
                          src=xst[p], srcol=lambda j: j * D,
                          dst=xbf[p], dstcol=lambda j: j * D,
                          war_s=(tch.h, t_tp_prev))
        apply_done[gi] = t_apply

        for j in range(NBLK):
            for h in range(2):
                bk = (j * 2 + h) % 2
                t_bankfree = bank_free[bk]

                def t_tp(e, p=p, j=j, h=h, bk=bk, t_bankfree=t_bankfree,
                         t_ap=t_apply):
                    e.wait_ge(sch.h, t_ap)
                    wge(e, t_consts_h, t_consts)
                    wge(e, vch.h, t_bankfree)
                    pbf = psAB[bk].bitcast(BF16)
                    for cc in range(4):
                        c = 4 * h + cc
                        ins = e.transpose(
                            out=pbf[:, cc * 128:(cc + 1) * 128],
                            in_=xbf[p][:, j * D + c * 128:j * D + (c + 1) * 128],
                            identity=ident_bf[:])
                    ins.then_inc(tch.h, 1)
                P.t(t_tp)
                t_tpg = tch.inc(1)

                def v_evac_tp(e, p=p, j=j, h=h, bk=bk, t_tpg=t_tpg):
                    e.wait_ge(tch.h, t_tpg)
                    dst = xfm[p].rearrange("p (c t) -> p c t", c=NCH)
                    e.tensor_copy(out=dst[:, 4 * h:4 * h + 4, j * 128:(j + 1) * 128],
                                  in_=psAB[bk].bitcast(BF16)[:, 0:512]
                                  .rearrange("p (c t) -> p c t", c=4)
                                  ).then_inc(vch.h, 1)
                P.v(v_evac_tp)
                bank_free[bk] = vch.inc(1)
        state[f"tp_{gi}"] = tch.n
        t_xfm_ready = vch.n

        if pass_kind == "kv":
            plist = [("k", wk_bf, t_wk, None), ("v", wv_bf, t_wv, None)]
        else:
            plist = [("q", wq_bf, t_wq, 1.0 / 32.0)]

        for (nm, wbf, t_w, scale) in plist:
            if nm in ("k", "q"):
                spl_sem = splp[p]
                t_spl_prev = state.get(f"splp_done_{gi - 2}", 0)
                for e_ch in range(NCH):
                    bk = e_ch % 2
                    t_bankfree = bank_free[bk]

                    def t_proj(e, p=p, e_ch=e_ch, bk=bk, wbf=wbf, t_w=t_w,
                               t_bankfree=t_bankfree, t_x=t_xfm_ready):
                        e.wait_ge(vch.h, max(t_x, t_w, t_bankfree))
                        for c in range(NCH):
                            ins = e.matmul(
                                out=psAB[bk][:, 0:TILE],
                                lhsT=wbf[:, c * D + e_ch * 128:c * D + (e_ch + 1) * 128],
                                rhs=xfm[p][:, c * TILE:(c + 1) * TILE],
                                start=(c == 0), stop=(c == NCH - 1))
                        ins.then_inc(tch.h, 1)
                    P.t(t_proj)
                    t_mm = tch.inc(1)

                    def v_evac(e, p=p, e_ch=e_ch, bk=bk, scale=scale, t_mm=t_mm,
                               spl_sem=spl_sem, t_spl_prev=t_spl_prev):
                        e.wait_ge(tch.h, t_mm)
                        if e_ch == 0:
                            wge(e, spl_sem.h, t_spl_prev)
                        if scale is None:
                            e.tensor_copy(out=prj[p][:, e_ch * TILE:(e_ch + 1) * TILE],
                                          in_=psAB[bk][:, 0:TILE]).then_inc(vch.h, 1)
                        else:
                            e.tensor_scalar_mul(out=prj[p][:, e_ch * TILE:(e_ch + 1) * TILE],
                                                in0=psAB[bk][:, 0:TILE],
                                                scalar1=scale).then_inc(vch.h, 1)
                    P.v(v_evac)
                    bank_free[bk] = vch.inc(1)
                    t_prj_e = bank_free[bk]

                    kq = ksq[e_ch % 2]
                    t_nk_prev = state.get(f"nk_{(e_ch - 2) % NCH}", 0)

                    def v_ksq(e, p=p, e_ch=e_ch, kq=kq, t_prj_e=t_prj_e,
                              t_nk_prev=t_nk_prev):
                        e.wait_ge(vch.h, t_prj_e)
                        wge(e, tch.h, t_nk_prev)
                        e.tensor_tensor(out=kq[:, 0:TILE],
                                        in0=prj[p][:, e_ch * TILE:(e_ch + 1) * TILE],
                                        in1=prj[p][:, e_ch * TILE:(e_ch + 1) * TILE],
                                        op=ALU.mult).then_inc(vch.h, 1)
                    P.v(v_ksq)
                    t_ksq = vch.inc(1)

                    t_o0 = state.get("t_o0_free", 0)

                    def t_nk(e, e_ch=e_ch, kq=kq, t_ksq=t_ksq, t_o0=t_o0):
                        e.wait_ge(vch.h, t_ksq)
                        wge(e, t_consts_h, t_consts)
                        if e_ch == 0:
                            wge(e, vch.h, t_o0)
                        e.matmul(out=psO0[0:1, 0:TILE], lhsT=ones_bf[:],
                                 rhs=kq[:, 0:TILE], start=(e_ch == 0),
                                 stop=(e_ch == NCH - 1)).then_inc(tch.h, 1)
                    P.t(t_nk)
                    state[f"nk_{e_ch}"] = tch.inc(1)
                t_nk_last = state[f"nk_{NCH - 1}"]

                if pass_kind == "q":
                    def v_nk_evac(e, i=i, t_nk_last=t_nk_last):
                        e.wait_ge(tch.h, t_nk_last)
                        e.tensor_copy(out=normq[0:1, i * TILE:(i + 1) * TILE],
                                      in_=psO0[0:1, 0:TILE]).then_inc(vch.h, 1)
                    P.v(v_nk_evac)
                    state["t_o0_free"] = vch.inc(1)
                else:
                    t_red_prev = state.get("t_nk_red", 0)

                    def v_nk_evac(e, t_nk_last=t_nk_last, t_red_prev=t_red_prev):
                        e.wait_ge(tch.h, t_nk_last)
                        wge(e, vch.h, t_red_prev)
                        e.tensor_copy(out=nkv[:], in_=psO0[0:1, 0:TILE]
                                      ).then_inc(vch.h, 1)
                    P.v(v_nk_evac)
                    state["t_o0_free"] = vch.inc(1)
                    t_ev = state["t_o0_free"]

                    def v_nk_red(e, t_ev=t_ev, first=(i == 0)):
                        e.wait_ge(vch.h, t_ev)
                        e.tensor_reduce(out=nk1[:], in_=nkv[:], axis=AX.X,
                                        op=ALU.max).then_inc(vch.h, 1)
                    P.v(v_nk_red)
                    t_r1 = vch.inc(1)

                    def v_nk_max(e, t_r1=t_r1):
                        e.wait_ge(vch.h, t_r1)
                        wge(e, vch.h, t_vinit)
                        e.tensor_tensor(out=mkrun[:], in0=mkrun[:], in1=nk1[:],
                                        op=ALU.max).then_inc(vch.h, 1)
                    P.v(v_nk_max)
                    state["t_nk_red"] = vch.inc(1)

                dd = kfm_d if pass_kind == "kv" else qfm_d
                dst = dd.ap().transpose([1, 0, 2])[:, :, i * TILE:(i + 1) * TILE]
                t_all_evac = t_prj_e

                def sy_spill(e, p=p, dst=dst, t_all_evac=t_all_evac,
                             spl_sem=spl_sem):
                    e.wait_ge(vch.h, t_all_evac)
                    e.dma_start(out=dst,
                                in_=prj[p].rearrange("p (c t) -> p c t", c=NCH)
                                ).then_inc(spl_sem.h, 16)
                P.sy(sy_spill)
                state[f"splp_done_{gi}"] = spl_sem.inc(16)
            else:
                t_splv_prev = state.get(f"splv_done_{gi - 2}", 0)
                for j in range(NBLK):
                    for hh in range(2):
                        bk = (j * 2 + hh) % 2
                        t_bankfree = bank_free[bk]

                        def t_vmm(e, p=p, j=j, hh=hh, bk=bk, wbf=wbf,
                                  t_bankfree=t_bankfree, t_x=t_xfm_ready, t_w=t_w):
                            e.wait_ge(vch.h, max(t_x, t_w, t_bankfree))
                            for c in range(NCH):
                                ins = e.matmul(
                                    out=psAB[bk][:, 0:512],
                                    lhsT=xfm[p][:, c * TILE + j * 128:c * TILE + (j + 1) * 128],
                                    rhs=wbf[:, c * D + hh * 512:c * D + (hh + 1) * 512],
                                    start=(c == 0), stop=(c == NCH - 1))
                            ins.then_inc(tch.h, 1)
                        P.t(t_vmm)
                        t_mm = tch.inc(1)

                        def v_vevac(e, p=p, j=j, hh=hh, bk=bk, t_mm=t_mm,
                                    t_splv_prev=t_splv_prev):
                            e.wait_ge(tch.h, t_mm)
                            wge(e, vch.h, t_vinit)
                            if j == 0 and hh == 0:
                                wge(e, splv[p].h, t_splv_prev)
                            e.tensor_copy(
                                out=vsb[p][:, j * 1025 + hh * 512:j * 1025 + (hh + 1) * 512],
                                in_=psAB[bk][:, 0:512]).then_inc(vch.h, 1)
                        P.v(v_vevac)
                        bank_free[bk] = vch.inc(1)
                t_v_evac = bank_free[(NBLK * 2 - 1) % 2]
                dstv = vtm_d.ap()[i * TILE:(i + 1) * TILE, :].rearrange(
                    "(j p) c -> p j c", p=128)

                def sy_vspill(e, p=p, dstv=dstv, t_v_evac=t_v_evac):
                    e.wait_ge(vch.h, t_v_evac)
                    e.dma_start(out=dstv,
                                in_=vsb[p].rearrange("p (j c) -> p j c", j=NBLK)
                                ).then_inc(splv[p].h, 16)
                P.sy(sy_vspill)
                state[f"splv_done_{gi}"] = splv[p].inc(16)

    gi = 0
    for i in range(nt_full):
        emit_tile(i, gi, x_full, "kv")
        gi += 1
    for i in range(nt_own):
        emit_tile(i, gi, x_own, "q")
        gi += 1
    t_a1_tensor_end = tch.n
    t_a1_scalar_end = sch.n

    t_normq_last = state["t_o0_free"]
    t_mk_last = state["t_nk_red"]

    def v_mk2(e):
        e.wait_ge(vch.h, max(t_normq_last, t_mk_last))
        e.tensor_scalar_max(out=mkrun[:], in0=mkrun[:], scalar1=1e-19)
        e.tensor_scalar_max(out=normq[0:1, 0:T_own], in0=normq[0:1, 0:T_own],
                            scalar1=1e-19).then_inc(vch.h, 1)
    P.v(v_mk2)
    t_mk2 = vch.inc(1)

    def s_mk3(e):
        e.wait_ge(vch.h, t_mk2)
        e.activation(out=mkrun[:], in_=mkrun[:], func=AF.Ln).then_inc(sch.h, 1)
    P.s(s_mk3)
    t_mk3 = sch.inc(1)

    def v_mk4(e):
        e.wait_ge(sch.h, t_mk3)
        e.tensor_scalar_mul(out=mksc[:], in0=mkrun[:], scalar1=0.5).then_inc(vch.h, 1)
    P.v(v_mk4)
    t_mk4 = vch.inc(1)

    def s_mk5(e):
        e.wait_ge(vch.h, t_mk4)
        e.activation(out=normq[0:1, 0:T_own], in_=normq[0:1, 0:T_own],
                     func=AF.Ln).then_inc(sch.h, 1)
    P.s(s_mk5)
    t_mk5 = sch.inc(1)

    def s_mk6(e):
        e.wait_ge(sch.h, t_mk5)
        e.activation(out=m_q[0:1, 0:T_own], in_=normq[0:1, 0:T_own], func=AF.Exp,
                     scale=0.5, bias=mksc[:]).then_inc(sch.h, 1)
    P.s(s_mk6)
    t_mq = sch.inc(1)

    abf.reset()
    af32.reset()
    qg_sb = abf.carve(NCH * QG)
    kg = [abf.carve(NCH * KSC) for _ in range(2)]
    vg = [abf.carve((KSC // 128) * 1025) for _ in range(2)]
    p_buf = abf.carve(NCHU_F * QG)
    o_acc = af32.carve(NSUB * 1024)
    den_sb = af32.carve(512)
    o_n = af32.carve(NSUB * D)
    xres = af32.carve(NSUB * D)

    spl_targets = [(splp[0].h, splp[0].n), (splp[1].h, splp[1].n),
                   (splv[0].h, splv[0].n), (splv[1].h, splv[1].n)]

    qg_free = 0
    pbuf_free = 0
    s1_stored = 0
    o_bank_free = {0: state.get("t_o0_free", 0), 1: 0}
    den_bank_free = 0

    for g in range(n_groups):
        t_bankfree = bank_free[0]

        def t_mbc(e, g=g, t_bankfree=t_bankfree):
            e.wait_ge(sch.h, t_mq)
            wge(e, t_consts_h, t_consts)
            wge(e, vch.h, t_bankfree)
            e.matmul(out=psA[:, 0:QG], lhsT=ones_row[:],
                     rhs=m_q[0:1, g * QG:(g + 1) * QG], start=True,
                     stop=True).then_inc(tch.h, 1)
        P.t(t_mbc)
        t_mbcmm = tch.inc(1)

        def v_mbc(e, t_mbcmm=t_mbcmm):
            e.wait_ge(tch.h, t_mbcmm)
            e.tensor_copy(out=m_bc[:], in_=psA[:, 0:QG]).then_inc(vch.h, 1)
        P.v(v_mbc)
        bank_free[0] = vch.inc(1)
        t_mbc_ready = bank_free[0]

        t_qgfree = qg_free
        srcq = qfm_d.ap().transpose([1, 0, 2])[:, :, g * QG:(g + 1) * QG]

        def sy_qg(e, srcq=srcq, t_qgfree=t_qgfree, g=g):
            if g == 0:
                for (h, n) in spl_targets:
                    wge(e, h, n)
                wge(e, tch.h, t_a1_tensor_end)
                wge(e, sch.h, t_mq)
            wge(e, tch.h, t_qgfree)
            e.dma_start(out=qg_sb.rearrange("p (c t) -> p c t", c=NCH),
                        in_=srcq).then_inc(ldqg.h, 16)
        P.sy(sy_qg)
        t_qg = ldqg.inc(16)

        for ki in range(n_ksc):
            par = ki % 2
            if ki >= 2:
                t_used = state.get(f"kvused_{g}_{ki - 2}", 0)
            else:
                t_used = state.get(f"kvused_{g - 1}_{n_ksc + ki - 2}", 0)
            srck = kfm_d.ap().transpose([1, 0, 2])[:, :, ki * KSC:(ki + 1) * KSC]
            srcv = vtm_d.ap()[ki * KSC:(ki + 1) * KSC, :].rearrange(
                "(j p) c -> p j c", p=128)

            def sy_kv(e, par=par, srck=srck, srcv=srcv, t_used=t_used, g=g, ki=ki):
                if g == 0 and ki <= 1:
                    for (h, n) in spl_targets:
                        wge(e, h, n)
                    wge(e, tch.h, t_a1_tensor_end)
                    wge(e, sch.h, t_mq)
                wge(e, tch.h, t_used)
                e.dma_start(out=kg[par].rearrange("p (c t) -> p c t", c=NCH),
                            in_=srck).then_inc(ldkv[par].h, 16)
                e.dma_start(out=vg[par].rearrange("p (j c) -> p j c", j=KSC // 128),
                            in_=srcv).then_inc(ldkv[par].h, 16)
            P.sy(sy_kv)
            state[f"tkv_{ki % 2}"] = ldkv[par].inc(32)
            t_kv = state[f"tkv_{ki % 2}"]

            for kc in range(n_kc):
                chunk = ki * n_kc + kc
                bk = kc % 2
                t_bankfree = bank_free[bk]
                t_exp_prev = state.get(f"exp_{g}_{chunk - 1}", 0)

                def t_sc(e, g=g, par=par, ki=ki, kc=kc, bk=bk,
                         t_bankfree=t_bankfree, t_kv=t_kv, t_qg=t_qg):
                    e.wait_ge(ldkv[par].h, t_kv)
                    e.wait_ge(ldqg.h, t_qg)
                    wge(e, vch.h, t_bankfree)
                    for c in range(NCH):
                        ins = e.matmul(
                            out=psAB[bk][:, 0:QG],
                            lhsT=kg[par][:, c * KSC + kc * 128:c * KSC + (kc + 1) * 128],
                            rhs=qg_sb[:, c * QG:(c + 1) * QG],
                            start=(c == 0), stop=(c == NCH - 1))
                    ins.then_inc(tch.h, 1)
                P.t(t_sc)
                t_scmm = tch.inc(1)
                if ki == n_ksc - 1 and kc == n_kc - 1:
                    qg_free = t_scmm

                def v_stt(e, bk=bk, t_scmm=t_scmm, t_mb=t_mbc_ready,
                          t_exp_prev=t_exp_prev):
                    e.wait_ge(tch.h, t_scmm)
                    wge(e, vch.h, t_mb)
                    wge(e, sch.h, t_exp_prev)
                    e.scalar_tensor_tensor(out=tsb[bk][:], in0=psAB[bk][:, 0:QG],
                                           scalar=1.0, in1=m_bc[:],
                                           op0=ALU.mult, op1=ALU.subtract
                                           ).then_inc(vch.h, 1)
                P.v(v_stt)
                bank_free[bk] = vch.inc(1)
                t_stt = bank_free[bk]

                def s_exp(e, chunk=chunk, bk=bk, t_stt=t_stt, t_pb=pbuf_free):
                    e.wait_ge(vch.h, t_stt)
                    wge(e, tch.h, t_pb)
                    e.activation(out=p_buf[:, chunk * QG:(chunk + 1) * QG],
                                 in_=tsb[bk][:], func=AF.Exp).then_inc(sch.h, 1)
                P.s(s_exp)
                state[f"exp_{g}_{chunk}"] = sch.inc(1)
            t_exp_ksc = state[f"exp_{g}_{ki * n_kc + n_kc - 1}"]
            state[f"texp_{ki % 2}"] = t_exp_ksc

            if ki % 2 == 0:
                continue
            for sh in range((NSUB + 1) // 2):
                for sub in (2 * sh, 2 * sh + 1):
                    if sub >= NSUB:
                        continue
                    om = sub % 2
                    t_ofree = o_bank_free[om]

                    def t_pv(e, g=g, ki=ki, sub=sub, om=om,
                             t_ofree=t_ofree,
                             t_exp0=state["texp_0"], t_exp1=state["texp_1"]):
                        e.wait_ge(sch.h, max(t_exp0, t_exp1))
                        wge(e, vch.h, t_ofree)
                        for kj in (ki - 1, ki):
                            grp_start = (kj == ki - 1)
                            grp_stop = (kj == ki)
                            vb = vg[kj % 2]
                            for kc in range(n_kc):
                                chunk = kj * n_kc + kc
                                lhsT = p_buf[:, chunk * QG + sub * 128:
                                             chunk * QG + (sub + 1) * 128]
                                st0 = grp_start and kc == 0
                                sp0 = grp_stop and kc == n_kc - 1
                                e.matmul(out=psO[om][:, 0:512], lhsT=lhsT,
                                         rhs=vb[:, kc * 1025:kc * 1025 + 512],
                                         start=st0, stop=sp0)
                                ins = e.matmul(out=psO[om][:, 512:1024], lhsT=lhsT,
                                               rhs=vb[:, kc * 1025 + 512:kc * 1025 + 1024],
                                               start=st0, stop=sp0)
                        ins.then_inc(tch.h, 1)
                    P.t(t_pv)
                    t_pvmm = tch.inc(1)
                    if sub in (NSUB - 1, NSUB - 2):
                        pass
                    if sh == (NSUB + 1) // 2 - 1 and sub == min(2 * sh + 1, NSUB - 1):
                        state[f"kvused_{g}_{ki - 1}"] = t_pvmm
                        state[f"kvused_{g}_{ki}"] = t_pvmm
                        if ki == n_ksc - 1:
                            pbuf_free = t_pvmm

                    def v_oacc(e, ki=ki, sub=sub, om=om, t_pvmm=t_pvmm,
                               t_s1st=s1_stored):
                        e.wait_ge(tch.h, t_pvmm)
                        if ki == 1:
                            wge(e, st.h, t_s1st)
                            e.tensor_copy(out=o_acc[:, sub * 1024:(sub + 1) * 1024],
                                          in_=psO[om][:, 0:1024]).then_inc(vch.h, 1)
                        else:
                            e.tensor_tensor(out=o_acc[:, sub * 1024:(sub + 1) * 1024],
                                            in0=o_acc[:, sub * 1024:(sub + 1) * 1024],
                                            in1=psO[om][:, 0:1024],
                                            op=ALU.add).then_inc(vch.h, 1)
                    P.v(v_oacc)
                    state[f"oevac_{g}_{ki}_{sub}"] = vch.inc(1)
                    o_bank_free[om] = state[f"oevac_{g}_{ki}_{sub}"]

            def t_den(e, g=g, ki=ki, t_dfree=den_bank_free):
                if ki == 1:
                    wge(e, vch.h, t_dfree)
                for kj in (ki - 1, ki):
                    for kc in range(n_kc):
                        chunk = kj * n_kc + kc
                        ins = e.matmul(
                            out=psDen[0:1, 0:QG], lhsT=ones_bf[:],
                            rhs=p_buf[:, chunk * QG:(chunk + 1) * QG],
                            start=(ki == 1 and kj == 0 and kc == 0),
                            stop=(ki == n_ksc - 1 and kj == ki and kc == n_kc - 1))
                ins.then_inc(tch.h, 1)
            P.t(t_den)
            t_denmm = tch.inc(1)
            if ki == n_ksc - 1:
                pbuf_free = t_denmm
        t_oacc_all = state[f"oevac_{g}_{n_ksc - 1}_{NSUB - 1}"]

        def v_den(e, t_denmm=t_denmm):
            e.wait_ge(tch.h, t_denmm)
            e.tensor_copy(out=den_sb[0:1, 0:QG],
                          in_=psDen[0:1, 0:QG]).then_inc(vch.h, 1)
        P.v(v_den)
        den_bank_free = vch.inc(1)
        t_vden = den_bank_free

        for sub in range(NSUB):
            bk = sub % 2
            t_bankfree = bank_free[bk]

            def t_dtp(e, sub=sub, bk=bk, t_vden=t_vden, t_bankfree=t_bankfree):
                e.wait_ge(vch.h, max(t_vden, t_bankfree))
                e.transpose(out=psAB[bk][:, 0:1],
                            in_=den_sb[0:1, sub * 128:(sub + 1) * 128],
                            identity=ident_f32[0:1, 0:1]).then_inc(tch.h, 1)
            P.t(t_dtp)
            t_tpd = tch.inc(1)

            def v_dtp(e, sub=sub, bk=bk, t_tpd=t_tpd):
                e.wait_ge(tch.h, t_tpd)
                e.tensor_copy(out=rden[:, sub:sub + 1],
                              in_=psAB[bk][:, 0:1]).then_inc(vch.h, 1)
            P.v(v_dtp)
            bank_free[bk] = vch.inc(1)
        t_rden = bank_free[(NSUB - 1) % 2]

        def v_ep1(e, t_oacc_all=t_oacc_all, t_rden=t_rden):
            e.wait_ge(vch.h, max(t_oacc_all, t_rden))
            e.reciprocal(out=rden[:, 0:NSUB],
                         in_=rden[:, 0:NSUB]).then_inc(vch.h, 1)
        P.v(v_ep1)
        t_ep1 = vch.inc(1)

        def v_ep2(e, t_ep1=t_ep1, t_s1st=s1_stored):
            e.wait_ge(vch.h, t_ep1)
            wge(e, st.h, t_s1st)
            for sub in range(NSUB):
                ins = e.tensor_scalar_mul(out=o_n[:, sub * D:(sub + 1) * D],
                                          in0=o_acc[:, sub * 1024:sub * 1024 + D],
                                          scalar1=rden[:, sub:sub + 1])
            ins.then_inc(vch.h, 1)
        P.v(v_ep2)
        t_ep2 = vch.inc(1)

        def v_pre_ep(e, t_ep2=t_ep2):
            e.wait_ge(vch.h, t_ep2)
        P.v(v_pre_ep)
        t_epap = emit_ln(P, lnb, vch, sch, nblk=NSUB,
                         src=o_n, srcol=lambda j: j * D,
                         dst=o_n, dstcol=lambda j: j * D)

        srcx = x_own[g * QG:(g + 1) * QG, :].rearrange("(j p) c -> p j c", p=128)

        def sy_xres(e, srcx=srcx, t_s1st=s1_stored):
            wge(e, st.h, t_s1st)
            e.dma_start(out=xres.rearrange("p (j c) -> p j c", j=NSUB),
                        in_=srcx).then_inc(ldxr.h, 16)
        P.sy(sy_xres)
        t_xres = ldxr.inc(16)

        def v_ep_res(e, t_epap=t_epap, t_xres=t_xres):
            e.wait_ge(sch.h, t_epap)
            e.wait_ge(ldxr.h, t_xres)
            e.tensor_tensor(out=o_n[:], in0=o_n[:], in1=xres[:],
                            op=ALU.add).then_inc(vch.h, 1)
        P.v(v_ep_res)
        t_eprs = vch.inc(1)

        dsts1 = s1[g * QG:(g + 1) * QG, :].rearrange("(j p) c -> p j c", p=128)

        def sy_s1(e, dsts1=dsts1, t_eprs=t_eprs):
            e.wait_ge(vch.h, t_eprs)
            e.dma_start(out=dsts1,
                        in_=o_n.rearrange("p (j c) -> p j c", j=NSUB)
                        ).then_inc(st.h, 16)
        P.sy(sy_s1)
        s1_stored = st.inc(16)

    def sy_fin(e, t=s1_stored):
        e.wait_ge(st.h, t)
    P.sy(sy_fin)

    if not standalone:
        finals = [(vch.h, vch.n), (sch.h, sch.n), (tch.h, tch.n),
                  (gch.h, gch.n), (st.h, st.n),
                  (splp[0].h, splp[0].n), (splp[1].h, splp[1].n),
                  (splv[0].h, splv[0].n), (splv[1].h, splv[1].n),
                  (ldx[0].h, ldx[0].n), (ldx[1].h, ldx[1].n),
                  (ldqg.h, ldqg.n), (ldkv[0].h, ldkv[0].n),
                  (ldkv[1].h, ldkv[1].n), (ldxr.h, ldxr.n), (ldw.h, ldw.n)]
        return finals

    block = es.enter_context(nc.Block())
    P.emit(block)
    es.close()
    return nc



def build_l0ffn(T_own, ST=256, G=None, io=None):
    H = 4096
    NHC = H // 128
    nst = T_own // ST
    NB = ST // 128
    W = ST + 2

    standalone = G is None
    if standalone:
        nc = bass.Bass("TRN2", target_bir_lowering=False)
        pre = ""
    else:
        nc = G["nc"]
        pre = "b_"
    io = io or {}

    def dram(name, shape, dt, kind):
        if name in io:
            return io[name]
        return nc.dram_tensor(name, shape, dt, kind=kind).ap()

    s1e = dram("s1e", [T_own + 2, D], F32, "ExternalInput")
    w1t = dram("w1t", [D, H], F32, "ExternalInput")
    cw = dram("cw", [H, 3], F32, "ExternalInput")
    cb = dram("cb", [H], F32, "ExternalInput")
    w2t = dram("w2t", [H, D], F32, "ExternalInput")
    s2 = dram("s2", [T_own, D], F32, "ExternalOutput")

    if standalone:
        es = contextlib.ExitStack()
        P = Prog(nc)
    else:
        es, P = G["es"], G["P"]
        P.barrier_step(G["entry"])

    vch = Ctr(es, nc, pre + "vch")
    sch = Ctr(es, nc, pre + "sch")
    tch = Ctr(es, nc, pre + "tch")
    gch = Ctr(es, nc, pre + "gch")
    ldw = [Ctr(es, nc, f"{pre}ldw{p}") for p in range(2)]
    ldx = [Ctr(es, nc, f"{pre}ldx{p}") for p in range(2)]
    ldxr = Ctr(es, nc, pre + "ldxr")
    ldxr1 = Ctr(es, nc, pre + "ldxr1")
    ldc = Ctr(es, nc, pre + "ldc")
    st_c = Ctr(es, nc, pre + "st")

    if standalone:
        consts = build_consts(nc, es, P, gch)
        t_consts_h, t_consts = gch.h, consts["ready"]
    else:
        consts = G["consts"]
        t_consts_h, t_consts = G["consts_sem"], G["consts_ready"]
    ident_f32 = consts["ident_f32"]
    ident_bf = consts["ident_bf"]
    ones_row = consts["ones_row"]

    if standalone:
        abf = Arena(nc, es, "abf", BF16, 47600 + NHC * ST + 2 * ST)
        af32 = Arena(nc, es, "af32", F32, 19744)
    else:
        abf, af32 = G["abf"], G["af32"]
        abf.reset()
        af32.reset()

    wq1_bf = abf.carve(NCH * H // 2).bitcast(FP8)
    wq2_bf = abf.carve(NHC * D // 2).bitcast(FP8)
    hbig = abf.carve(NHC * ST)
    xqfm = [abf.carve(NCH * W) for _ in range(2)]
    hsb = [abf.carve(W) for _ in range(2)]
    xwin_bf = abf.carve((NB + 1) * D)

    stg = [af32.carve(H) for _ in range(2)]
    af32.reset()
    xwin = [af32.carve((NB + 1) * D) for _ in range(2)]
    cacc = [af32.carve(ST) for _ in range(2)]
    tmp32 = af32.carve(ST)
    sc2b = af32.carve(ST)
    out2 = af32.carve(NB * D)
    xres = af32.carve(2 * D)

    lnb = (G["lnb"] if not standalone else mk_ln_bufs(nc, es, NB + 1))
    cwsb = af32.carve_high(NHC * 3)
    cbsb = af32.carve_high(NHC)
    _b1 = af32.carve_high(160)
    ones_col32 = _b1[:, 0:1]
    ones128 = af32.carve_high(128)
    colsum = _b1[:, 1:2]
    asum = _b1[:, 2:3]
    wsum = _b1[0:1, 3:4]
    meanc = [_b1[0:1, 4 + i:5 + i] for i in range(2)]
    meanb = [_b1[:, 6 + i:7 + i] for i in range(2)]
    wsb = _b1[:, 8:9]
    magb = _b1[:, 9:10]
    amaxb = _b1[:, 16:16 + NB + 1]
    scb = _b1[:, 24:24 + NB + 1]
    fb = _b1[:, 32:32 + NB + 1]
    tmax = _b1[:, 40:40 + NB]
    sc2_tm = _b1[:, 48:48 + NB]
    g_tm = _b1[:, 56:56 + NB]
    cbstg0 = af32.carve_high(128)
    maxacc = af32.carve_high(ST)
    rep = af32.carve_high(128)

    if standalone:
        psA = es.enter_context(nc.psum_tensor("psA", [128, 512], F32))[:]
        psB = es.enter_context(nc.psum_tensor("psB", [128, 512], F32))[:]
        psH = [es.enter_context(nc.psum_tensor(f"psH{p}", [128, W], F32))[:]
               for p in range(2)]
        psM = es.enter_context(nc.psum_tensor("psM", [128, 1024], F32))[:]
    else:
        psA, psB = G["psA"], G["psB"]
        if W <= 512:
            psH = [G["psO0"][:, 0:W], G["psO0"][:, 512:512 + W]]
            psM = G["psO1"][:, 0:1024]
        else:
            psH = [G["psO0"][:, 0:W], G["psO1"][:, 0:W]]
            psM = G["psO0"][:, 512:1536]
    psAB = [psA, psB]

    def v_init(e):
        e.memset(ones_col32[:], 1.0)
        e.memset(lnb["veps"][:], 1.0)
        e.memset(amaxb[:], 1.0)
        e.memset(scb[:], 1.0)
        e.memset(fb[:], 1.0)
        e.memset(_b1[:, 64:120], 1.0)
        e.memset(magb[:], MAGIC)
        e.memset(ones128[:], 1.0).then_inc(vch.h, 1)
    P.v(v_init)
    t_vinit = vch.inc(1)

    cbstg = cbstg0[0:32, :]

    def sy_cw(e):
        e.dma_start(out=cwsb[:].rearrange("p (n m) -> p n m", n=NHC),
                    in_=cw.rearrange("(n p) m -> p n m", p=128)).then_inc(ldc.h, 16)
        e.dma_start(out=cbstg[:], in_=cb.rearrange("(n p) -> n p", p=128)
                    ).then_inc(ldc.h, 16)
    P.sy(sy_cw)
    t_ldc0 = ldc.inc(32)

    def t_cb(e):
        e.wait_ge(ldc.h, t_ldc0)
        wge(e, t_consts_h, t_consts)
        e.transpose(out=psB[:, 0:32], in_=cbstg[:],
                    identity=ident_f32[0:32, 0:32]).then_inc(tch.h, 1)
    P.t(t_cb)
    t_cbt = tch.inc(1)

    def v_cb(e):
        e.wait_ge(tch.h, t_cbt)
        e.tensor_copy(out=cbsb[:], in_=psB[:, 0:32]).then_inc(vch.h, 1)
    P.v(v_cb)
    t_cb_ev = vch.inc(1)
    t_ldc = t_ldc0

    state = {}

    def emit_wquant(widx, src_rows_ap, n_chunks, ccols, wq_dst, dstcol,
                    t_prev_done=0):
        numel = n_chunks * 128 * ccols
        for ch in range(n_chunks):
            par = ch % 2
            t_used = state.get(f"wq{widx}_used_{ch - 2}", t_prev_done)
            if ch < 2:
                t_used = max(t_used, t_prev_done)
            src = src_rows_ap[ch * 128:(ch + 1) * 128, :]

            def sy_ld(e, par=par, src=src, ccols=ccols, t_used=t_used):
                wge(e, vch.h, t_used)
                e.dma_start(out=stg[par][:, 0:ccols], in_=src).then_inc(ldw[par].h, 16)
            P.sy(sy_ld)
            t_ld = ldw[par].inc(16)

            t_prev_acc = state.get(f"wq{widx}_used_{ch - 1}", 0)

            def v_red(e, par=par, ccols=ccols, t_ld=t_ld, ch=ch,
                      t_prev_acc=t_prev_acc):
                e.wait_ge(ldw[par].h, t_ld)
                wge(e, vch.h, t_prev_acc)
                e.tensor_reduce(out=asum[:], in_=stg[par][:, 0:ccols], axis=AX.X,
                                op=ALU.add, apply_absolute_value=True
                                ).then_inc(vch.h, 1)
            P.v(v_red)
            t_red = vch.inc(1)

            def v_accum(e, ch=ch, t_red=t_red):
                e.wait_ge(vch.h, t_red)
                if ch == 0:
                    e.tensor_copy(out=colsum[:], in_=asum[:]).then_inc(vch.h, 1)
                else:
                    e.tensor_tensor(out=colsum[:], in0=colsum[:], in1=asum[:],
                                    op=ALU.add).then_inc(vch.h, 1)
            P.v(v_accum)
            state[f"wq{widx}_used_{ch}"] = vch.inc(1)
        t_sum = state[f"wq{widx}_used_{n_chunks - 1}"]

        def t_psum(e, t_sum=t_sum):
            e.wait_ge(vch.h, max(t_sum, t_vinit))
            e.matmul(out=psA[0:1, 0:1], lhsT=colsum[:], rhs=ones_col32[:],
                     start=True, stop=True).then_inc(tch.h, 1)
        P.t(t_psum)
        t_ps = tch.inc(1)

        def v_mean(e, t_ps=t_ps, numel=numel, widx=widx):
            e.wait_ge(tch.h, t_ps)
            e.tensor_scalar(out=meanc[widx][:], in0=psA[0:1, 0:1],
                            scalar1=1.0 / numel, scalar2=1e-5,
                            op0=ALU.mult, op1=ALU.max).then_inc(vch.h, 1)
        P.v(v_mean)
        t_mean = vch.inc(1)

        def v_ws(e, t_mean=t_mean, widx=widx):
            e.wait_ge(vch.h, t_mean)
            e.reciprocal(out=wsum[:], in_=meanc[widx][:]).then_inc(vch.h, 1)
        P.v(v_ws)
        t_wsv = vch.inc(1)

        def t_bc(e, t_wsv=t_wsv, widx=widx):
            e.wait_ge(vch.h, t_wsv)
            wge(e, t_consts_h, t_consts)
            e.matmul(out=psA[:, 1:2], lhsT=ones_row[:], rhs=wsum[:],
                     start=True, stop=True)
            e.matmul(out=psA[:, 2:3], lhsT=ones_row[:], rhs=meanc[widx][:],
                     start=True, stop=True).then_inc(tch.h, 1)
        P.t(t_bc)
        t_bcm = tch.inc(1)

        def v_bc(e, t_bcm=t_bcm, widx=widx):
            e.wait_ge(tch.h, t_bcm)
            e.tensor_copy(out=wsb[:], in_=psA[:, 1:2])
            e.tensor_copy(out=meanb[widx][:], in_=psA[:, 2:3]).then_inc(vch.h, 1)
        P.v(v_bc)
        t_wsb = vch.inc(1)

        for ch in range(n_chunks):
            par = ch % 2
            t_used = state.get(f"wq{widx}b_used_{ch - 2}", 0)
            src = src_rows_ap[ch * 128:(ch + 1) * 128, :]

            def sy_ld2(e, par=par, src=src, ccols=ccols, t_used=t_used,
                       t_wsb=t_wsb, ch=ch):
                if ch <= 1:
                    wge(e, vch.h, t_wsb)
                wge(e, vch.h, t_used)
                e.dma_start(out=stg[par][:, 0:ccols], in_=src).then_inc(ldw[par].h, 16)
            P.sy(sy_ld2)
            t_ld = ldw[par].inc(16)

            def v_q1(e, par=par, ccols=ccols, t_ld=t_ld, t_wsb=t_wsb):
                e.wait_ge(ldw[par].h, t_ld)
                wge(e, vch.h, t_wsb)
                e.tensor_scalar(out=stg[par][:, 0:ccols], in0=stg[par][:, 0:ccols],
                                scalar1=wsb[:], scalar2=MAGIC,
                                op0=ALU.mult, op1=ALU.add).then_inc(vch.h, 1)
            P.v(v_q1)
            t_q1 = vch.inc(1)

            def v_q2(e, par=par, ccols=ccols, t_q1=t_q1):
                e.wait_ge(vch.h, t_q1)
                e.tensor_scalar(out=stg[par][:, 0:ccols], in0=stg[par][:, 0:ccols],
                                scalar1=MAGIC, scalar2=1.0,
                                op0=ALU.subtract, op1=ALU.min).then_inc(vch.h, 1)
            P.v(v_q2)
            t_q2 = vch.inc(1)

            def v_q3(e, par=par, ccols=ccols, t_q2=t_q2, ch=ch):
                e.wait_ge(vch.h, t_q2)
                e.tensor_scalar_max(out=wq_dst[:, dstcol(ch):dstcol(ch) + ccols],
                                    in0=stg[par][:, 0:ccols],
                                    scalar1=-1.0).then_inc(vch.h, 1)
            P.v(v_q3)
            state[f"wq{widx}b_used_{ch}"] = vch.inc(1)
        return state[f"wq{widx}b_used_{n_chunks - 1}"]

    t_wq1 = emit_wquant(0, w1t, NCH, H, wq1_bf, lambda ch: ch * H)
    t_wq2 = emit_wquant(1, w2t, NHC, D, wq2_bf, lambda ch: ch * D,
                        t_prev_done=t_wq1)

    bank_free = {0: 0, 1: t_cb_ev}
    psh_free = {0: 0, 1: 0}
    psm_free = 0
    apply_done = {}
    s2_stored = 0

    def emit_front_a(si):
        par = si % 2
        t0 = si * ST
        blocks = [(j * D, j * D, 128) for j in range(NB)] + [(NB * D, NB * D, 2)]
        amaxb = _b1[:, 64 + 8 * par:64 + 8 * par + NB + 1]
        scb = _b1[:, 80 + 8 * par:80 + 8 * par + NB + 1]
        fb = _b1[:, 96 + 8 * par:96 + 8 * par + NB + 1]
        fbM = _b1[:, 112 + 8 * par:112 + 8 * par + NB + 1]
        t_xwfree = state.get(f"fbq2_{si - 2}", ("v", 0))[1]

        def sy_win(e, t0=t0, par=par, t_xwfree=t_xwfree, si=si):
            if si < 2:
                wge(e, vch.h, t_wq2)
            wge(e, vch.h, t_xwfree)
            e.dma_start(out=xwin[par].rearrange("p (j c) -> p j c", j=NB + 1)[:, 0:NB, :],
                        in_=s1e[t0:t0 + NB * 128, :].rearrange("(j p) c -> p j c", p=128)
                        ).then_inc(ldx[par].h, 16)
            e.dma_start(out=xwin[par][0:2, NB * D:(NB + 1) * D],
                        in_=s1e[t0 + NB * 128:t0 + W, :]).then_inc(ldx[par].h, 16)
        P.sy(sy_win)
        t_win = ldx[par].inc(32)

        def v_predma(e, par=par, t_win=t_win):
            e.wait_ge(ldx[par].h, t_win)
        P.v(v_predma)
        t_apply = emit_ln_blocks(P, lnb, vch, sch, src=xwin[par], dst=xwin[par],
                                 blocks=blocks, war_s=None)
        apply_done[si] = t_apply

        def v_amax(e, par=par, blocks=blocks, amaxb=amaxb, t_apply=t_apply):
            e.wait_ge(sch.h, t_apply)
            for j, (sc, dc, r) in enumerate(blocks):
                ins = e.tensor_reduce(out=amaxb[0:r, j:j + 1],
                                      in_=xwin[par][0:r, sc:sc + D], axis=AX.X,
                                      op=ALU.max, apply_absolute_value=True)
            ins.then_inc(vch.h, 1)
        P.v(v_amax)
        t_am = vch.inc(1)

        def v_clip(e, amaxb=amaxb, t_am=t_am):
            e.wait_ge(vch.h, t_am)
            e.tensor_scalar_max(out=amaxb[:, 0:NB + 1], in0=amaxb[:, 0:NB + 1],
                                scalar1=1e-5).then_inc(vch.h, 1)
        P.v(v_clip)
        t_cl = vch.inc(1)

        def v_rsc(e, amaxb=amaxb, scb=scb, t_cl=t_cl):
            e.wait_ge(vch.h, t_cl)
            e.reciprocal(out=scb[:, 0:NB + 1], in_=amaxb[:, 0:NB + 1]
                         ).then_inc(vch.h, 1)
        P.v(v_rsc)
        t_rs = vch.inc(1)

        def v_sc(e, amaxb=amaxb, scb=scb, fb=fb, t_rs=t_rs, t_w=t_wq1):
            e.wait_ge(vch.h, max(t_rs, t_w))
            e.tensor_scalar_mul(out=scb[:, 0:NB + 1], in0=scb[:, 0:NB + 1],
                                scalar1=127.0)
            e.tensor_scalar_mul(out=fb[:, 0:NB + 1], in0=amaxb[:, 0:NB + 1],
                                scalar1=1.0 / 127.0).then_inc(vch.h, 1)
        P.v(v_sc)
        t_sc = vch.inc(1)

        def v_f(e, fb=fb, t_sc=t_sc):
            e.wait_ge(vch.h, t_sc)
            for j in range(NB + 1):
                ins = e.tensor_tensor(out=fb[:, j:j + 1], in0=fb[:, j:j + 1],
                                      in1=meanb[0][:], op=ALU.mult)
            ins.then_inc(vch.h, 1)
        P.v(v_f)
        t_f0 = vch.inc(1)

        t_f = t_f0

        def s_bq1(e, par=par, blocks=blocks, scb=scb, t_f=t_f):
            wge(e, vch.h, t_f)
            for j, (sc, dc, r) in enumerate(blocks):
                ins = e.activation(out=xwin[par][0:r, sc:sc + D],
                                   in_=xwin[par][0:r, sc:sc + D],
                                   func=AF.Identity, scale=scb[0:r, j:j + 1],
                                   bias=magb[0:r, 0:1])
            ins.then_inc(sch.h, 1)
        P.s(s_bq1)
        t_b1 = sch.inc(1)

        t_tp_war = state.get(f"btp_{si - 1}", 0)

        def v_bq2(e, par=par, blocks=blocks, fb=fb, t_b1=t_b1,
                  t_tp_war=t_tp_war):
            e.wait_ge(sch.h, t_b1)
            wge(e, tch.h, t_tp_war)
            for j, (sc, dc, r) in enumerate(blocks):
                ins = e.tensor_scalar(out=xwin_bf[0:r, sc:sc + D],
                                      in0=xwin[par][0:r, sc:sc + D],
                                      scalar1=MAGIC, scalar2=fb[0:r, j:j + 1],
                                      op0=ALU.subtract, op1=ALU.mult)
            ins.then_inc(vch.h, 1)
        P.v(v_bq2)
        state[f"fbq2_{si}"] = ("v", vch.inc(1))

    def emit_front_b(si):
        par = si % 2
        t_b3 = state[f"fbq2_{si}"][1]
        t_xq_war = state.get(f"bmm1_{si - 2}", 0)
        for j in range(NB + 1):
            r = 128 if j < NB else 2
            bk = 0
            t_bankfree = bank_free[bk]

            def t_tp(e, j=j, r=r, bk=bk, t_bankfree=t_bankfree, t_b3=t_b3):
                e.wait_ge(vch.h, max(t_b3, t_bankfree))
                wge(e, t_consts_h, t_consts)
                pbf = psAB[bk].bitcast(BF16)
                for c in range(NCH):
                    ins = e.transpose(
                        out=pbf[:, c * 128:c * 128 + r],
                        in_=xwin_bf[0:r, j * D + c * 128:j * D + (c + 1) * 128],
                        identity=ident_bf[0:r, 0:r])
                ins.then_inc(tch.h, 1)
            P.t(t_tp)
            t_tpg = tch.inc(1)

            def v_evtp(e, par=par, j=j, r=r, bk=bk, t_tpg=t_tpg,
                       t_xq_war=t_xq_war):
                e.wait_ge(tch.h, max(t_tpg, t_xq_war))
                dst = xqfm[par].rearrange("p (c t) -> p c t", c=NCH)
                e.tensor_copy(out=dst[:, 0:NCH, j * 128:j * 128 + r],
                              in_=psAB[bk].bitcast(BF16)[:, 0:1024]
                              .rearrange("p (c t) -> p c t", c=NCH)[:, :, 0:r]
                              ).then_inc(vch.h, 1)
            P.v(v_evtp)
            bank_free[bk] = vch.inc(1)
        state[f"btp_{si}"] = tch.n
        state[f"fxq_{si}"] = vch.n

    emit_front_a(0)
    emit_front_b(0)
    if nst > 1:
        emit_front_a(1)

    for si in range(nst):
        par = si % 2
        t0 = si * ST
        t_xq_ready = state[f"fxq_{si}"]
        t_tpm_prev = state.get(f"tpm_{si - 1}", 0)

        def v_mx0(e, t_tpm_prev=t_tpm_prev):
            wge(e, tch.h, t_tpm_prev)
            e.memset(maxacc[:, 0:ST], 0.0).then_inc(vch.h, 1)
        P.v(v_mx0)
        t_mx0 = vch.inc(1)

        t_psm_cross = psm_free
        for hc in range(NHC):
            php = hc % 2
            t_phfree = psh_free[php]

            same_bank = (W * 4 <= 2048)

            def t_mm1(e, hc=hc, php=php, par=par, t_phfree=t_phfree,
                      t_xq=t_xq_ready,
                      t_w=t_wq1, t_psm_cross=t_psm_cross, same_bank=same_bank):
                e.wait_ge(vch.h, max(t_xq, t_w))
                if hc < 2:
                    wge(e, sch.h, t_psm_cross)
                if isinstance(t_phfree, tuple):
                    wge(e, t_phfree[0], t_phfree[1])
                else:
                    wge(e, vch.h, t_phfree)
                for c in range(NCH):
                    e.matmul(out=psH[php][:, 0:ST],
                             lhsT=wq1_bf[:, c * H + hc * 128:c * H + (hc + 1) * 128],
                             rhs=xqfm[par][:, c * W:c * W + ST],
                             start=(c == 0), stop=(False if same_bank else c == NCH - 1))
                    ins = e.matmul(out=psH[php][:, ST:W],
                                   lhsT=wq1_bf[:, c * H + hc * 128:c * H + (hc + 1) * 128],
                                   rhs=xqfm[par][:, c * W + ST:c * W + W],
                                   start=(False if same_bank else c == 0),
                                   stop=(c == NCH - 1))
                ins.then_inc(tch.h, 1)
            P.t(t_mm1)
            t_m1 = tch.inc(1)
            if hc == NHC - 1:
                state[f"bmm1_{si}"] = t_m1

            t_conv_prev = state.get(f"conv3_{hc - 2}", 0)

            def s_ehs(e, hc=hc, php=php, t_m1=t_m1, t_conv_prev=t_conv_prev):
                e.wait_ge(tch.h, t_m1)
                wge(e, vch.h, t_conv_prev)
                e.activation(out=hsb[php], in_=psH[php][:, 0:W],
                             func=AF.Identity).then_inc(sch.h, 1)
            P.s(s_ehs)
            t_eh = sch.inc(1)
            psh_free[php] = (sch.h, t_eh)

            t_cacc_free = state.get(f"conv3_{hc - 2}", 0)

            def s_c1(e, hc=hc, php=php, t_eh=t_eh, t_ldc=t_ldc,
                     t_cacc_free=t_cacc_free):
                e.wait_ge(sch.h, t_eh)
                e.wait_ge(vch.h, t_cacc_free)
                e.wait_ge(ldc.h, t_ldc)
                e.activation(out=cacc[php][:, 0:ST], in_=hsb[php][:, 0:ST],
                             func=AF.Identity,
                             scale=cwsb[:, hc * 3:hc * 3 + 1]).then_inc(sch.h, 1)
            P.s(s_c1)
            t_c1 = sch.inc(1)

            def v_c2(e, hc=hc, php=php, t_c1=t_c1):
                e.wait_ge(sch.h, t_c1)
                e.scalar_tensor_tensor(out=cacc[php][:, 0:ST],
                                       in0=hsb[php][:, 1:1 + ST],
                                       scalar=cwsb[:, hc * 3 + 1:hc * 3 + 2],
                                       in1=cacc[php][:, 0:ST],
                                       op0=ALU.mult, op1=ALU.add).then_inc(vch.h, 1)
            P.v(v_c2)
            state[f"conv1_{hc}"] = vch.inc(1)
            t_c2 = state[f"conv1_{hc}"]

            t_tmp_free = state.get(f"silu_{hc - 1}", 0)

            def v_c3(e, hc=hc, php=php, t_c2=t_c2, t_tmp_free=t_tmp_free):
                e.wait_ge(vch.h, t_c2)
                wge(e, sch.h, t_tmp_free)
                e.scalar_tensor_tensor(out=tmp32[:, 0:ST],
                                       in0=hsb[php][:, 2:2 + ST],
                                       scalar=cwsb[:, hc * 3 + 2:hc * 3 + 3],
                                       in1=cacc[php][:, 0:ST],
                                       op0=ALU.mult, op1=ALU.add).then_inc(vch.h, 1)
            P.v(v_c3)
            state[f"conv3_{hc}"] = vch.inc(1)
            t_c3 = state[f"conv3_{hc}"]

            t_q2done = state.get(f"q2_{hc}", None)
            t_q2done = state.get(f"bq2_{si - 1}_last", 0)

            def s_silu(e, hc=hc, t_c3=t_c3, t_q2done=t_q2done):
                e.wait_ge(vch.h, max(t_c3, t_cb_ev))
                wge(e, tch.h, t_q2done)
                e.activation(out=hbig[:, hc * ST:(hc + 1) * ST], in_=tmp32[:, 0:ST],
                             func=AF.Silu, bias=cbsb[:, hc:hc + 1]).then_inc(sch.h, 1)
            P.s(s_silu)
            state[f"silu_{hc}"] = sch.inc(1)
            t_sl = state[f"silu_{hc}"]

            def v_mx(e, hc=hc, t_sl=t_sl, t_mx0=t_mx0):
                e.wait_ge(sch.h, t_sl)
                e.wait_ge(vch.h, t_mx0)
                e.tensor_tensor(out=maxacc[:, 0:ST], in0=maxacc[:, 0:ST],
                                in1=hbig[:, hc * ST:(hc + 1) * ST],
                                op=ALU.max).then_inc(vch.h, 1)
            P.v(v_mx)
            state[f"mx_{hc}"] = vch.inc(1)
        if si + 1 < nst:
            emit_front_b(si + 1)
        if si + 2 < nst:
            emit_front_a(si + 2)

        t_mx_pre = state[f"mx_{NHC - 1}"]

        def v_mxf(e, t_mx_pre=t_mx_pre):
            e.wait_ge(vch.h, t_mx_pre)
            e.tensor_scalar_max(out=maxacc[:, 0:ST], in0=maxacc[:, 0:ST],
                                scalar1=0.27847).then_inc(vch.h, 1)
        P.v(v_mxf)
        t_mx_all = vch.inc(1)

        for j in range(NB):
            bk = 1
            t_bankfree = bank_free[bk]

            def t_tpm(e, j=j, bk=bk, t_mx_all=t_mx_all, t_bankfree=t_bankfree):
                e.wait_ge(vch.h, max(t_mx_all, t_bankfree))
                e.transpose(out=psAB[bk][:, 0:128],
                            in_=maxacc[:, j * 128:(j + 1) * 128],
                            identity=ident_f32[:]).then_inc(tch.h, 1)
            P.t(t_tpm)
            t_tj = tch.inc(1)

            def v_redm(e, j=j, bk=bk, t_tj=t_tj):
                e.wait_ge(tch.h, t_tj)
                e.tensor_reduce(out=tmax[:, j:j + 1], in_=psAB[bk][:, 0:128],
                                axis=AX.X, op=ALU.max).then_inc(vch.h, 1)
            P.v(v_redm)
            bank_free[bk] = vch.inc(1)
        t_tmax = bank_free[1]
        state[f"tpm_{si}"] = tch.n

        def v_sc2(e, t_tmax=t_tmax, t_w2=t_wq2):
            e.wait_ge(vch.h, max(t_tmax, t_w2))
            e.tensor_scalar_max(out=tmax[:, 0:NB], in0=tmax[:, 0:NB],
                                scalar1=1e-5).then_inc(vch.h, 1)
        P.v(v_sc2)
        t_s2pre = vch.inc(1)

        def v_sc2r(e, t_s2pre=t_s2pre):
            e.wait_ge(vch.h, t_s2pre)
            e.reciprocal(out=sc2_tm[:, 0:NB], in_=tmax[:, 0:NB]).then_inc(vch.h, 1)
        P.v(v_sc2r)
        t_s2a = vch.inc(1)

        def v_sc2b(e, t_s2a=t_s2a):
            e.wait_ge(vch.h, t_s2a)
            e.tensor_scalar_mul(out=sc2_tm[:, 0:NB], in0=sc2_tm[:, 0:NB],
                                scalar1=127.0)
            e.tensor_scalar_mul(out=g_tm[:, 0:NB], in0=tmax[:, 0:NB],
                                scalar1=1.0 / 127.0).then_inc(vch.h, 1)
        P.v(v_sc2b)
        t_s2b = vch.inc(1)

        def v_g(e, t_s2b=t_s2b):
            e.wait_ge(vch.h, t_s2b)
            for j in range(NB):
                ins = e.tensor_tensor(out=g_tm[:, j:j + 1], in0=g_tm[:, j:j + 1],
                                      in1=meanb[1][:], op=ALU.mult)
            ins.then_inc(vch.h, 1)
        P.v(v_g)
        t_g = vch.inc(1)

        for j in range(NB):
            bk = 1
            t_bankfree = bank_free[bk]

            def v_rep(e, j=j, t_s2b=t_s2b, t_vinit=t_vinit):
                e.wait_ge(vch.h, max(t_s2b, t_vinit))
                e.tensor_scalar_mul(out=rep[:], in0=ones128[:],
                                    scalar1=sc2_tm[:, j:j + 1]).then_inc(vch.h, 1)
            P.v(v_rep)
            t_rep = vch.inc(1)

            def t_tpr(e, j=j, bk=bk, t_rep=t_rep, t_bankfree=t_bankfree):
                e.wait_ge(vch.h, max(t_rep, t_bankfree))
                e.transpose(out=psAB[bk][:, 0:128], in_=rep[:],
                            identity=ident_f32[:]).then_inc(tch.h, 1)
            P.t(t_tpr)
            t_tpj = tch.inc(1)

            def v_esc(e, j=j, bk=bk, t_tpj=t_tpj):
                e.wait_ge(tch.h, t_tpj)
                e.tensor_copy(out=sc2b[:, j * 128:(j + 1) * 128],
                              in_=psAB[bk][:, 0:128]).then_inc(vch.h, 1)
            P.v(v_esc)
            bank_free[bk] = vch.inc(1)
        t_sc2b_ready = bank_free[1]

        for hc in range(NHC):
            t_qh_prev = state.get(f"qh_{hc - 1}", 0)

            def v_qh1(e, hc=hc, t_sc2b_ready=t_sc2b_ready, t_qh_prev=t_qh_prev):
                e.wait_ge(vch.h, max(t_sc2b_ready, t_qh_prev))
                e.tensor_tensor(out=tmp32[:, 0:ST], in0=hbig[:, hc * ST:(hc + 1) * ST],
                                in1=sc2b[:, 0:ST], op=ALU.mult).then_inc(vch.h, 1)
            P.v(v_qh1)
            t_h1 = vch.inc(1)

            def v_qh2(e, hc=hc, t_h1=t_h1):
                e.wait_ge(vch.h, t_h1)
                e.tensor_scalar(out=hbig[:, hc * ST:(hc + 1) * ST],
                                in0=tmp32[:, 0:ST],
                                scalar1=MAGIC, scalar2=MAGIC,
                                op0=ALU.add, op1=ALU.subtract).then_inc(vch.h, 1)
            P.v(v_qh2)
            state[f"qh_{hc}"] = vch.inc(1)
        t_q_all = state[f"qh_{NHC - 1}"]

        for j in range(NB):
            def t_mm2(e, j=j, t_q_all=t_q_all, t_w2=t_wq2, t_psm=psm_free):
                e.wait_ge(vch.h, max(t_q_all, t_w2))
                wge(e, sch.h, t_psm)
                for hc in range(NHC):
                    e.matmul(out=psM[:, 0:512],
                             lhsT=hbig[:, hc * ST + j * 128:hc * ST + (j + 1) * 128],
                             rhs=wq2_bf[:, hc * D:hc * D + 512],
                             start=(hc == 0), stop=(hc == NHC - 1))
                    ins = e.matmul(out=psM[:, 512:1024],
                                   lhsT=hbig[:, hc * ST + j * 128:hc * ST + (j + 1) * 128],
                                   rhs=wq2_bf[:, hc * D + 512:hc * D + 1024],
                                   start=(hc == 0), stop=(hc == NHC - 1))
                ins.then_inc(tch.h, 1)
            P.t(t_mm2)
            t_m2 = tch.inc(1)
            if j == NB - 1:
                state[f"bq2_{si}_last"] = t_m2

            def s_em2(e, j=j, t_m2=t_m2, t_g=t_g, t_s2st=s2_stored):
                e.wait_ge(tch.h, t_m2)
                wge(e, vch.h, t_g)
                wge(e, st_c.h, t_s2st)
                e.activation(out=out2[:, j * D:(j + 1) * D], in_=psM[:, 0:1024],
                             func=AF.Identity,
                             scale=g_tm[:, j:j + 1]).then_inc(sch.h, 1)
            P.s(s_em2)
            psm_free = sch.inc(1)
        t_out2 = psm_free

        def v_pre(e, t_out2=t_out2):
            e.wait_ge(sch.h, t_out2)
        P.v(v_pre)
        t_ap2 = emit_ln(P, lnb, vch, sch, nblk=NB,
                        src=out2, srcol=lambda j: j * D,
                        dst=out2, dstcol=lambda j: j * D)

        for j in range(NB):
            px = j % 2
            lx = ldxr if px == 0 else ldxr1
            t_pfree = state.get(f"bres_{si}_{j - 2}", 0)

            def sy_xres(e, t0=t0, j=j, px=px, lx=lx, t_s2st=s2_stored,
                        t_pfree=t_pfree):
                wge(e, st_c.h, t_s2st)
                wge(e, vch.h, t_pfree)
                e.dma_start(out=xres[:, px * D:(px + 1) * D],
                            in_=s1e[t0 + 1 + j * 128:t0 + 1 + (j + 1) * 128, :]
                            ).then_inc(lx.h, 16)
            P.sy(sy_xres)
            t_xr = lx.inc(16)

            def v_res(e, j=j, px=px, lx=lx, t_ap2=t_ap2, t_xr=t_xr,
                      t_s2st=s2_stored):
                if j == 0:
                    e.wait_ge(sch.h, t_ap2)
                    wge(e, st_c.h, t_s2st)
                e.wait_ge(lx.h, t_xr)
                e.tensor_tensor(out=out2[:, j * D:(j + 1) * D],
                                in0=out2[:, j * D:(j + 1) * D],
                                in1=xres[:, px * D:(px + 1) * D],
                                op=ALU.add).then_inc(vch.h, 1)
            P.v(v_res)
            state[f"bres_{si}_{j}"] = vch.inc(1)
        t_resv = state[f"bres_{si}_{NB - 1}"]

        def sy_s2(e, t0=t0, t_resv=t_resv):
            e.wait_ge(vch.h, t_resv)
            e.dma_start(out=s2[t0:t0 + ST, :].rearrange("(j p) c -> p j c", p=128),
                        in_=out2.rearrange("p (j c) -> p j c", j=NB)
                        ).then_inc(st_c.h, 16)
        P.sy(sy_s2)
        s2_stored = st_c.inc(16)

    def sy_fin(e, t=s2_stored):
        e.wait_ge(st_c.h, t)
    P.sy(sy_fin)

    if not standalone:
        finals = [(vch.h, vch.n), (sch.h, sch.n), (tch.h, tch.n),
                  (gch.h, gch.n), (st_c.h, st_c.n), (ldc.h, ldc.n),
                  (ldw[0].h, ldw[0].n), (ldw[1].h, ldw[1].n),
                  (ldx[0].h, ldx[0].n), (ldx[1].h, ldx[1].n),
                  (ldxr.h, ldxr.n)]
        return finals

    block = es.enter_context(nc.Block())
    P.emit(block)
    es.close()
    return nc



def l1_proj_tiles(ctx, si, t0, *, do_q):
    (nc, P, es, state, vch, sch, tch, gch) = (ctx[k] for k in
        ("nc", "P", "es", "state", "vch", "sch", "tch", "gch"))
    psAB = ctx["psAB"]
    bank_free = ctx["bank_free"]
    ident_f32 = ctx["ident_f32"]
    ident_bf = ctx["ident_bf"]
    t_consts = ctx["t_consts"]
    t_consts_h = ctx["t_consts_h"]
    lnb = ctx["lnb"]
    ldx = ctx["ldx"]
    s2src = ctx["s2src"]
    xst = ctx["xst"]
    xfm = ctx["xfm"]
    TILE = 512
    NBLK = 4
    nbuf = ctx["nbuf"]

    p = si % nbuf
    xbf = ctx["xbf"]
    t_apfree = state.get(f"l1ap_{si - nbuf}", 0)
    t_tp_prev = state.get(f"l1tp_{si - 1}", 0)
    src = s2src[t0:t0 + TILE, :].rearrange("(j p) c -> p j c", p=128)

    def sy_load(e, src=src, p=p, t_apfree=t_apfree, t_tp_prev=t_tp_prev, si=si):
        if si < nbuf:
            wge(e, vch.h, ctx.get("t_wdone", 0))
        wge(e, sch.h, t_apfree)
        wge(e, tch.h, t_tp_prev)
        e.dma_start(out=xst[p].rearrange("p (j c) -> p j c", j=NBLK),
                    in_=src).then_inc(ldx[p].h, 16)
    P.sy(sy_load)
    t_ldx = ldx[p].inc(16)

    def v_predma(e, p=p, t_ldx=t_ldx):
        e.wait_ge(ldx[p].h, t_ldx)
    P.v(v_predma)
    t_apply = emit_ln(P, lnb, vch, sch, nblk=NBLK,
                      src=xst[p], srcol=lambda j: j * D,
                      dst=xbf, dstcol=lambda j: j * D,
                      war_s=(tch.h, t_tp_prev))
    state[f"l1ap_{si}"] = t_apply

    for j in range(NBLK):
        for h in range(2):
            bk = (j * 2 + h) % 2
            t_bankfree = bank_free[bk]

            def t_tp(e, p=p, j=j, h=h, bk=bk, t_bankfree=t_bankfree,
                     t_ap=t_apply):
                e.wait_ge(sch.h, t_ap)
                wge(e, t_consts_h, t_consts)
                wge(e, vch.h, t_bankfree)
                pbf = psAB[bk].bitcast(BF16)
                for cc in range(4):
                    c = 4 * h + cc
                    ins = e.transpose(
                        out=pbf[:, cc * 128:(cc + 1) * 128],
                        in_=xbf[:, j * D + c * 128:j * D + (c + 1) * 128],
                        identity=ident_bf[:])
                ins.then_inc(tch.h, 1)
            P.t(t_tp)
            t_tpg = tch.inc(1)

            def v_evac_tp(e, p=p, j=j, h=h, bk=bk, t_tpg=t_tpg):
                e.wait_ge(tch.h, t_tpg)
                dst = xfm[p].rearrange("p (c t) -> p c t", c=NCH)
                e.tensor_copy(out=dst[:, 4 * h:4 * h + 4, j * 128:(j + 1) * 128],
                              in_=psAB[bk].bitcast(BF16)[:, 0:512]
                              .rearrange("p (c t) -> p c t", c=4)
                              ).then_inc(vch.h, 1)
            P.v(v_evac_tp)
            bank_free[bk] = vch.inc(1)
    state[f"l1tp_{si}"] = tch.n
    t_xfm_ready = vch.n

    plist = [("k", True), ("v", False)] + ([("q", True)] if do_q else [])
    for (nm, phi) in plist:
        wbf, t_w = ctx[f"w_{nm}"]
        bcol = ctx["bias_fm"]
        boff = {"k": 0, "v": NCH, "q": 2 * NCH}[nm]
        dstb = ctx[f"p_{nm}"][p]
        kb = ctx["kb"]
        mn = ctx["mn"]
        emb = ctx["emb"]
        for e_ch in range(NCH):
            bk = e_ch % 2
            t_bankfree = bank_free[bk]

            def t_proj(e, p=p, e_ch=e_ch, bk=bk, wbf=wbf, t_w=t_w,
                       t_bankfree=t_bankfree, t_x=t_xfm_ready):
                e.wait_ge(vch.h, max(t_x, t_w, t_bankfree))
                for c in range(NCH):
                    ins = e.matmul(
                        out=psAB[bk][:, 0:TILE],
                        lhsT=wbf[:, c * D + e_ch * 128:c * D + (e_ch + 1) * 128],
                        rhs=xfm[p][:, c * TILE:(c + 1) * TILE],
                        start=(c == 0), stop=(c == NCH - 1))
                ins.then_inc(tch.h, 1)
            P.t(t_proj)
            t_mm = tch.inc(1)

            if not phi:
                def s_ev(e, e_ch=e_ch, bk=bk, dstb=dstb, boff=boff, t_mm=t_mm,
                         t_b=ctx["t_bias"]):
                    e.wait_ge(tch.h, t_mm)
                    wge(e, vch.h, t_b)
                    e.activation(out=dstb[:, e_ch * TILE:(e_ch + 1) * TILE],
                                 in_=psAB[bk][:, 0:TILE], func=AF.Identity,
                                 bias=bcol[:, boff + e_ch:boff + e_ch + 1]
                                 ).then_inc(sch.h, 1)
                P.s(s_ev)
                t_sev = sch.inc(1)

                def v_done(e, t_sev=t_sev):
                    e.wait_ge(sch.h, t_sev)
                    e.engine_nop().then_inc(vch.h, 1)
                P.v(v_done)
                bank_free[bk] = vch.inc(1)
            else:
                boff1 = {"k": 3 * NCH, "q": 4 * NCH}[nm]
                t_em_free = state.get("phi_last", 0)

                def s_em(e, e_ch=e_ch, bk=bk, boff=boff, t_mm=t_mm,
                         t_b=ctx["t_bias"], t_em_free=t_em_free):
                    e.wait_ge(tch.h, t_mm)
                    wge(e, vch.h, max(t_b, t_em_free))
                    e.activation(out=emb[:, 0:TILE], in_=psAB[bk][:, 0:TILE],
                                 func=AF.Exp,
                                 bias=bcol[:, boff + e_ch:boff + e_ch + 1]
                                 ).then_inc(sch.h, 1)
                P.s(s_em)
                t_em = sch.inc(1)

                def v_mn(e, t_em=t_em):
                    e.wait_ge(sch.h, t_em)
                    e.tensor_scalar_min(out=emb[:, 0:TILE], in0=emb[:, 0:TILE],
                                        scalar1=1.0).then_inc(vch.h, 1)
                P.v(v_mn)
                t_mn = vch.inc(1)

                def v_phi(e, e_ch=e_ch, bk=bk, dstb=dstb, boff1=boff1,
                          t_mn=t_mn):
                    e.wait_ge(vch.h, t_mn)
                    e.scalar_tensor_tensor(
                        out=dstb[:, e_ch * TILE:(e_ch + 1) * TILE],
                        in0=psAB[bk][:, 0:TILE],
                        scalar=bcol[:, boff1 + e_ch:boff1 + e_ch + 1],
                        in1=emb[:, 0:TILE],
                        op0=ALU.add, op1=ALU.max).then_inc(vch.h, 1)
                P.v(v_phi)
                state["phi_last"] = vch.inc(1)
                bank_free[bk] = state["phi_last"]
    t_proj_all = vch.n
    t_proj_all_s = sch.n

    kt_tm = ctx["kt_tm"][p]
    va_tm = ctx["va_tm"][p]
    npsT = ctx["npsT"]
    for cch in range(4):
        for pr in range(NCH):
            bk = (cch * NCH + pr) % npsT
            t_bankfree = state.get(f"ptp_ev_{bk}", 0)

            def t_tpp(e, p=p, cch=cch, pr=pr, bk=bk, t_bankfree=t_bankfree,
                      t_pa=t_proj_all, t_ps=t_proj_all_s):
                e.wait_ge(vch.h, max(t_pa, t_bankfree))
                e.wait_ge(sch.h, t_ps)
                pk = ctx["p_k"][p]
                pv = ctx["p_v"][p]
                psT = ctx["psT"]
                e.transpose(out=psT[bk][:, 0:128],
                            in_=pk[:, pr * 512 + cch * 128:pr * 512 + (cch + 1) * 128],
                            identity=ident_bf[:])
                e.transpose(out=psT[bk][:, 128:256],
                            in_=pv[:, pr * 512 + cch * 128:pr * 512 + (cch + 1) * 128],
                            identity=ident_bf[:]).then_inc(tch.h, 1)
            P.t(t_tpp)
            t_tpg = tch.inc(1)

            def v_evpp(e, p=p, cch=cch, pr=pr, bk=bk, t_tpg=t_tpg,
                       t_vo=ctx["t_vones"]):
                e.wait_ge(tch.h, t_tpg)
                wge(e, vch.h, t_vo)
                psT = ctx["psT"]
                e.tensor_copy(out=kt_tm[:, cch * 1024 + pr * 128:cch * 1024 + (pr + 1) * 128],
                              in_=psT[bk][:, 0:128])
                dst = va_tm[:, cch * 1040 + pr * 130:]
                e.tensor_copy(
                    out=bass.AP(va_tm.tensor, va_tm.offset + cch * 1040 + pr * 130,
                                [[va_tm.ap[0][0], 128], [65, 2], [1, 64]])
                    if False else
                    va_tm.rearrange("p (cc hh x) -> p cc hh x", cc=4, hh=16)[:, cch, 2 * pr:2 * pr + 2, 0:64],
                    in_=psT[bk][:, 128:256].rearrange("p (hh x) -> p hh x", hh=2)
                    ).then_inc(vch.h, 1)
            P.v(v_evpp)
            state[f"ptp_ev_{bk}"] = vch.inc(1)
    state[f"l1ptp_{si}"] = tch.n
    return vch.n


def _l1_common(nc, es, P, T_own, with_q, with_wo, nbuf=2, abf_sz=59000,
               af_sz=13200, G=None, pre=""):
    ctx = {}
    vch = Ctr(es, nc, pre + "vch"); sch = Ctr(es, nc, pre + "sch")
    tch = Ctr(es, nc, pre + "tch"); gch = Ctr(es, nc, pre + "gch")
    ldw = Ctr(es, nc, pre + "ldwq")
    ldx = [Ctr(es, nc, f"{pre}ldx{p}") for p in range(2)]
    ldb = Ctr(es, nc, pre + "ldb")
    st_c = Ctr(es, nc, pre + "st")
    ctx.update(nc=nc, es=es, P=P, vch=vch, sch=sch, tch=tch, gch=gch,
               ldx=ldx, ldb=ldb, st=st_c, state={}, nbuf=nbuf)

    if G is None:
        consts = build_consts(nc, es, P, gch)
        ctx["t_consts_h"], ctx["t_consts"] = gch.h, consts["ready"]
    else:
        consts = G["consts"]
        ctx["t_consts_h"], ctx["t_consts"] = G["consts_sem"], G["consts_ready"]
    ctx["ident_f32"] = consts["ident_f32"]
    ctx["ident_bf"] = consts["ident_bf"]
    ctx["ones_row"] = consts["ones_row"]

    if G is None:
        abf = Arena(nc, es, "abf", BF16, abf_sz)
        af32 = Arena(nc, es, "af32", F32, af_sz)
    else:
        abf, af32 = G["abf"], G["af32"]
        abf.reset()
        af32.reset()
    ctx["abf"] = abf; ctx["af32"] = af32

    names = ["k", "v"] + (["q"] if with_q else []) + (["o"] if with_wo else [])
    wtens = {}
    for nm in names:
        wtens[nm] = abf.carve(NCH * D)
    wstage = af32.carve(NCH * D // 2)
    af32.reset()
    xst = [af32.carve(4 * D) for _ in range(nbuf)]
    ctx["xst"] = xst
    ctx["xbf"] = af32.carve(2 * D).bitcast(BF16)
    ctx["xfm"] = [abf.carve(NCH * 512) for _ in range(nbuf)]
    ctx["p_k"] = [abf.carve(NCH * 512) for _ in range(nbuf)]
    ctx["p_v"] = [abf.carve(NCH * 512) for _ in range(nbuf)]
    if with_q:
        ctx["p_q"] = [abf.carve(NCH * 512) for _ in range(nbuf)]
    ctx["kt_tm"] = [abf.carve(4 * 1024) for _ in range(nbuf)]
    ctx["va_tm"] = [abf.carve(4 * 1040) for _ in range(nbuf)]
    ctx["kb"] = af32.carve(512)
    ctx["mn"] = af32.carve(512)
    ctx["emb"] = af32.carve(512)

    ctx["lnb"] = (G["lnb"] if G is not None else mk_ln_bufs(nc, es, 4))
    ctx["bias_fm"] = af32.carve_high(5 * NCH)
    ctx["bstg"] = ctx["mn"][0:NCH, 0:128]
    ctx["S_sb"] = af32.carve_high(NCH * 65)
    ctx["S_bf"] = abf.carve_high(NCH * 65)

    if G is None:
        psA = es.enter_context(nc.psum_tensor("psA", [128, 512], F32))[:]
        psB = es.enter_context(nc.psum_tensor("psB", [128, 512], F32))[:]
        psT_single = es.enter_context(nc.psum_tensor("psT0", [128, 256], BF16))[:]
        psS_single = es.enter_context(nc.psum_tensor("psS0", [128, 65], F32))[:]
        ctx["psT"] = [psT_single, psT_single]
        ctx["psS"] = [psS_single, psS_single]
        ctx["npsT"] = 1
        ctx["npsS"] = 1
    else:
        psA, psB = G["psA"], G["psB"]
        if with_wo:
            ctx["psT"] = [G["psO1"].bitcast(BF16)[:, 2048:2304]] * 2
            ctx["npsT"] = 1
        else:
            ctx["psT"] = [G["psO1"].bitcast(BF16)[:, 0:256],
                          G["psO1"].bitcast(BF16)[:, 2048:2304]]
            ctx["npsT"] = 2
        ctx["psS"] = [G["psO0"][:, 0:65], G["psO0"][:, 512:577]]
        ctx["npsS"] = 2
    ctx["psAB"] = [psA, psB]
    ctx["bank_free"] = {0: 0, 1: 0}

    wsrc = ctx["wsrc"] = {}
    return ctx, wtens, wstage, ldw


def build_l1state(T_own, G=None, io=None):
    standalone = G is None
    if standalone:
        nc = bass.Bass("TRN2", target_bir_lowering=False)
        pre = ""
    else:
        nc = G["nc"]
        pre = "c_"
    io = io or {}

    def dram(name, shape, dt, kind):
        if name in io:
            return io[name]
        return nc.dram_tensor(name, shape, dt, kind=kind).ap()

    s2 = dram("s2", [T_own, D], F32, "ExternalInput")
    wkT = dram("wkT", [D, D], F32, "ExternalInput")
    wvT = dram("wvT", [D, D], F32, "ExternalInput")
    bk = dram("bk", [D], F32, "ExternalInput")
    bv = dram("bv", [D], F32, "ExternalInput")
    stout = dram("state_loc", [D, 65], F32, "ExternalOutput")

    if standalone:
        es = contextlib.ExitStack()
        P = Prog(nc)
    else:
        es, P = G["es"], G["P"]
        P.barrier_step(G["entry"])
    ctx, wtens, wstage, ldw = _l1_common(nc, es, P, T_own, with_q=False,
                                         with_wo=False, G=G, pre=pre)
    vch, sch, tch, gch = ctx["vch"], ctx["sch"], ctx["tch"], ctx["gch"]
    state = ctx["state"]
    st_c = ctx["st"]
    ldb = ctx["ldb"]
    ctx["s2src"] = s2

    tw = load_weights_bf(P, wstage, [(wtens["k"], wkT, D), (wtens["v"], wvT, D)],
                         ldw, vch)
    ctx["w_k"] = (wtens["k"], tw[0])
    ctx["w_v"] = (wtens["v"], tw[1])
    ctx["t_wdone"] = tw[-1]

    def sy_b(e):
        e.dma_start(out=ctx["bstg"][:], in_=bk.rearrange("(n p) -> n p", p=128)
                    ).then_inc(ldb.h, 16)
    P.sy(sy_b)
    t_b1 = ldb.inc(16)

    def t_bt(e):
        e.wait_ge(ldb.h, t_b1)
        wge(e, ctx["t_consts_h"], ctx["t_consts"])
        e.transpose(out=ctx["psAB"][0][:, 0:NCH], in_=ctx["bstg"][:],
                    identity=ctx["ident_f32"][0:NCH, 0:NCH]).then_inc(tch.h, 1)
    P.t(t_bt)
    t_b2 = tch.inc(1)

    def v_bt(e):
        e.wait_ge(tch.h, t_b2)
        e.tensor_copy(out=ctx["bias_fm"][:, 0:NCH],
                      in_=ctx["psAB"][0][:, 0:NCH]).then_inc(vch.h, 1)
    P.v(v_bt)
    t_b3 = vch.inc(1)

    def sy_b2(e):
        e.wait_ge(vch.h, t_b3)
        e.dma_start(out=ctx["bstg"][:], in_=bv.rearrange("(n p) -> n p", p=128)
                    ).then_inc(ldb.h, 16)
    P.sy(sy_b2)
    t_b4 = ldb.inc(16)

    def t_bt2(e):
        e.wait_ge(ldb.h, t_b4)
        e.transpose(out=ctx["psAB"][0][:, 0:NCH], in_=ctx["bstg"][:],
                    identity=ctx["ident_f32"][0:NCH, 0:NCH]).then_inc(tch.h, 1)
    P.t(t_bt2)
    t_b5 = tch.inc(1)

    def v_bt2(e):
        e.wait_ge(tch.h, t_b5)
        e.tensor_copy(out=ctx["bias_fm"][:, NCH:2 * NCH],
                      in_=ctx["psAB"][0][:, 0:NCH]).then_inc(vch.h, 1)
    P.v(v_bt2)
    t_bv2 = vch.inc(1)

    def v_b1c(e, t_bv2=t_bv2):
        e.wait_ge(vch.h, t_bv2)
        e.tensor_scalar_add(out=ctx["bias_fm"][:, 3 * NCH:4 * NCH],
                            in0=ctx["bias_fm"][:, 0:NCH],
                            scalar1=1.0).then_inc(vch.h, 1)
    P.v(v_b1c)
    ctx["t_bias"] = vch.inc(1)
    ctx["bank_free"][0] = ctx["t_bias"]

    def v_init(e):
        for p in range(2):
            va = ctx["va_tm"][p]
            e.memset(va.rearrange("p (cc hh x) -> p cc hh x", cc=4, hh=16)[:, :, :, 64:65], 1.0)
        e.memset(ctx["S_sb"][:], 0.0).then_inc(vch.h, 1)
    P.v(v_init)
    ctx["t_vones"] = vch.inc(1)

    nt = T_own // 512
    psS = ctx["psS"]
    for si in range(nt):
        t_ready = l1_proj_tiles(ctx, si, si * 512, do_q=False)
        p = si % 2
        kt_tm = ctx["kt_tm"][p]
        va_tm = ctx["va_tm"][p]
        for pr in range(NCH):
            pp = pr % ctx["npsS"]
            t_psfree = state.get(f"sevac_any_{(si * NCH + pr) - ctx['npsS']}", 0)

            def t_sc(e, pr=pr, pp=pp, kt_tm=kt_tm, va_tm=va_tm,
                     t_ready=t_ready, t_psfree=t_psfree):
                e.wait_ge(vch.h, max(t_ready, t_psfree))
                for hh in range(2):
                    for cch in range(4):
                        lhsT = kt_tm[:, cch * 1024 + pr * 128 + hh * 64:
                                     cch * 1024 + pr * 128 + (hh + 1) * 64]
                        rhs = va_tm[:, cch * 1040 + pr * 130 + hh * 65:
                                    cch * 1040 + pr * 130 + (hh + 1) * 65]
                        ins = e.matmul(out=psS[pp][hh * 64:(hh + 1) * 64, 0:65],
                                       lhsT=lhsT, rhs=rhs,
                                       start=(cch == 0), stop=(cch == 3))
                ins.then_inc(tch.h, 1)
            P.t(t_sc)
            t_scm = tch.inc(1)

            def v_sacc(e, si=si, pr=pr, pp=pp, t_scm=t_scm):
                e.wait_ge(tch.h, t_scm)
                if si == 0:
                    e.tensor_copy(out=ctx["S_sb"][:, pr * 65:(pr + 1) * 65],
                                  in_=psS[pp][:, 0:65]).then_inc(vch.h, 1)
                else:
                    e.tensor_tensor(out=ctx["S_sb"][:, pr * 65:(pr + 1) * 65],
                                    in0=ctx["S_sb"][:, pr * 65:(pr + 1) * 65],
                                    in1=psS[pp][:, 0:65],
                                    op=ALU.add).then_inc(vch.h, 1)
            P.v(v_sacc)
            state[f"sevac_any_{si * NCH + pr}"] = vch.inc(1)
    t_all = state[f"sevac_any_{nt * NCH - 1}"]

    t_out_target = st_c.inc(16)

    def sy_out(e, t_all=t_all, t_out_target=t_out_target):
        e.wait_ge(vch.h, t_all)
        e.dma_start(out=stout.rearrange("(n p) m -> p n m", p=128),
                    in_=ctx["S_sb"][:].rearrange("p (n m) -> p n m", n=NCH)
                    ).then_inc(st_c.h, 16)
        e.wait_ge(st_c.h, t_out_target)
    P.sy(sy_out)

    if not standalone:
        finals = [(vch.h, vch.n), (sch.h, sch.n), (tch.h, tch.n),
                  (gch.h, gch.n), (st_c.h, st_c.n), (ldb.h, ldb.n),
                  (ldw.h, ldw.n),
                  (ctx["ldx"][0].h, ctx["ldx"][0].n),
                  (ctx["ldx"][1].h, ctx["ldx"][1].n)]
        return finals

    block = es.enter_context(nc.Block())
    P.emit(block)
    es.close()
    return nc


def build_l1attn(T_own, G=None, io=None):
    EPS = 1e-6
    standalone = G is None
    if standalone:
        nc = bass.Bass("TRN2", target_bir_lowering=False)
        pre = ""
    else:
        nc = G["nc"]
        pre = "d_"
    io = io or {}

    def dram(name, shape, dt, kind):
        if name in io:
            return io[name]
        return nc.dram_tensor(name, shape, dt, kind=kind).ap()

    s2 = dram("s2", [T_own, D], F32, "ExternalInput")
    wqT = dram("wqT", [D, D], F32, "ExternalInput")
    wkT = dram("wkT", [D, D], F32, "ExternalInput")
    wvT = dram("wvT", [D, D], F32, "ExternalInput")
    woT = dram("woT", [D, D], F32, "ExternalInput")
    bq = dram("bq", [D], F32, "ExternalInput")
    bk = dram("bk", [D], F32, "ExternalInput")
    bv = dram("bv", [D], F32, "ExternalInput")
    bo = dram("bo", [D], F32, "ExternalInput")
    if "state_full" not in io:
        stv = dram("state_v", [64, 1024], F32, "ExternalInput")
        stk = dram("state_k", [64, 16], F32, "ExternalInput")
    else:
        stv = stk = None
    s3 = dram("s3", [T_own, D], F32, "ExternalOutput")

    if standalone:
        es = contextlib.ExitStack()
        P = Prog(nc)
    else:
        es, P = G["es"], G["P"]
        P.barrier_step(G["entry"])
    ctx, wtens, wstage, ldw = _l1_common(nc, es, P, T_own, with_q=True,
                                         with_wo=True, nbuf=1, abf_sz=66000,
                                         af_sz=19744, G=G, pre=pre)
    vch, sch, tch, gch = ctx["vch"], ctx["sch"], ctx["tch"], ctx["gch"]
    state = ctx["state"]
    st_c = ctx["st"]
    ldb = ctx["ldb"]
    ldxr = Ctr(es, nc, pre + "ldxr")
    ldxr1 = Ctr(es, nc, pre + "ldxr1")
    ldsi = Ctr(es, nc, pre + "ldsi")
    ldsh = Ctr(es, nc, pre + "ldsh")
    ctx["s2src"] = s2
    abf, af32 = ctx["abf"], ctx["af32"]
    psAB = ctx["psAB"]
    psT = ctx["psT"]
    bank_free = ctx["bank_free"]
    lnb = ctx["lnb"]

    tw = load_weights_bf(P, wstage, [(wtens["k"], wkT, D), (wtens["v"], wvT, D),
                                     (wtens["q"], wqT, D), (wtens["o"], woT, D)],
                         ldw, vch)
    ctx["w_k"] = (wtens["k"], tw[0])
    ctx["w_v"] = (wtens["v"], tw[1])
    ctx["w_q"] = (wtens["q"], tw[2])
    wo_bf, t_wo = wtens["o"], tw[3]
    ctx["t_wdone"] = tw[-1]

    attn_buf = ctx["xfm"][0]
    afm = abf.carve(NCH * 512)
    q_od = afm
    pt_sb = [abf.carve(512) for _ in range(2)]
    mask4 = abf.carve(512)
    out2 = af32.carve(4 * D)
    xres = af32.carve(2 * D)
    och = af32.carve(272)
    bo_bc = abf.carve(D)
    k_od = af32.carve(2048).bitcast(BF16)
    rden4 = och[:, 264:268]
    borow = xres[0:1, 0:D]
    S_v = af32.carve_high(1024)[0:64, :]
    S_k = af32.carve_high(16)[0:64, 0:16]
    S_vb = abf.carve_high(1024)[0:64, :]
    S_kb = abf.carve_high(32)[0:64, 0:16]
    if standalone:
        psO4 = [es.enter_context(nc.psum_tensor(f"psO4_{i}", [128, 260], F32))[:]
                for i in range(2)]
        _psSv = es.enter_context(nc.psum_tensor("psSv", [64, 512], F32))[:]
        psSv2 = [_psSv, _psSv]
        psSk = es.enter_context(nc.psum_tensor("psSk", [64, 16], F32))[:]
    else:
        psO4 = [G["psO0"][:, 0:260], G["psO0"][:, 512:772]]
        psSv2 = [G["psO1"][0:64, 0:512], G["psO1"][0:64, 512:1024]]
        psSk = G["psO0"][0:64, 1024:1040]
    ones_col_bf = abf.carve_high(32)[:, 0:1]

    bsrc = {"k": bk, "v": bv, "q": bq}
    boffm = {"k": 0, "v": NCH, "q": 2 * NCH}
    t_prev = 0
    for nm in ("k", "v", "q"):
        t_l = ldb.inc(16)

        def sy_b(e, nm=nm, t_prev=t_prev):
            wge(e, vch.h, t_prev)
            e.dma_start(out=ctx["bstg"][:],
                        in_=bsrc[nm].rearrange("(n p) -> n p", p=128)
                        ).then_inc(ldb.h, 16)
        P.sy(sy_b)

        def t_bt(e, t_l=t_l):
            e.wait_ge(ldb.h, t_l)
            wge(e, ctx["t_consts_h"], ctx["t_consts"])
            e.transpose(out=psAB[0][:, 0:NCH], in_=ctx["bstg"][:],
                        identity=ctx["ident_f32"][0:NCH, 0:NCH]).then_inc(tch.h, 1)
        P.t(t_bt)
        t_t = tch.inc(1)

        def v_bt(e, nm=nm, t_t=t_t):
            e.wait_ge(tch.h, t_t)
            e.tensor_copy(out=ctx["bias_fm"][:, boffm[nm]:boffm[nm] + NCH],
                          in_=psAB[0][:, 0:NCH]).then_inc(vch.h, 1)
        P.v(v_bt)
        t_prev = vch.inc(1)

    def v_b1d(e, t_prev=t_prev):
        e.wait_ge(vch.h, t_prev)
        e.tensor_scalar_add(out=ctx["bias_fm"][:, 3 * NCH:4 * NCH],
                            in0=ctx["bias_fm"][:, 0:NCH], scalar1=1.0)
        e.tensor_scalar_add(out=ctx["bias_fm"][:, 4 * NCH:5 * NCH],
                            in0=ctx["bias_fm"][:, 2 * NCH:3 * NCH],
                            scalar1=1.0).then_inc(vch.h, 1)
    P.v(v_b1d)
    ctx["t_bias"] = vch.inc(1)
    bank_free[0] = ctx["t_bias"]

    t_bo = ldb.inc(16)

    def sy_bo(e, t_bo_prev=t_prev):
        wge(e, vch.h, t_bo_prev)
        e.dma_start(out=borow[:], in_=bo.rearrange("(a d) -> a d", a=1)
                    ).then_inc(ldb.h, 16)
    P.sy(sy_bo)

    def t_bobc(e, t_bo=t_bo, t_bf=bank_free[0]):
        e.wait_ge(ldb.h, t_bo)
        wge(e, ctx["t_consts_h"], ctx["t_consts"])
        wge(e, vch.h, t_bf)
        e.matmul(out=psAB[0][:, 0:512], lhsT=ctx["ones_row"][:],
                 rhs=borow[0:1, 0:512], start=True, stop=True)
        e.matmul(out=psAB[1][:, 0:512], lhsT=ctx["ones_row"][:],
                 rhs=borow[0:1, 512:1024], start=True, stop=True).then_inc(tch.h, 1)
    P.t(t_bobc)
    t_bob = tch.inc(1)

    def v_bobc(e, t_bob=t_bob):
        e.wait_ge(tch.h, t_bob)
        e.tensor_copy(out=bo_bc[:, 0:512], in_=psAB[0][:, 0:512])
        e.tensor_copy(out=bo_bc[:, 512:1024], in_=psAB[1][:, 0:512]).then_inc(vch.h, 1)
    P.v(v_bobc)
    t_bobc_v = vch.inc(1)
    bank_free[0] = t_bobc_v
    bank_free[1] = t_bobc_v

    state_full = io.get("state_full")
    if state_full is not None:
        s_tmp = af32.carve_high(1056)[0:64, 0:1040]

        def sy_si(e):
            e.dma_start(out=s_tmp.rearrange("p (h m) -> p h m", h=16),
                        in_=state_full.rearrange("(h p) m -> p h m", p=64)
                        ).then_inc(ldsi.h, 16)
        P.sy(sy_si)
        t_si0 = ldsi.inc(16)

        def v_split(e):
            e.wait_ge(ldsi.h, t_si0)
            e.tensor_copy(out=S_v[:].rearrange("p (h m) -> p h m", h=16),
                          in_=s_tmp.rearrange("p (h m) -> p h m", h=16)[:, :, 0:64])
            e.tensor_copy(out=S_k[:],
                          in_=s_tmp[:, 64:1040:65]).then_inc(ldsi.h, 16)
        P.v(v_split)
        t_si = ldsi.inc(16)
    else:
        stv3 = stv.rearrange("p (h m) -> p h m", h=16)
        stk3 = stk.rearrange("p (h m) -> p h m", m=1)

        def sy_si(e):
            e.dma_start(out=S_v[:].rearrange("p (h m) -> p h m", h=16),
                        in_=stv3).then_inc(ldsi.h, 16)
            e.dma_start(out=S_k[:].rearrange("p (h m) -> p h m", m=1),
                        in_=stk3).then_inc(ldsi.h, 16)
        P.sy(sy_si)
        t_si = ldsi.inc(32)

    def v_init(e):
        e.wait_ge(ldsi.h, t_si)
        for p in range(ctx["nbuf"]):
            va = ctx["va_tm"][p]
            e.memset(va.rearrange("p (cc hh x) -> p cc hh x", cc=4, hh=16)[:, :, :, 64:65], 1.0)
        e.memset(ones_col_bf[:], 1.0)
        e.tensor_copy(out=S_vb[:], in_=S_v[:])
        e.tensor_copy(out=S_kb[:], in_=S_k[:])
        e.memset(mask4, 1.0).then_inc(vch.h, 1)
    P.v(v_init)
    ctx["t_vones"] = vch.inc(1)
    t_vinit = ctx["t_vones"]

    def g_mask(e, t_vinit=t_vinit):
        e.wait_ge(vch.h, t_vinit)
        e.affine_select(out=mask4, in_=mask4, pattern=[[0, 4], [1, 128]],
                        compare_op=ALU.is_ge, fill=0.0,
                        base=0, channel_multiplier=-1).then_inc(gch.h, 1)
    P.g(g_mask)
    t_mask = gch.inc(1)

    nt = T_own // 512
    s3_stored = 0
    t_supd_prev = t_vinit
    for si in range(nt):
        t_ready = l1_proj_tiles(ctx, si, si * 512, do_q=True)
        p = 0
        kt_tm = ctx["kt_tm"][p]
        va_tm = ctx["va_tm"][p]
        p_q = ctx["p_q"][p]
        p_k = ctx["p_k"][p]

        t_qod_used = state.get(f"afm_used_{si - 1}", 0)
        t_kod_used = state.get(f"kod_used_{si - 1}", 0)

        def sy_shift(e, t_ready=t_ready, t_qod_used=t_qod_used,
                     t_kod_used=t_kod_used):
            e.wait_ge(vch.h, t_ready)
            wge(e, tch.h, max(t_qod_used, t_kod_used))
            e.dma_start(out=q_od[0:64, :], in_=p_q[64:128, :]).then_inc(ldsh.h, 16)
            e.dma_start(out=k_od[0:64, :], in_=p_k[64:128, :]).then_inc(ldsh.h, 16)
        P.sy(sy_shift)
        t_shift = ldsh.inc(32)

        def v_shift(e, t_shift=t_shift):
            e.wait_ge(ldsh.h, t_shift)
            e.engine_nop().then_inc(vch.h, 1)
        P.v(v_shift)
        t_shift_v = vch.inc(1)

        def ksl(h, cch):
            src = p_k if h % 2 == 0 else k_od
            return src[0:64, (h // 2) * 512 + cch * 128:(h // 2) * 512 + (cch + 1) * 128]

        def qsl(h, cch):
            src = p_q if h % 2 == 0 else q_od
            return src[0:64, (h // 2) * 512 + cch * 128:(h // 2) * 512 + (cch + 1) * 128]

        for cch in range(4):
            for g4 in range(4):
                bk2 = g4 % 2
                t_bankfree = bank_free[bk2]
                t_pt_free = state.get(f"pvmm_{si}_{cch}_{g4 - 2}", 0) if g4 >= 2 \
                    else state.get(f"pvmm_{si}_{cch - 1}_{g4 + 2}", 0)

                def t_scor(e, cch=cch, g4=g4, bk2=bk2, t_bankfree=t_bankfree,
                           t_sh=t_shift_v):
                    e.wait_ge(vch.h, max(t_sh, t_bankfree))
                    for hh4 in range(4):
                        h = g4 * 4 + hh4
                        ins = e.matmul(out=psAB[bk2][:, hh4 * 128:(hh4 + 1) * 128],
                                       lhsT=ksl(h, cch), rhs=qsl(h, cch),
                                       start=True, stop=True)
                    ins.then_inc(tch.h, 1)
                P.t(t_scor)
                t_sg = tch.inc(1)

                def v_mask(e, g4=g4, bk2=bk2, t_sg=t_sg, t_mask=t_mask,
                           t_pt_free=t_pt_free):
                    e.wait_ge(tch.h, t_sg)
                    wge(e, gch.h, t_mask)
                    wge(e, tch.h, t_pt_free)
                    e.scalar_tensor_tensor(out=pt_sb[g4 % 2], in0=psAB[bk2][:, 0:512],
                                           scalar=1.0, in1=mask4,
                                           op0=ALU.mult, op1=ALU.mult
                                           ).then_inc(vch.h, 1)
                P.v(v_mask)
                bank_free[bk2] = vch.inc(1)
                t_pt = bank_free[bk2]

                om = g4 % 2
                t_o4free = state.get(f"oevac_{si}_{cch}_{g4 - 2}", 0) if g4 >= 2 \
                    else state.get(f"oevac_{si}_{cch - 1}_{g4 + 2}", 0)

                def t_pv(e, cch=cch, g4=g4, om=om, t_pt=t_pt, t_o4free=t_o4free,
                         t_supd=t_supd_prev):
                    e.wait_ge(vch.h, max(t_pt, t_o4free, t_supd))
                    for hh4 in range(4):
                        h = g4 * 4 + hh4
                        pr = h // 2
                        e.matmul(out=psO4[om][:, hh4 * 65:(hh4 + 1) * 65],
                                 lhsT=pt_sb[g4 % 2][:, hh4 * 128:(hh4 + 1) * 128],
                                 rhs=va_tm[:, cch * 1040 + pr * 130 + (h % 2) * 65:
                                           cch * 1040 + pr * 130 + (h % 2 + 1) * 65],
                                 start=True, stop=False)
                        e.matmul(out=psO4[om][:, hh4 * 65:hh4 * 65 + 64],
                                 lhsT=qsl(h, cch),
                                 rhs=S_vb[0:64, h * 64:(h + 1) * 64],
                                 start=False, stop=False)
                        ins = e.matmul(out=psO4[om][:, hh4 * 65 + 64:hh4 * 65 + 65],
                                       lhsT=qsl(h, cch),
                                       rhs=S_kb[0:64, h:h + 1],
                                       start=False, stop=True)
                    ins.then_inc(tch.h, 1)
                P.t(t_pv)
                state[f"pvmm_{si}_{cch}_{g4}"] = tch.inc(1)
                t_pvm = state[f"pvmm_{si}_{cch}_{g4}"]

                t_och_free = state.get("t_och", 0)

                def v_oe(e, g4=g4, om=om, t_pvm=t_pvm, t_och_free=t_och_free):
                    e.wait_ge(tch.h, t_pvm)
                    wge(e, vch.h, t_och_free)
                    e.tensor_copy(out=och[:, 0:260], in_=psO4[om][:, 0:260]
                                  ).then_inc(vch.h, 1)
                P.v(v_oe)
                state[f"oevac_{si}_{cch}_{g4}"] = vch.inc(1)
                t_oe = state[f"oevac_{si}_{cch}_{g4}"]

                def v_eps(e, t_oe=t_oe):
                    e.wait_ge(vch.h, t_oe)
                    e.tensor_scalar_add(out=och[:, 64:260:65], in0=och[:, 64:260:65],
                                        scalar1=EPS).then_inc(vch.h, 1)
                P.v(v_eps)
                t_ep = vch.inc(1)

                def v_rec(e, t_ep=t_ep):
                    e.wait_ge(vch.h, t_ep)
                    e.reciprocal(out=rden4[:, 0:4], in_=och[:, 64:260:65]
                                 ).then_inc(vch.h, 1)
                P.v(v_rec)
                t_rc = vch.inc(1)

                t_abuf_used = state.get("abuf_used", 0)

                def v_div(e, cch=cch, g4=g4, t_rc=t_rc, t_abuf_used=t_abuf_used):
                    e.wait_ge(vch.h, t_rc)
                    wge(e, tch.h, t_abuf_used)
                    for hh4 in range(4):
                        h = g4 * 4 + hh4
                        ins = e.tensor_scalar_mul(
                            out=attn_buf[:, cch * 1024 + h * 64:cch * 1024 + (h + 1) * 64],
                            in0=och[:, hh4 * 65:hh4 * 65 + 64],
                            scalar1=rden4[:, hh4:hh4 + 1])
                    ins.then_inc(vch.h, 1)
                P.v(v_div)
                state["t_och"] = vch.inc(1)
            t_pv_all = state[f"pvmm_{si}_{cch}_{3}"]

            nsv = 2 if psSv2[0] is not psSv2[1] else 1
            for half in range(2):
                svb = half % nsv
                t_svfree = state.get(f"svev_{(si * 4 + cch) * 2 + half - nsv}", 0)

                def t_sv(e, cch=cch, half=half, svb=svb, t_ready=t_ready,
                         t_svfree=t_svfree, t_pv_all=t_pv_all):
                    e.wait_ge(vch.h, max(t_ready, t_svfree))
                    wge(e, tch.h, t_pv_all)
                    psSv = psSv2[svb]
                    for hh in range(8):
                        h = half * 8 + hh
                        pr = h // 2
                        ins = e.matmul(
                            out=psSv[:, hh * 64:(hh + 1) * 64],
                            lhsT=kt_tm[:, cch * 1024 + pr * 128 + (h % 2) * 64:
                                       cch * 1024 + pr * 128 + (h % 2 + 1) * 64],
                            rhs=va_tm[:, cch * 1040 + pr * 130 + (h % 2) * 65:
                                      cch * 1040 + pr * 130 + (h % 2) * 65 + 64],
                            start=True, stop=True)
                    ins.then_inc(tch.h, 1)
                P.t(t_sv)
                t_svm = tch.inc(1)

                def v_svacc(e, half=half, svb=svb, t_svm=t_svm):
                    e.wait_ge(tch.h, t_svm)
                    e.tensor_tensor(out=S_v[:, half * 512:(half + 1) * 512],
                                    in0=S_v[:, half * 512:(half + 1) * 512],
                                    in1=psSv2[svb][:, 0:512], op=ALU.add
                                    ).then_inc(vch.h, 1)
                P.v(v_svacc)
                state[f"svev_{(si * 4 + cch) * 2 + half}"] = vch.inc(1)
            t_sv_done = state[f"svev_{(si * 4 + cch) * 2 + 1}"]

            t_skfree = state.get(f"skev_{si * 4 + cch - 1}", 0)

            def t_sk(e, cch=cch, t_ready=t_ready, t_skfree=t_skfree,
                     t_pv_all=t_pv_all, t_vo=t_vinit):
                e.wait_ge(vch.h, max(t_ready, t_skfree, t_vo))
                wge(e, tch.h, t_pv_all)
                for h in range(16):
                    pr = h // 2
                    ins = e.matmul(
                        out=psSk[:, h:h + 1],
                        lhsT=kt_tm[:, cch * 1024 + pr * 128 + (h % 2) * 64:
                                   cch * 1024 + pr * 128 + (h % 2 + 1) * 64],
                        rhs=ones_col_bf[:], start=True, stop=True)
                ins.then_inc(tch.h, 1)
            P.t(t_sk)
            t_skm = tch.inc(1)

            def v_skacc(e, t_skm=t_skm):
                e.wait_ge(tch.h, t_skm)
                e.tensor_tensor(out=S_k[:], in0=S_k[:], in1=psSk[:, 0:16],
                                op=ALU.add).then_inc(vch.h, 1)
            P.v(v_skacc)
            state[f"skev_{si * 4 + cch}"] = vch.inc(1)
            t_sk_done = state[f"skev_{si * 4 + cch}"]

            def v_sbf(e, t_sv_done=t_sv_done, t_sk_done=t_sk_done):
                e.wait_ge(vch.h, max(t_sv_done, t_sk_done))
                e.tensor_copy(out=S_vb[:], in_=S_v[:])
                e.tensor_copy(out=S_kb[:], in_=S_k[:]).then_inc(vch.h, 1)
            P.v(v_sbf)
            t_supd_prev = vch.inc(1)

        t_attn_all = state["t_och"]
        for cch in range(4):
            for ech in range(NCH):
                bk2 = 0
                t_tfree = state.get(f"afm_ev_{(cch * NCH + ech) - 1}", 0)

                def t_at(e, cch=cch, ech=ech, bk2=bk2, t_attn_all=t_attn_all,
                         t_tfree=t_tfree, si=si):
                    e.wait_ge(vch.h, max(t_attn_all, t_tfree))
                    e.transpose(out=psT[bk2][:, 0:128],
                                in_=attn_buf[:, cch * 1024 + ech * 128:
                                             cch * 1024 + (ech + 1) * 128],
                                identity=ctx["ident_bf"][:]).then_inc(tch.h, 1)
                P.t(t_at)
                t_tg = tch.inc(1)

                def v_ae(e, cch=cch, ech=ech, bk2=bk2, t_tg=t_tg):
                    e.wait_ge(tch.h, t_tg)
                    e.tensor_copy(out=afm[:, ech * 512 + cch * 128:
                                          ech * 512 + (cch + 1) * 128],
                                  in_=psT[bk2][:, 0:128]).then_inc(vch.h, 1)
                P.v(v_ae)
                state[f"afm_ev_{cch * NCH + ech}"] = vch.inc(1)
        t_afm = state[f"afm_ev_{4 * NCH - 1}"]
        state["abuf_used"] = tch.n

        for tb in range(4):
            t_bf0 = bank_free[0]
            t_bf1 = bank_free[1]

            def t_om(e, tb=tb, t_afm=t_afm, t_wo=t_wo, t_bf0=t_bf0, t_bf1=t_bf1):
                e.wait_ge(vch.h, max(t_afm, t_wo, t_bf0, t_bf1))
                for ech in range(NCH):
                    lh = afm[:, ech * 512 + tb * 128:ech * 512 + (tb + 1) * 128]
                    e.matmul(out=psAB[0][:, 0:512], lhsT=lh,
                             rhs=wo_bf[:, ech * D:ech * D + 512],
                             start=(ech == 0), stop=(ech == NCH - 1))
                    ins = e.matmul(out=psAB[1][:, 0:512], lhsT=lh,
                                   rhs=wo_bf[:, ech * D + 512:ech * D + 1024],
                                   start=(ech == 0), stop=(ech == NCH - 1))
                ins.then_inc(tch.h, 1)
            P.t(t_om)
            t_omm = tch.inc(1)
            if tb == 3:
                state[f"afm_used_{si}"] = t_omm
                state[f"kod_used_{si}"] = state[f"pvmm_{si}_{3}_{3}"]

            def v_oev(e, tb=tb, t_omm=t_omm, t_bo=t_bobc_v, t_s3=s3_stored):
                e.wait_ge(tch.h, t_omm)
                wge(e, vch.h, t_bo)
                wge(e, st_c.h, t_s3)
                e.tensor_tensor(out=out2[:, tb * D:tb * D + 512],
                                in0=psAB[0][:, 0:512], in1=bo_bc[:, 0:512],
                                op=ALU.add)
                e.tensor_tensor(out=out2[:, tb * D + 512:(tb + 1) * D],
                                in0=psAB[1][:, 0:512], in1=bo_bc[:, 512:1024],
                                op=ALU.add).then_inc(vch.h, 1)
            P.v(v_oev)
            bank_free[0] = vch.inc(1)
            bank_free[1] = bank_free[0]
        t_out2 = bank_free[0]

        def v_pre(e, t_out2=t_out2):
            e.wait_ge(vch.h, t_out2)
        P.v(v_pre)
        t_ap2 = emit_ln(P, lnb, vch, sch, nblk=4,
                        src=out2, srcol=lambda j: j * D,
                        dst=out2, dstcol=lambda j: j * D)

        for j in range(4):
            px = j % 2
            lx = ldxr if px == 0 else ldxr1
            t_pfree = state.get(f"dres_{si}_{j - 2}", 0)

            def sy_xr(e, si=si, j=j, px=px, lx=lx, t_s3=s3_stored,
                      t_pfree=t_pfree):
                wge(e, st_c.h, t_s3)
                wge(e, vch.h, t_pfree)
                if si == 0 and j == 0:
                    wge(e, vch.h, t_bobc_v)
                e.dma_start(out=xres[:, px * D:(px + 1) * D],
                            in_=s2[si * 512 + j * 128:si * 512 + (j + 1) * 128, :]
                            ).then_inc(lx.h, 16)
            P.sy(sy_xr)
            t_xr = lx.inc(16)

            def v_res(e, j=j, px=px, lx=lx, t_ap2=t_ap2, t_xr=t_xr,
                      t_s3=s3_stored):
                if j == 0:
                    e.wait_ge(sch.h, t_ap2)
                    wge(e, st_c.h, t_s3)
                e.wait_ge(lx.h, t_xr)
                e.tensor_tensor(out=out2[:, j * D:(j + 1) * D],
                                in0=out2[:, j * D:(j + 1) * D],
                                in1=xres[:, px * D:(px + 1) * D],
                                op=ALU.add).then_inc(vch.h, 1)
            P.v(v_res)
            state[f"dres_{si}_{j}"] = vch.inc(1)
        t_rv = state[f"dres_{si}_{3}"]

        def sy_s3(e, si=si, t_rv=t_rv):
            e.wait_ge(vch.h, t_rv)
            e.dma_start(out=s3[si * 512:(si + 1) * 512, :].rearrange(
                "(j p) c -> p j c", p=128),
                in_=out2.rearrange("p (j c) -> p j c", j=4)).then_inc(st_c.h, 16)
        P.sy(sy_s3)
        s3_stored = st_c.inc(16)

    t_fin = s3_stored

    def sy_fin(e, t=t_fin):
        e.wait_ge(st_c.h, t)
    P.sy(sy_fin)

    if not standalone:
        finals = [(vch.h, vch.n), (sch.h, sch.n), (tch.h, tch.n),
                  (gch.h, gch.n), (st_c.h, st_c.n), (ldb.h, ldb.n),
                  (ldw.h, ldw.n), (ldxr.h, ldxr.n), (ldxr1.h, ldxr1.n),
                  (ldsi.h, ldsi.n), (ldsh.h, ldsh.n),
                  (ctx["ldx"][0].h, ctx["ldx"][0].n),
                  (ctx["ldx"][1].h, ctx["ldx"][1].n)]
        return finals

    block = es.enter_context(nc.Block())
    P.emit(block)
    es.close()
    return nc




def build_l1ffn(T_own, G=None, io=None):
    standalone = G is None
    if standalone:
        nc = bass.Bass("TRN2", target_bir_lowering=False)
        pre = ""
    else:
        nc = G["nc"]
        pre = "e_"
    io = io or {}

    def dram(name, shape, dt, kind):
        if name in io:
            return io[name]
        return nc.dram_tensor(name, shape, dt, kind=kind).ap()

    s3e = dram("s3e", [T_own + 1, D], F32, "ExternalInput")
    wrT = dram("wrT", [D, D], F32, "ExternalInput")
    wkT = dram("wkT", [D, D], F32, "ExternalInput")
    wvT = dram("wvT", [D, D], F32, "ExternalInput")
    mu_r = dram("mu_r", [D], F32, "ExternalInput")
    mu_k = dram("mu_k", [D], F32, "ExternalInput")
    s4 = dram("s4", [T_own, D], F32, "ExternalOutput")

    if standalone:
        es = contextlib.ExitStack()
        P = Prog(nc)
    else:
        es, P = G["es"], G["P"]
        P.barrier_step(G["entry"])
    vch = Ctr(es, nc, pre + "vch"); sch = Ctr(es, nc, pre + "sch")
    tch = Ctr(es, nc, pre + "tch"); gch = Ctr(es, nc, pre + "gch")
    ldw = Ctr(es, nc, pre + "ldw")
    ldx = [Ctr(es, nc, f"{pre}ldx{p}") for p in range(2)]
    ldb = Ctr(es, nc, pre + "ldb")
    ldxr = Ctr(es, nc, pre + "ldxr")
    st_c = Ctr(es, nc, pre + "st")

    if standalone:
        consts = build_consts(nc, es, P, gch)
        t_consts_h, t_consts = gch.h, consts["ready"]
    else:
        consts = G["consts"]
        t_consts_h, t_consts = G["consts_sem"], G["consts_ready"]
    ident_f32 = consts["ident_f32"]

    TP1 = T_own + 1
    WIN = 513
    if standalone:
        abf = Arena(nc, es, "abf", BF16, 46600)
    else:
        abf = G["abf"]
        abf.reset()

    wr_bf = abf.carve(NCH * D)
    wk_bf = abf.carve(NCH * D)
    wv_bf = abf.carve(NCH * D)
    ln4 = abf.carve(NCH * WIN)
    xr_fm = abf.carve(NCH * 512)
    xk_fm = abf.carve(NCH * 512)
    k2_fm = abf.carve(NCH * 512)
    r_tm = abf.carve(4 * 1024)

    if standalone:
        af32b = Arena(nc, es, "af32b", F32, 14200)
    else:
        af32b = G["af32"]
        af32b.reset()
    wstage = af32b.carve(NCH * D // 2)
    af32b.reset()
    xst1 = af32b.carve(5 * D)
    xst = [xst1, xst1]
    out2 = af32b.carve(4 * D)
    xres = af32b.carve(4 * D)
    tmpE = af32b.carve(512)

    lnb = (G["lnb"] if G is not None else mk_ln_bufs(nc, es, 5))
    mub = af32b.carve_high(2 * NCH)
    bstg = af32b.carve_high(128)[0:NCH, :]
    dbuf = abf.carve_high(512)

    if standalone:
        def v_veps_init(e):
            e.memset(lnb["veps"][:], 1.0).then_inc(vch.h, 1)
        P.v(v_veps_init)
        vch.inc(1)

    if standalone:
        psA = es.enter_context(nc.psum_tensor("psA", [128, 512], F32))[:]
        psB = es.enter_context(nc.psum_tensor("psB", [128, 512], F32))[:]
        psC = es.enter_context(nc.psum_tensor("psC", [128, 512], F32))[:]
        psD = es.enter_context(nc.psum_tensor("psD", [128, 512], F32))[:]
    else:
        psA, psB = G["psA"], G["psB"]
        psC = G["psO0"][:, 0:512]
        psD = G["psO0"][:, 512:1024]
    psAB = [psA, psB]
    psCD = [psC, psD]

    tw = load_weights_bf(P, wstage, [(wr_bf, wrT, D), (wk_bf, wkT, D),
                                     (wv_bf, wvT, D)], ldw, vch)
    t_wr, t_wk, t_wv = tw
    t_wdone = tw[-1]

    state = {}
    bank_free = {0: 0, 1: 0}
    t_prev = 0
    for (i, mu) in enumerate((mu_r, mu_k)):
        t_l = ldb.inc(16)

        def sy_mu(e, mu=mu, t_prev=t_prev):
            wge(e, vch.h, t_prev)
            e.dma_start(out=bstg[:], in_=mu.rearrange("(n p) -> n p", p=128)
                        ).then_inc(ldb.h, 16)
        P.sy(sy_mu)

        def t_mu(e, t_l=t_l):
            e.wait_ge(ldb.h, t_l)
            wge(e, t_consts_h, t_consts)
            e.transpose(out=psA[:, 0:NCH], in_=bstg[:],
                        identity=ident_f32[0:NCH, 0:NCH]).then_inc(tch.h, 1)
        P.t(t_mu)
        t_t = tch.inc(1)

        def v_mu(e, i=i, t_t=t_t):
            e.wait_ge(tch.h, t_t)
            e.tensor_copy(out=mub[:, i * NCH:(i + 1) * NCH],
                          in_=psA[:, 0:NCH]).then_inc(vch.h, 1)
        P.v(v_mu)
        t_prev = vch.inc(1)
    t_mu_done = t_prev
    bank_free[0] = t_mu_done

    s4_stored = 0
    apply_done = {}
    for si in range(T_own // 512):
        t0 = si * 512
        p = 0
        blocks = [(j * D, j * D, 128) for j in range(4)] + [(4 * D, 4 * D, 1)]
        t_apfree = apply_done.get(si - 1, 0)
        t_tp_prev = state.get(f"etp_{si - 1}", 0)

        def sy_ld(e, t0=t0, t_apfree=t_apfree, t_tp_prev=t_tp_prev, si=si):
            if si == 0:
                wge(e, vch.h, t_wdone)
            wge(e, sch.h, t_apfree)
            wge(e, tch.h, t_tp_prev)
            e.dma_start(out=xst[0].rearrange("p (j c) -> p j c", j=5)[:, 0:4, :],
                        in_=s3e[t0:t0 + 512, :].rearrange(
                            "(j p) c -> p j c", p=128)).then_inc(ldx[0].h, 16)
            e.dma_start(out=xst[0][0:1, 4 * D:5 * D],
                        in_=s3e[t0 + 512:t0 + 513, :]).then_inc(ldx[0].h, 16)
        P.sy(sy_ld)
        t_ldx = ldx[0].inc(32)

        def v_pre0(e, t_ldx=t_ldx):
            e.wait_ge(ldx[0].h, t_ldx)
        P.v(v_pre0)
        t_apply = emit_ln_blocks(P, lnb, vch, sch, src=xst[0], dst=xst[0],
                                 blocks=blocks, war_s=(tch.h, t_tp_prev))
        apply_done[si] = t_apply

        t_mix_used_prev = state.get(f"emix_used_{si - 1}", 0)
        for j in range(5):
            rr = 128 if j < 4 else 1
            for h in range(2):
                bk = (j * 2 + h) % 2
                t_bankfree = bank_free[bk]

                def t_tp(e, j=j, h=h, rr=rr, bk=bk, t_bankfree=t_bankfree,
                         t_ap=t_apply):
                    e.wait_ge(sch.h, t_ap)
                    wge(e, t_consts_h, t_consts)
                    wge(e, vch.h, t_bankfree)
                    for cc in range(4):
                        c = 4 * h + cc
                        ins = e.transpose(
                            out=psAB[bk][:, cc * 128:cc * 128 + rr],
                            in_=xst[0][0:rr, j * D + c * 128:j * D + (c + 1) * 128],
                            identity=ident_f32[0:rr, 0:rr])
                    ins.then_inc(tch.h, 1)
                P.t(t_tp)
                t_tpg = tch.inc(1)

                def v_ev(e, j=j, h=h, rr=rr, bk=bk, t_tpg=t_tpg,
                         t_mu2=t_mix_used_prev):
                    e.wait_ge(tch.h, t_tpg)
                    wge(e, tch.h, t_mu2)
                    dst = ln4.rearrange("p (c t) -> p c t", c=NCH)
                    e.tensor_copy(out=dst[:, 4 * h:4 * h + 4, j * 128:j * 128 + rr],
                                  in_=psAB[bk][:].rearrange("p (c t) -> p c t", c=4)[:, :, 0:rr]
                                  ).then_inc(vch.h, 1)
                P.v(v_ev)
                bank_free[bk] = vch.inc(1)
        state[f"etp_{si}"] = tch.n
        t_ln4 = vch.n

        t_xr_prev = state.get(f"emix_used_{si - 1}", 0)
        for c in range(NCH):
            t_dprev = state.get(f"emix_{si}_{c - 1}", 0)

            def v_d(e, c=c, t_ln4=t_ln4, t_xr_prev=t_xr_prev, t_dprev=t_dprev):
                e.wait_ge(vch.h, max(t_ln4, t_dprev))
                wge(e, tch.h, t_xr_prev)
                e.tensor_tensor(out=dbuf[:],
                                in0=ln4[:, c * WIN + 1:c * WIN + 513],
                                in1=ln4[:, c * WIN:c * WIN + 512],
                                op=ALU.subtract).then_inc(vch.h, 1)
            P.v(v_d)
            t_d = vch.inc(1)

            def v_xr(e, c=c, t_d=t_d, t_mu=t_mu_done):
                e.wait_ge(vch.h, max(t_d, t_mu))
                e.scalar_tensor_tensor(out=xr_fm[:, c * 512:(c + 1) * 512],
                                       in0=dbuf[:], scalar=mub[:, c:c + 1],
                                       in1=ln4[:, c * WIN:c * WIN + 512],
                                       op0=ALU.mult, op1=ALU.add)
                e.scalar_tensor_tensor(out=xk_fm[:, c * 512:(c + 1) * 512],
                                       in0=dbuf[:], scalar=mub[:, NCH + c:NCH + c + 1],
                                       in1=ln4[:, c * WIN:c * WIN + 512],
                                       op0=ALU.mult, op1=ALU.add).then_inc(vch.h, 1)
            P.v(v_xr)
            state[f"emix_{si}_{c}"] = vch.inc(1)
        t_mix = state[f"emix_{si}_{NCH - 1}"]

        for tb in range(4):
            bk = tb % 2
            t_bankfree = bank_free[bk]
            t_rtm_used = state.get(f"ekv_ev_{si - 1}_{tb}", 0)

            def t_wr_mm(e, tb=tb, bk=bk, t_mix=t_mix, t_w=t_wr,
                        t_bankfree=t_bankfree):
                e.wait_ge(vch.h, max(t_mix, t_w, t_bankfree))
                for c in range(NCH):
                    e.matmul(out=psAB[bk][:, 0:512],
                             lhsT=xr_fm[:, c * 512 + tb * 128:c * 512 + (tb + 1) * 128],
                             rhs=wr_bf[:, c * D:c * D + 512],
                             start=(c == 0), stop=(c == NCH - 1))
                    ins = e.matmul(out=psCD[bk][:, 0:512],
                                   lhsT=xr_fm[:, c * 512 + tb * 128:c * 512 + (tb + 1) * 128],
                                   rhs=wr_bf[:, c * D + 512:c * D + 1024],
                                   start=(c == 0), stop=(c == NCH - 1))
                ins.then_inc(tch.h, 1)
            P.t(t_wr_mm)
            t_mm = tch.inc(1)

            def s_sig(e, tb=tb, bk=bk, t_mm=t_mm, t_rtm_used=t_rtm_used):
                e.wait_ge(tch.h, t_mm)
                wge(e, vch.h, t_rtm_used)
                e.activation(out=r_tm[:, tb * D:tb * D + 512], in_=psAB[bk][:, 0:512],
                             func=AF.Sigmoid)
                e.activation(out=r_tm[:, tb * D + 512:(tb + 1) * D],
                             in_=psCD[bk][:, 0:512],
                             func=AF.Sigmoid).then_inc(sch.h, 1)
            P.s(s_sig)
            t_sg = sch.inc(1)

            def v_free(e, t_sg=t_sg):
                e.wait_ge(sch.h, t_sg)
                e.engine_nop().then_inc(vch.h, 1)
            P.v(v_free)
            bank_free[bk] = vch.inc(1)
            state[f"er_{si}_{tb}"] = bank_free[bk]
        t_r_all_s = sch.n

        for ech in range(NCH):
            bk = ech % 2
            t_bankfree = bank_free[bk]
            t_tmp_used = state.get("ek2_last", 0)

            def t_wk_mm(e, ech=ech, bk=bk, t_mix=t_mix, t_w=t_wk,
                        t_bankfree=t_bankfree):
                e.wait_ge(vch.h, max(t_mix, t_w, t_bankfree))
                for c in range(NCH):
                    ins = e.matmul(out=psAB[bk][:, 0:512],
                                   lhsT=wk_bf[:, c * D + ech * 128:c * D + (ech + 1) * 128],
                                   rhs=xk_fm[:, c * 512:(c + 1) * 512],
                                   start=(c == 0), stop=(c == NCH - 1))
                ins.then_inc(tch.h, 1)
            P.t(t_wk_mm)
            t_mm = tch.inc(1)

            def v_relu(e, ech=ech, bk=bk, t_mm=t_mm, t_tmp_used=t_tmp_used):
                e.wait_ge(tch.h, t_mm)
                wge(e, vch.h, t_tmp_used)
                e.tensor_scalar_max(out=tmpE[:, 0:512], in0=psAB[bk][:, 0:512],
                                    scalar1=0.0).then_inc(vch.h, 1)
            P.v(v_relu)
            t_rl = vch.inc(1)
            bank_free[bk] = t_rl

            def v_sq(e, ech=ech, t_rl=t_rl):
                e.wait_ge(vch.h, t_rl)
                e.tensor_tensor(out=k2_fm[:, ech * 512:(ech + 1) * 512],
                                in0=tmpE[:, 0:512], in1=tmpE[:, 0:512],
                                op=ALU.mult).then_inc(vch.h, 1)
            P.v(v_sq)
            state["ek2_last"] = vch.inc(1)
        t_k2 = state["ek2_last"]

        for tb in range(4):
            bk = tb % 2
            t_bankfree = bank_free[bk]

            def t_wv_mm(e, tb=tb, bk=bk, t_k2=t_k2, t_w=t_wv,
                        t_bankfree=t_bankfree):
                e.wait_ge(vch.h, max(t_k2, t_w, t_bankfree))
                for ech in range(NCH):
                    e.matmul(out=psAB[bk][:, 0:512],
                             lhsT=k2_fm[:, ech * 512 + tb * 128:ech * 512 + (tb + 1) * 128],
                             rhs=wv_bf[:, ech * D:ech * D + 512],
                             start=(ech == 0), stop=(ech == NCH - 1))
                    ins = e.matmul(out=psCD[bk][:, 0:512],
                                   lhsT=k2_fm[:, ech * 512 + tb * 128:ech * 512 + (tb + 1) * 128],
                                   rhs=wv_bf[:, ech * D + 512:ech * D + 1024],
                                   start=(ech == 0), stop=(ech == NCH - 1))
                ins.then_inc(tch.h, 1)
            P.t(t_wv_mm)
            t_mm = tch.inc(1)
            state[f"ekv_ev_{si}_{tb}"] = t_mm

            def v_kv(e, tb=tb, bk=bk, t_mm=t_mm, t_rs=t_r_all_s,
                     t_s4=s4_stored):
                e.wait_ge(tch.h, t_mm)
                e.wait_ge(sch.h, t_rs)
                wge(e, st_c.h, t_s4)
                e.tensor_tensor(out=out2[:, tb * D:tb * D + 512],
                                in0=psAB[bk][:, 0:512],
                                in1=r_tm[:, tb * D:tb * D + 512], op=ALU.mult)
                e.tensor_tensor(out=out2[:, tb * D + 512:(tb + 1) * D],
                                in0=psCD[bk][:, 0:512],
                                in1=r_tm[:, tb * D + 512:(tb + 1) * D],
                                op=ALU.mult).then_inc(vch.h, 1)
            P.v(v_kv)
            bank_free[bk] = vch.inc(1)
        t_out2 = bank_free[(4 - 1) % 2]

        def v_pre2(e, t_out2=t_out2, t_o2=bank_free[0]):
            e.wait_ge(vch.h, max(t_out2, t_o2))
        P.v(v_pre2)
        t_ap2 = emit_ln(P, lnb, vch, sch, nblk=4,
                        src=out2, srcol=lambda j: j * D,
                        dst=out2, dstcol=lambda j: j * D)

        def sy_xr(e, t0=t0, t_s4=s4_stored):
            wge(e, st_c.h, t_s4)
            e.dma_start(out=xres.rearrange("p (j c) -> p j c", j=4),
                        in_=s3e[t0 + 1:t0 + 513, :].rearrange(
                            "(j p) c -> p j c", p=128)).then_inc(ldxr.h, 16)
        P.sy(sy_xr)
        t_xr = ldxr.inc(16)

        def v_res(e, t_ap2=t_ap2, t_xr=t_xr, t_s4=s4_stored):
            e.wait_ge(sch.h, t_ap2)
            e.wait_ge(ldxr.h, t_xr)
            wge(e, st_c.h, t_s4)
            e.tensor_tensor(out=out2[:, 0:4 * D], in0=out2[:, 0:4 * D],
                            in1=xres[:, 0:4 * D], op=ALU.add).then_inc(vch.h, 1)
        P.v(v_res)
        t_rv = vch.inc(1)

        def sy_s4(e, t0=t0, t_rv=t_rv):
            e.wait_ge(vch.h, t_rv)
            e.dma_start(out=s4[t0:t0 + 512, :].rearrange("(j p) c -> p j c", p=128),
                        in_=out2.rearrange("p (j c) -> p j c", j=4)
                        ).then_inc(st_c.h, 16)
        P.sy(sy_s4)
        s4_stored = st_c.inc(16)
        state[f"emix_used_{si}"] = state[f"ekv_ev_{si}_{3}"]

    def sy_fin(e, t=s4_stored):
        e.wait_ge(st_c.h, t)
    P.sy(sy_fin)

    if not standalone:
        finals = [(vch.h, vch.n), (sch.h, sch.n), (tch.h, tch.n),
                  (gch.h, gch.n), (st_c.h, st_c.n), (ldb.h, ldb.n),
                  (ldw.h, ldw.n), (ldxr.h, ldxr.n),
                  (ldx[0].h, ldx[0].n), (ldx[1].h, ldx[1].n)]
        return finals

    block = es.enter_context(nc.Block())
    P.emit(block)
    es.close()
    return nc



def _emit_exchange(G, tag, contribs, cc_in, cc_out, outs, entry):
    nc, es, P = G["nc"], G["es"], G["P"]
    P.barrier_step(entry)
    af32 = G["af32"]
    af32.reset()
    n = cc_in.shape[-1]
    nrow = len(contribs)
    stg = af32.carve(n)
    vch = Ctr(es, nc, f"{tag}_vch")
    ld = Ctr(es, nc, f"{tag}_ld")
    ccs = Ctr(es, nc, f"{tag}_ccs")
    wr = Ctr(es, nc, f"{tag}_wr")

    for i, (src, mask) in enumerate(contribs):
        t_l = ld.inc(16)
        t_wprev = wr.n
        t_vprev = vch.n

        def sy_l(e, src=src, t_wprev=t_wprev, t_vprev=t_vprev):
            wge(e, vch.h, t_vprev)
            wge(e, wr.h, t_wprev)
            e.dma_start(out=stg[0:1, 0:n], in_=src).then_inc(ld.h, 16)
        P.sy(sy_l)
        t_m = vch.inc(1)

        def v_m(e, mask=mask, t_l=t_l):
            e.wait_ge(ld.h, t_l)
            e.tensor_scalar_mul(out=stg[0:1, 0:n], in0=stg[0:1, 0:n],
                                scalar1=mask).then_inc(vch.h, 1)
        P.v(v_m)
        t_w = wr.inc(16)

        def sy_w(e, i=i, t_m=t_m):
            e.wait_ge(vch.h, t_m)
            e.dma_start(out=cc_in[i:i + 1, :], in_=stg[0:1, 0:n]).then_inc(wr.h, 16)
        P.sy(sy_w)

    def g_cc(e, t_w=wr.n):
        e.wait_ge(wr.h, t_w)
        e.collective_compute(
            "AllReduce", ALU.add,
            replica_groups=[[0, 1], [2, 3], [4, 5], [6, 7]],
            ins=[cc_in.opt()], outs=[cc_out.opt()]).then_inc(ccs.h, 1)
    P.g(g_cc)
    t_cc = ccs.inc(1)

    for (row, mask, dst) in outs:
        t_l = ld.inc(16)
        t_wprev = wr.n
        t_vprev = vch.n

        def sy_l2(e, row=row, t_cc=t_cc, t_vprev=t_vprev, t_wprev=t_wprev):
            e.wait_ge(ccs.h, t_cc)
            wge(e, vch.h, t_vprev)
            wge(e, wr.h, t_wprev)
            e.dma_start(out=stg[0:1, 0:n], in_=cc_out[row:row + 1, :]
                        ).then_inc(ld.h, 16)
        P.sy(sy_l2)
        t_m = vch.inc(1)

        def v_m2(e, mask=mask, t_l=t_l):
            e.wait_ge(ld.h, t_l)
            e.tensor_scalar_mul(out=stg[0:1, 0:n], in0=stg[0:1, 0:n],
                                scalar1=mask).then_inc(vch.h, 1)
        P.v(v_m2)
        t_w = wr.inc(16)

        def sy_w2(e, dst=dst, t_m=t_m):
            e.wait_ge(vch.h, t_m)
            e.dma_start(out=dst, in_=stg[0:1, 0:n]).then_inc(wr.h, 16)
        P.sy(sy_w2)

    def sy_waitw(e, t=wr.n):
        e.wait_ge(wr.h, t)
    P.sy(sy_waitw)
    return [(vch.h, vch.n), (ld.h, ld.n), (ccs.h, ccs.n), (wr.h, wr.n)]


def _emit_state_exchange(G, stloc, cc_in, cc_out, state_masked,
                         m_even_col, m_odd_col, entry):
    nc, es, P = G["nc"], G["es"], G["P"]
    P.barrier_step(entry)
    af32 = G["af32"]
    af32.reset()
    stg = af32.carve(520)
    vch = Ctr(es, nc, "x2_vch")
    ld = Ctr(es, nc, "x2_ld")
    ccs = Ctr(es, nc, "x2_ccs")
    wr = Ctr(es, nc, "x2_wr")

    t_l0 = ld.inc(16)

    def sy_l0(e):
        e.dma_start(out=stg[:, 0:520].rearrange("p (n m) -> p n m", n=NCH),
                    in_=stloc.rearrange("(n p) m -> p n m", p=128)
                    ).then_inc(ld.h, 16)
    P.sy(sy_l0)
    t_m = vch.inc(1)

    def v_m(e, t_l0=t_l0):
        e.wait_ge(ld.h, t_l0)
        e.tensor_scalar_mul(out=stg[:, 0:520], in0=stg[:, 0:520],
                            scalar1=m_even_col).then_inc(vch.h, 1)
    P.v(v_m)
    t_w1 = wr.inc(16)

    def sy_w(e, t_m=t_m):
        e.wait_ge(vch.h, t_m)
        e.dma_start(out=cc_in.rearrange("(n p) m -> p n m", p=128),
                    in_=stg[:, 0:520].rearrange("p (n m) -> p n m", n=NCH)
                    ).then_inc(wr.h, 16)
    P.sy(sy_w)

    def g_cc(e, t_w1=t_w1):
        e.wait_ge(wr.h, t_w1)
        e.collective_compute(
            "AllReduce", ALU.add,
            replica_groups=[[0, 1], [2, 3], [4, 5], [6, 7]],
            ins=[cc_in.opt()], outs=[cc_out.opt()]).then_inc(ccs.h, 1)
    P.g(g_cc)
    t_cc = ccs.inc(1)
    t_l = ld.inc(16)

    def sy_l(e, t_cc=t_cc):
        e.wait_ge(ccs.h, t_cc)
        e.dma_start(out=stg[:, 0:520].rearrange("p (n m) -> p n m", n=NCH),
                    in_=cc_out.rearrange("(n p) m -> p n m", p=128)
                    ).then_inc(ld.h, 16)
    P.sy(sy_l)
    t_m2 = vch.inc(1)

    def v_m2(e, t_l=t_l):
        e.wait_ge(ld.h, t_l)
        e.tensor_scalar_mul(out=stg[:, 0:520], in0=stg[:, 0:520],
                            scalar1=m_odd_col).then_inc(vch.h, 1)
    P.v(v_m2)
    t_w2 = wr.inc(16)

    def sy_w2(e, t_m2=t_m2):
        e.wait_ge(vch.h, t_m2)
        e.dma_start(out=state_masked.rearrange("(n p) m -> p n m", p=128),
                    in_=stg[:, 0:520].rearrange("p (n m) -> p n m", n=NCH)
                    ).then_inc(wr.h, 16)
        e.wait_ge(wr.h, t_w2)
    P.sy(sy_w2)
    return [(vch.h, vch.n), (ld.h, ld.n), (ccs.h, ccs.n), (wr.h, wr.n)]


def build_fused(T_own, T_full, QG=512, KSC=512, ST=512):
    nc = bass.Bass("TRN2", target_bir_lowering=False)
    es = contextlib.ExitStack()
    P = Prog(nc)

    def din(name, shape):
        return nc.dram_tensor(name, shape, F32, kind="ExternalInput").ap()

    x_full = din("x_full", [T_full, D])
    x_own = din("x_own", [T_own, D])
    wqT0 = din("wqT0", [D, D]); wkT0 = din("wkT0", [D, D]); wvT0 = din("wvT0", [D, D])
    w1t = din("w1t", [D, 4096]); cw = din("cw", [4096, 3]); cb = din("cb", [4096])
    w2t = din("w2t", [4096, D])
    wqT1 = din("wqT1", [D, D]); wkT1 = din("wkT1", [D, D]); wvT1 = din("wvT1", [D, D])
    woT1 = din("woT1", [D, D])
    bq1 = din("bq1", [D]); bk1 = din("bk1", [D]); bv1 = din("bv1", [D]); bo1 = din("bo1", [D])
    wrT2 = din("wrT2", [D, D]); wkT2 = din("wkT2", [D, D]); wvT2 = din("wvT2", [D, D])
    mu_r = din("mu_r", [D]); mu_k = din("mu_k", [D])
    m_even = din("m_even", [128, 1])
    m_odd = din("m_odd", [128, 1])
    s4 = nc.dram_tensor("s4", [T_own, D], F32, kind="ExternalOutput").ap()

    s1_int = nc.dram_tensor("s1_int", [T_own + 2, D], F32, kind="Internal").ap()
    s2_int = nc.dram_tensor("s2_int", [T_own, D], F32, kind="Internal").ap()
    s3_int = nc.dram_tensor("s3_int", [T_own + 1, D], F32, kind="Internal").ap()
    stloc = nc.dram_tensor("stloc", [D, 65], F32, kind="Internal").ap()
    stmask = nc.dram_tensor("stmask", [D, 65], F32, kind="Internal").ap()
    cc1_in = nc.dram_tensor("cc1_in", [2, D], F32, kind="Internal").ap()
    cc1_out = nc.dram_tensor("cc1_out", [2, D], F32, kind="Internal").ap()
    cc2_in = nc.dram_tensor("cc2_in", [D, 65], F32, kind="Internal").ap()
    cc2_out = nc.dram_tensor("cc2_out", [D, 65], F32, kind="Internal").ap()
    cc3_in = nc.dram_tensor("cc3_in", [1, D], F32, kind="Internal").ap()
    cc3_out = nc.dram_tensor("cc3_out", [1, D], F32, kind="Internal").ap()

    kgch = Ctr(es, nc, "k_gch")
    consts = build_consts(nc, es, P, kgch)
    abf = Arena(nc, es, "abf", BF16, 66000)
    af32 = Arena(nc, es, "af32", F32, 19744)
    psA = es.enter_context(nc.psum_tensor("psA", [128, 512], F32))[:]
    psB = es.enter_context(nc.psum_tensor("psB", [128, 512], F32))[:]
    psO0 = es.enter_context(nc.psum_tensor("psO0", [128, 1536], F32))[:]
    psO1 = es.enter_context(nc.psum_tensor("psO1", [128, 1536], F32))[:]
    lnb = mk_ln_bufs(nc, es, 5)
    msk_sb = es.enter_context(nc.sbuf_tensor("msk_sb", [128, 2], F32))
    ldm = Ctr(es, nc, "k_ldm")

    def sy_msk(e):
        e.dma_start(out=msk_sb[:, 0:1], in_=m_even).then_inc(ldm.h, 16)
        e.dma_start(out=msk_sb[:, 1:2], in_=m_odd).then_inc(ldm.h, 16)
    P.sy(sy_msk)
    t_msk = ldm.inc(32)

    def v_lninit(e):
        e.wait_ge(ldm.h, t_msk)
        e.memset(lnb["veps"][:], 1.0).then_inc(ldm.h, 16)
    P.v(v_lninit)
    t_lninit = ldm.inc(16)

    G = dict(nc=nc, es=es, P=P, consts=consts, consts_sem=kgch.h,
             consts_ready=kgch.n, abf=abf, af32=af32,
             psA=psA, psB=psB, psO0=psO0, psO1=psO1, lnb=lnb)

    G["entry"] = []
    fin_a = build_l0attn(T_own, T_full, QG=QG, KSC=KSC, G=G, io=dict(
        x_full=x_full, x_own=x_own, wqT=wqT0, wkT=wkT0, wvT=wvT0,
        s1=s1_int[1:T_own + 1, :]))

    mev1 = msk_sb[0:1, 0:1]
    mod1 = msk_sb[0:1, 1:2]
    fin_x1 = _emit_exchange(
        G, "x1",
        contribs=[(s1_int[T_own:T_own + 1, :], mev1),
                  (s1_int[1:2, :], mod1)],
        cc_in=cc1_in, cc_out=cc1_out,
        outs=[(0, mod1, s1_int[0:1, :]),
              (1, mev1, s1_int[T_own + 1:T_own + 2, :])],
        entry=fin_a + [(ldm.h, t_msk)])

    G["entry"] = fin_a + fin_x1 + [(ldm.h, t_lninit)]
    fin_b = build_l0ffn(T_own, ST=ST, G=G, io=dict(
        s1e=s1_int, w1t=w1t, cw=cw, cb=cb, w2t=w2t, s2=s2_int))

    G["entry"] = fin_b
    fin_c = build_l1state(T_own, G=G, io=dict(
        s2=s2_int, wkT=wkT1, wvT=wvT1, bk=bk1, bv=bv1, state_loc=stloc))

    fin_x2 = _emit_state_exchange(
        G, stloc, cc2_in, cc2_out, stmask, msk_sb[:, 0:1], msk_sb[:, 1:2],
        entry=fin_c + [(ldm.h, t_msk)])

    G["entry"] = fin_c + fin_x2
    fin_d = build_l1attn(T_own, G=G, io=dict(
        s2=s2_int, wqT=wqT1, wkT=wkT1, wvT=wvT1, woT=woT1,
        bq=bq1, bk=bk1, bv=bv1, bo=bo1,
        state_full=stmask,
        s3=s3_int[1:T_own + 1, :]))

    fin_x3 = _emit_exchange(
        G, "x3",
        contribs=[(s3_int[T_own:T_own + 1, :], mev1)],
        cc_in=cc3_in, cc_out=cc3_out,
        outs=[(0, mod1, s3_int[0:1, :])],
        entry=fin_d + [(ldm.h, t_msk)])

    G["entry"] = fin_d + fin_x3 + [(ldm.h, t_lninit)]
    fin_e = build_l1ffn(T_own, G=G, io=dict(
        s3e=s3_int, wrT=wrT2, wkT=wkT2, wvT=wvT2, mu_r=mu_r, mu_k=mu_k,
        s4=s4))

    block = es.enter_context(nc.Block())
    P.emit(block)
    es.close()
    return nc



_B, _T, _DM = 4, 8192, 1024
_TO = _T // 2

_cache = {}


def _fused_prog():
    if "F" not in _cache:
        _cache["F"] = build_fused(_TO, _T)
    return _cache["F"]


def _tr(w):
    return np.ascontiguousarray(np.asarray(w).T.astype(np.float32))


def kernel(L0_attn_x, L0_attn_Wq, L0_attn_Wk, L0_attn_Wv,
           L0_ffn_w1, L0_ffn_conv_w, L0_ffn_conv_b, L0_ffn_w2,
           L1_attn_Wq, L1_attn_bq, L1_attn_Wk, L1_attn_bk, L1_attn_Wv,
           L1_attn_bv, L1_attn_Wo, L1_attn_bo,
           L1_ffn_W_r, L1_ffn_W_k, L1_ffn_W_v, L1_ffn_mu_r, L1_ffn_mu_k):
    prog = _fused_prog()
    x = np.ascontiguousarray(np.asarray(L0_attn_x, dtype=np.float32))
    cores = list(range(8))
    f32 = lambda a: np.ascontiguousarray(np.asarray(a, dtype=np.float32))
    base = dict(
        wqT0=_tr(L0_attn_Wq), wkT0=_tr(L0_attn_Wk), wvT0=_tr(L0_attn_Wv),
        w1t=_tr(L0_ffn_w1),
        cw=f32(np.asarray(L0_ffn_conv_w).reshape(4096, 3)),
        cb=f32(L0_ffn_conv_b), w2t=_tr(L0_ffn_w2),
        wqT1=_tr(L1_attn_Wq), wkT1=_tr(L1_attn_Wk), wvT1=_tr(L1_attn_Wv),
        woT1=_tr(L1_attn_Wo),
        bq1=f32(L1_attn_bq), bk1=f32(L1_attn_bk), bv1=f32(L1_attn_bv),
        bo1=f32(L1_attn_bo),
        wrT2=_tr(L1_ffn_W_r), wkT2=_tr(L1_ffn_W_k), wvT2=_tr(L1_ffn_W_v),
        mu_r=f32(L1_ffn_mu_r), mu_k=f32(L1_ffn_mu_k),
    )
    in_maps = []
    for c in cores:
        b, h = c // 2, c % 2
        m = dict(base)
        m["x_full"] = x[b]
        m["x_own"] = x[b, h * _TO:(h + 1) * _TO]
        m["m_even"] = np.full((128, 1), 1.0 - h, np.float32)
        m["m_odd"] = np.full((128, 1), float(h), np.float32)
        in_maps.append(m)
    res = run_bass_kernel_spmd(prog, in_maps, core_ids=cores)
    out = np.empty((_B, _T, _DM), np.float32)
    for c in cores:
        b, h = c // 2, c % 2
        out[b, h * _TO:(h + 1) * _TO] = np.asarray(res.results[c]["s4"])
    return out


# version 12 (final)
# speedup vs baseline: 1.0358x; 1.0358x over previous
import contextlib
import numpy as np
import concourse.bass as bass
import concourse.mybir as mybir
from concourse.bass_utils import run_bass_kernel_spmd

F32 = mybir.dt.float32
BF16 = mybir.dt.bfloat16
FP8 = mybir.dt.float8e4
AF = mybir.ActivationFunctionType
ALU = mybir.AluOpType
AX = mybir.AxisListType

MAGIC = 12582912.0
ENGINES = ("sync", "vector", "scalar", "tensor", "gpsimd")
D = 1024
NCH = D // 128


class Ctr:
    def __init__(self, es, nc, name):
        self.h = es.enter_context(nc.semaphore(name))
        self.n = 0

    def inc(self, amt=1):
        self.n += amt
        return self.n


class Prog:
    def __init__(self, nc):
        self.nc = nc
        self.steps = {k: [] for k in ENGINES}

    def barrier_step(self, waits):
        waits = [w for w in waits if w[1]]
        for k in ENGINES:
            def fn(e, waits=tuple(waits)):
                for (h, n) in waits:
                    e.wait_ge(h, n)
            self.steps[k].append(fn)

    def sy(self, fn): self.steps["sync"].append(fn)
    def v(self, fn): self.steps["vector"].append(fn)
    def s(self, fn): self.steps["scalar"].append(fn)
    def t(self, fn): self.steps["tensor"].append(fn)
    def g(self, fn): self.steps["gpsimd"].append(fn)

    def emit(self, block):
        for key in ENGINES:
            fns = self.steps[key]
            if not fns:
                continue
            dec = getattr(block, key)

            def make(fns):
                def _(eng):
                    for f in fns:
                        f(eng)
                return _
            dec(make(fns))


def wge(e, h, n):
    if n:
        e.wait_ge(h, n)


class Arena:
    def __init__(self, nc, es, name, dtype, n_elems):
        self.t = es.enter_context(nc.sbuf_tensor(name, [128, n_elems], dtype))
        self.n = n_elems
        self.off = 0
        self.high = n_elems

    def carve(self, n):
        n_al = (n + 31) & ~31
        assert self.off + n_al <= self.high, \
            f"arena overflow {self.off}+{n_al}>{self.high}"
        ap = self.t[:, self.off:self.off + n]
        self.off += n_al
        return ap

    def carve_high(self, n):
        n_al = (n + 31) & ~31
        assert self.off <= self.high - n_al, "arena high overflow"
        self.high -= n_al
        return self.t[:, self.high:self.high + n]

    def reset(self):
        self.off = 0
        self.high = self.n


def build_consts(nc, es, P, gch):
    ident_f32 = es.enter_context(nc.sbuf_tensor("ident_f32", [128, 128], F32))
    ident_bf = es.enter_context(nc.sbuf_tensor("ident_bf", [128, 128], BF16))
    ones_bf = es.enter_context(nc.sbuf_tensor("ones_bf", [128, 1], BF16))
    ones_row = es.enter_context(nc.sbuf_tensor("ones_row", [1, 128], F32))

    def g_fn(g):
        g.memset(ident_bf[:], 1.0).then_inc(gch.h, 1)
        g.memset(ident_f32[:], 1.0).then_inc(gch.h, 1)
        g.memset(ones_bf[:], 1.0)
        g.memset(ones_row[:], 1.0)
        g.wait_ge(gch.h, 2)
        g.affine_select(out=ident_bf[:], in_=ident_bf[:], pattern=[[-1, 128]],
                        compare_op=ALU.is_equal, fill=0.0,
                        base=0, channel_multiplier=1)
        g.affine_select(out=ident_f32[:], in_=ident_f32[:], pattern=[[-1, 128]],
                        compare_op=ALU.is_equal, fill=0.0,
                        base=0, channel_multiplier=1).then_inc(gch.h, 1)
    P.g(g_fn)
    gch.inc(3)
    return dict(ident_bf=ident_bf, ident_f32=ident_f32, ones_bf=ones_bf,
                ones_row=ones_row, ready=gch.n)


def load_weights_bf(P, stage, names, ldw, vch):
    out = []
    prev_cast = 0
    for (wbf, ap, c_out) in names:
        hn = NCH // 2
        for hf in range(2):
            src = ap[hf * (D // 2):(hf + 1) * (D // 2), :].rearrange(
                "(n p) m -> p n m", p=128)
            t_ld = ldw.inc(16)

            def sy_fn(e, src=src, c_out=c_out, prev_cast=prev_cast, hn=hn):
                wge(e, vch.h, prev_cast)
                e.dma_start(
                    out=stage[:, 0:hn * c_out].rearrange("p (n m) -> p n m", n=hn),
                    in_=src).then_inc(ldw.h, 16)
            P.sy(sy_fn)
            t_cast = vch.inc(1)

            def v_fn(e, wbf=wbf, c_out=c_out, t_ld=t_ld, hf=hf, hn=hn):
                e.wait_ge(ldw.h, t_ld)
                e.tensor_copy(out=wbf[:, hf * hn * c_out:(hf + 1) * hn * c_out],
                              in_=stage[:, 0:hn * c_out]).then_inc(vch.h, 1)
            P.v(v_fn)
            prev_cast = t_cast
        out.append(prev_cast)
    return out


def emit_ln_blocks(P, bufs, vch, sch, *, src, dst, blocks, war_s=None):
    stats, mv, veps, rstd, negmr = (bufs[k] for k in
                                    ("stats", "mv", "veps", "rstd", "negmr"))
    nblk = len(blocks)

    def v_stats(e):
        for j, (sc, dc, r) in enumerate(blocks):
            e.bn_stats(out=stats[0:r, 12 * j:12 * j + 6],
                       in_=src[0:r, sc:sc + 512])
            ins = e.bn_stats(out=stats[0:r, 12 * j + 6:12 * j + 12],
                             in_=src[0:r, sc + 512:sc + D])
        ins.then_inc(vch.h, 1)
    P.v(v_stats)
    t_stats = vch.inc(1)

    def v_aggr(e):
        e.wait_ge(vch.h, t_stats)
        for j, (sc, dc, r) in enumerate(blocks):
            ins = e.bn_aggr(out=mv[0:r, 2 * j:2 * j + 2],
                            in_=stats[0:r, 12 * j:12 * j + 12])
        ins.then_inc(vch.h, 1)
    P.v(v_aggr)
    t_aggr = vch.inc(1)

    def v_veps(e):
        e.wait_ge(vch.h, t_aggr)
        for j, (sc, dc, r) in enumerate(blocks):
            ins = e.tensor_scalar_add(out=veps[0:r, j:j + 1],
                                      in0=mv[0:r, 2 * j + 1:2 * j + 2], scalar1=1e-5)
        ins.then_inc(vch.h, 1)
    P.v(v_veps)
    t_veps = vch.inc(1)

    maxr = max(r for (_, _, r) in blocks)

    def s_ln(e):
        e.wait_ge(vch.h, t_veps)
        e.activation(out=rstd[0:maxr, 0:nblk], in_=veps[0:maxr, 0:nblk],
                     func=AF.Ln).then_inc(sch.h, 1)
    P.s(s_ln)
    t_ln = sch.inc(1)

    def s_exp(e):
        e.wait_ge(sch.h, t_ln)
        e.activation(out=rstd[0:maxr, 0:nblk], in_=rstd[0:maxr, 0:nblk],
                     func=AF.Exp, scale=-0.5).then_inc(sch.h, 1)
    P.s(s_exp)
    t_rstd = sch.inc(1)

    def v_negmr(e):
        e.wait_ge(sch.h, t_rstd)
        for j, (sc, dc, r) in enumerate(blocks):
            ins = e.scalar_tensor_tensor(out=negmr[0:r, j:j + 1],
                                         in0=mv[0:r, 2 * j:2 * j + 1], scalar=-1.0,
                                         in1=rstd[0:r, j:j + 1],
                                         op0=ALU.mult, op1=ALU.mult)
        ins.then_inc(vch.h, 1)
    P.v(v_negmr)
    t_negmr = vch.inc(1)

    def s_apply(e):
        e.wait_ge(vch.h, t_negmr)
        if war_s is not None:
            h, n = war_s
            wge(e, h, n)
        for j, (sc, dc, r) in enumerate(blocks):
            ins = e.activation(out=dst[0:r, dc:dc + D],
                               in_=src[0:r, sc:sc + D],
                               func=AF.Identity, bias=negmr[0:r, j:j + 1],
                               scale=rstd[0:r, j:j + 1])
        ins.then_inc(sch.h, 1)
    P.s(s_apply)
    return sch.inc(1)


def emit_ln(P, bufs, vch, sch, *, nblk, src, srcol, dst, dstcol, war_s=None):
    blocks = [(srcol(j), dstcol(j), 128) for j in range(nblk)]
    return emit_ln_blocks(P, bufs, vch, sch, src=src, dst=dst, blocks=blocks,
                          war_s=war_s)


def mk_ln_bufs(nc, es, maxblk):
    return dict(
        stats=es.enter_context(nc.sbuf_tensor("lnstats", [128, 12 * maxblk], F32)),
        mv=es.enter_context(nc.sbuf_tensor("lnmv", [128, 2 * maxblk], F32)),
        veps=es.enter_context(nc.sbuf_tensor("lnveps", [128, maxblk], F32)),
        rstd=es.enter_context(nc.sbuf_tensor("lnrstd", [128, maxblk], F32)),
        negmr=es.enter_context(nc.sbuf_tensor("lnnegmr", [128, maxblk], F32)),
    )



def build_l0attn(T_own, T_full, QG=512, KSC=512, G=None, io=None):
    standalone = G is None
    if standalone:
        nc = bass.Bass("TRN2", target_bir_lowering=False)
        pre = ""
    else:
        nc = G["nc"]
        pre = "a_"
    io = io or {}

    def dram(name, shape, dt, kind):
        if name in io:
            return io[name]
        return nc.dram_tensor(name, shape, dt, kind=kind).ap()

    x_full = dram("x_full", [T_full, D], F32, "ExternalInput")
    x_own = dram("x_own", [T_own, D], F32, "ExternalInput")
    wqT = dram("wqT", [D, D], F32, "ExternalInput")
    wkT = dram("wkT", [D, D], F32, "ExternalInput")
    wvT = dram("wvT", [D, D], F32, "ExternalInput")
    s1 = dram("s1", [T_own, D], F32, "ExternalOutput")

    kfm_d = nc.dram_tensor("kfm_d", [NCH, 128, T_full], BF16, kind="Internal")
    vtm_d = nc.dram_tensor("vtm_d", [T_full, 1025], BF16, kind="Internal")
    qfm_d = nc.dram_tensor("qfm_d", [NCH, 128, T_own], BF16, kind="Internal")

    TILE = 512
    NBLK = TILE // 128
    nt_full = T_full // TILE
    nt_own = T_own // TILE
    n_ksc = T_full // KSC
    n_kc = KSC // 128
    n_groups = T_own // QG
    NSUB = QG // 128
    NCHU_F = T_full // 128

    if standalone:
        es = contextlib.ExitStack()
        P = Prog(nc)
    else:
        es, P = G["es"], G["P"]
        P.barrier_step(G["entry"])

    vch = Ctr(es, nc, pre + "vch")
    sch = Ctr(es, nc, pre + "sch")
    tch = Ctr(es, nc, pre + "tch")
    gch = Ctr(es, nc, pre + "gch")
    ldw = Ctr(es, nc, pre + "ldw")
    ldx = [Ctr(es, nc, f"{pre}ldx{p}") for p in range(2)]
    splp = [Ctr(es, nc, f"{pre}splp{p}") for p in range(2)]
    splv = [Ctr(es, nc, f"{pre}splv{p}") for p in range(2)]
    ldqg = Ctr(es, nc, pre + "ldqg")
    ldkv = [Ctr(es, nc, f"{pre}ldkv{p}") for p in range(2)]
    ldxr = Ctr(es, nc, pre + "ldxr")
    st = Ctr(es, nc, pre + "st")

    if standalone:
        consts = build_consts(nc, es, P, gch)
        t_consts_h, t_consts = gch.h, consts["ready"]
    else:
        consts = G["consts"]
        t_consts_h, t_consts = G["consts_sem"], G["consts_ready"]
    ident_f32 = consts["ident_f32"]
    ident_bf = consts["ident_bf"]
    ones_bf = consts["ones_bf"]
    ones_row = consts["ones_row"]

    if standalone:
        abf = Arena(nc, es, "abf", BF16, 66000)
        af32 = Arena(nc, es, "af32", F32, 19200)
    else:
        abf, af32 = G["abf"], G["af32"]
        abf.reset()
        af32.reset()

    wq_bf = abf.carve(NCH * D)
    wk_bf = abf.carve(NCH * D)
    wv_bf = abf.carve(NCH * D)
    xfm = [abf.carve(NCH * TILE) for _ in range(2)]
    xbf1 = abf.carve(NBLK * D)
    xbf = [xbf1, xbf1]
    prj = [abf.carve(NCH * TILE) for _ in range(2)]
    vsb = [abf.carve(NBLK * 1025) for _ in range(2)]
    a1_bf_end = abf.off
    xst = [af32.carve(NBLK * D) for _ in range(2)]
    wstage = abf.carve(NCH * D).bitcast(F32)
    a1_f32_end = af32.off

    ksq = [abf.carve_high(512) for _ in range(2)]
    lnb = (G["lnb"] if not standalone else mk_ln_bufs(nc, es, 4))
    normq = af32.carve_high(max(T_own, 512))[0:1, :]
    _misc = af32.carve_high(96)
    nkv = _misc[0:1, 0:512] if False else af32.carve_high(512)[0:1, :]
    nk1 = _misc[0:1, 0:1]
    mkrun = _misc[0:1, 1:2]
    mksc = _misc[0:1, 2:3]
    m_q = normq
    tsb0 = af32.carve_high(QG)
    tsb = [tsb0, tsb0]
    m_bc = af32.carve_high(QG)
    rden = _misc[:, 32:36]

    W = load_weights_bf(P, wstage, [(wq_bf, wqT, D), (wk_bf, wkT, D),
                                    (wv_bf, wvT, D)], ldw, vch)
    t_wq, t_wk, t_wv = W

    if standalone:
        psA = es.enter_context(nc.psum_tensor("psA", [128, 512], F32))[:]
        psB = es.enter_context(nc.psum_tensor("psB", [128, 512], F32))[:]
        psO0 = es.enter_context(nc.psum_tensor("psO0", [128, 1536], F32))[:]
        psO1 = es.enter_context(nc.psum_tensor("psO1", [128, 1536], F32))[:]
    else:
        psA, psB = G["psA"], G["psB"]
        psO0 = G["psO0"]
        psO1 = G["psO1"]
    psAB = [psA, psB]
    psO = [psO0[:, 0:1024], psO1[:, 0:1024]]
    psDen = psO1[:, 1024:1536]

    def v_init(e):
        e.memset(vsb[0][:, 1024::1025], 1.0)
        e.memset(vsb[1][:, 1024::1025], 1.0)
        e.memset(mkrun[:], 0.0).then_inc(vch.h, 1)
    P.v(v_init)
    t_vinit = vch.inc(1)

    bank_free = {0: 0, 1: 0}
    state = {}
    apply_done = {}

    def emit_tile(i, gi, src_ap, pass_kind):
        p = gi % 2
        t_apfree = apply_done.get(gi - 2, 0)
        t_tp_prev = state.get(f"tp_{gi - 1}", 0)
        src = src_ap[i * TILE:(i + 1) * TILE, :].rearrange("(j p) c -> p j c", p=128)

        def sy_load(e, src=src, p=p, t_apfree=t_apfree, t_tp_prev=t_tp_prev):
            wge(e, sch.h, t_apfree)
            wge(e, tch.h, t_tp_prev)
            e.dma_start(out=xst[p].rearrange("p (j c) -> p j c", j=NBLK),
                        in_=src).then_inc(ldx[p].h, 16)
        P.sy(sy_load)
        t_ldx = ldx[p].inc(16)

        def v_predma(e, p=p, t_ldx=t_ldx):
            e.wait_ge(ldx[p].h, t_ldx)
        P.v(v_predma)
        t_apply = emit_ln(P, lnb, vch, sch, nblk=NBLK,
                          src=xst[p], srcol=lambda j: j * D,
                          dst=xbf[p], dstcol=lambda j: j * D,
                          war_s=(tch.h, t_tp_prev))
        apply_done[gi] = t_apply

        for j in range(NBLK):
            for h in range(2):
                bk = (j * 2 + h) % 2
                t_bankfree = bank_free[bk]

                def t_tp(e, p=p, j=j, h=h, bk=bk, t_bankfree=t_bankfree,
                         t_ap=t_apply):
                    e.wait_ge(sch.h, t_ap)
                    wge(e, t_consts_h, t_consts)
                    wge(e, vch.h, t_bankfree)
                    pbf = psAB[bk].bitcast(BF16)
                    for cc in range(4):
                        c = 4 * h + cc
                        ins = e.transpose(
                            out=pbf[:, cc * 128:(cc + 1) * 128],
                            in_=xbf[p][:, j * D + c * 128:j * D + (c + 1) * 128],
                            identity=ident_bf[:])
                    ins.then_inc(tch.h, 1)
                P.t(t_tp)
                t_tpg = tch.inc(1)

                def v_evac_tp(e, p=p, j=j, h=h, bk=bk, t_tpg=t_tpg):
                    e.wait_ge(tch.h, t_tpg)
                    dst = xfm[p].rearrange("p (c t) -> p c t", c=NCH)
                    e.tensor_copy(out=dst[:, 4 * h:4 * h + 4, j * 128:(j + 1) * 128],
                                  in_=psAB[bk].bitcast(BF16)[:, 0:512]
                                  .rearrange("p (c t) -> p c t", c=4)
                                  ).then_inc(vch.h, 1)
                P.v(v_evac_tp)
                bank_free[bk] = vch.inc(1)
        state[f"tp_{gi}"] = tch.n
        t_xfm_ready = vch.n

        if pass_kind == "kv":
            plist = [("k", wk_bf, t_wk, None), ("v", wv_bf, t_wv, None)]
        else:
            plist = [("q", wq_bf, t_wq, 1.0 / 32.0)]

        for (nm, wbf, t_w, scale) in plist:
            if nm in ("k", "q"):
                spl_sem = splp[p]
                t_spl_prev = state.get(f"splp_done_{gi - 2}", 0)
                for e_ch in range(NCH):
                    bk = e_ch % 2
                    t_bankfree = bank_free[bk]

                    def t_proj(e, p=p, e_ch=e_ch, bk=bk, wbf=wbf, t_w=t_w,
                               t_bankfree=t_bankfree, t_x=t_xfm_ready):
                        e.wait_ge(vch.h, max(t_x, t_w, t_bankfree))
                        for c in range(NCH):
                            ins = e.matmul(
                                out=psAB[bk][:, 0:TILE],
                                lhsT=wbf[:, c * D + e_ch * 128:c * D + (e_ch + 1) * 128],
                                rhs=xfm[p][:, c * TILE:(c + 1) * TILE],
                                start=(c == 0), stop=(c == NCH - 1))
                        ins.then_inc(tch.h, 1)
                    P.t(t_proj)
                    t_mm = tch.inc(1)

                    def v_evac(e, p=p, e_ch=e_ch, bk=bk, scale=scale, t_mm=t_mm,
                               spl_sem=spl_sem, t_spl_prev=t_spl_prev):
                        e.wait_ge(tch.h, t_mm)
                        if e_ch == 0:
                            wge(e, spl_sem.h, t_spl_prev)
                        if scale is None:
                            e.tensor_copy(out=prj[p][:, e_ch * TILE:(e_ch + 1) * TILE],
                                          in_=psAB[bk][:, 0:TILE]).then_inc(vch.h, 1)
                        else:
                            e.tensor_scalar_mul(out=prj[p][:, e_ch * TILE:(e_ch + 1) * TILE],
                                                in0=psAB[bk][:, 0:TILE],
                                                scalar1=scale).then_inc(vch.h, 1)
                    P.v(v_evac)
                    bank_free[bk] = vch.inc(1)
                    t_prj_e = bank_free[bk]

                    kq = ksq[e_ch % 2]
                    t_nk_prev = state.get(f"nk_{(e_ch - 2) % NCH}", 0)

                    def v_ksq(e, p=p, e_ch=e_ch, kq=kq, t_prj_e=t_prj_e,
                              t_nk_prev=t_nk_prev):
                        e.wait_ge(vch.h, t_prj_e)
                        wge(e, tch.h, t_nk_prev)
                        e.tensor_tensor(out=kq[:, 0:TILE],
                                        in0=prj[p][:, e_ch * TILE:(e_ch + 1) * TILE],
                                        in1=prj[p][:, e_ch * TILE:(e_ch + 1) * TILE],
                                        op=ALU.mult).then_inc(vch.h, 1)
                    P.v(v_ksq)
                    t_ksq = vch.inc(1)

                    t_o0 = state.get("t_o0_free", 0)

                    def t_nk(e, e_ch=e_ch, kq=kq, t_ksq=t_ksq, t_o0=t_o0):
                        e.wait_ge(vch.h, t_ksq)
                        wge(e, t_consts_h, t_consts)
                        if e_ch == 0:
                            wge(e, vch.h, t_o0)
                        e.matmul(out=psO0[0:1, 0:TILE], lhsT=ones_bf[:],
                                 rhs=kq[:, 0:TILE], start=(e_ch == 0),
                                 stop=(e_ch == NCH - 1)).then_inc(tch.h, 1)
                    P.t(t_nk)
                    state[f"nk_{e_ch}"] = tch.inc(1)
                t_nk_last = state[f"nk_{NCH - 1}"]

                if pass_kind == "q":
                    def v_nk_evac(e, i=i, t_nk_last=t_nk_last):
                        e.wait_ge(tch.h, t_nk_last)
                        e.tensor_copy(out=normq[0:1, i * TILE:(i + 1) * TILE],
                                      in_=psO0[0:1, 0:TILE]).then_inc(vch.h, 1)
                    P.v(v_nk_evac)
                    state["t_o0_free"] = vch.inc(1)
                else:
                    t_red_prev = state.get("t_nk_red", 0)

                    def v_nk_evac(e, t_nk_last=t_nk_last, t_red_prev=t_red_prev):
                        e.wait_ge(tch.h, t_nk_last)
                        wge(e, vch.h, t_red_prev)
                        e.tensor_copy(out=nkv[:], in_=psO0[0:1, 0:TILE]
                                      ).then_inc(vch.h, 1)
                    P.v(v_nk_evac)
                    state["t_o0_free"] = vch.inc(1)
                    t_ev = state["t_o0_free"]

                    def v_nk_red(e, t_ev=t_ev, first=(i == 0)):
                        e.wait_ge(vch.h, t_ev)
                        e.tensor_reduce(out=nk1[:], in_=nkv[:], axis=AX.X,
                                        op=ALU.max).then_inc(vch.h, 1)
                    P.v(v_nk_red)
                    t_r1 = vch.inc(1)

                    def v_nk_max(e, t_r1=t_r1):
                        e.wait_ge(vch.h, t_r1)
                        wge(e, vch.h, t_vinit)
                        e.tensor_tensor(out=mkrun[:], in0=mkrun[:], in1=nk1[:],
                                        op=ALU.max).then_inc(vch.h, 1)
                    P.v(v_nk_max)
                    state["t_nk_red"] = vch.inc(1)

                dd = kfm_d if pass_kind == "kv" else qfm_d
                dst = dd.ap().transpose([1, 0, 2])[:, :, i * TILE:(i + 1) * TILE]
                t_all_evac = t_prj_e

                def sy_spill(e, p=p, dst=dst, t_all_evac=t_all_evac,
                             spl_sem=spl_sem):
                    e.wait_ge(vch.h, t_all_evac)
                    e.dma_start(out=dst,
                                in_=prj[p].rearrange("p (c t) -> p c t", c=NCH)
                                ).then_inc(spl_sem.h, 16)
                P.g(sy_spill)
                state[f"splp_done_{gi}"] = spl_sem.inc(16)
            else:
                t_splv_prev = state.get(f"splv_done_{gi - 2}", 0)
                for j in range(NBLK):
                    for hh in range(2):
                        bk = (j * 2 + hh) % 2
                        t_bankfree = bank_free[bk]

                        def t_vmm(e, p=p, j=j, hh=hh, bk=bk, wbf=wbf,
                                  t_bankfree=t_bankfree, t_x=t_xfm_ready, t_w=t_w):
                            e.wait_ge(vch.h, max(t_x, t_w, t_bankfree))
                            for c in range(NCH):
                                ins = e.matmul(
                                    out=psAB[bk][:, 0:512],
                                    lhsT=xfm[p][:, c * TILE + j * 128:c * TILE + (j + 1) * 128],
                                    rhs=wbf[:, c * D + hh * 512:c * D + (hh + 1) * 512],
                                    start=(c == 0), stop=(c == NCH - 1))
                            ins.then_inc(tch.h, 1)
                        P.t(t_vmm)
                        t_mm = tch.inc(1)

                        def v_vevac(e, p=p, j=j, hh=hh, bk=bk, t_mm=t_mm,
                                    t_splv_prev=t_splv_prev):
                            e.wait_ge(tch.h, t_mm)
                            wge(e, vch.h, t_vinit)
                            if j == 0 and hh == 0:
                                wge(e, splv[p].h, t_splv_prev)
                            e.tensor_copy(
                                out=vsb[p][:, j * 1025 + hh * 512:j * 1025 + (hh + 1) * 512],
                                in_=psAB[bk][:, 0:512]).then_inc(vch.h, 1)
                        P.v(v_vevac)
                        bank_free[bk] = vch.inc(1)
                t_v_evac = bank_free[(NBLK * 2 - 1) % 2]
                dstv = vtm_d.ap()[i * TILE:(i + 1) * TILE, :].rearrange(
                    "(j p) c -> p j c", p=128)

                def sy_vspill(e, p=p, dstv=dstv, t_v_evac=t_v_evac):
                    e.wait_ge(vch.h, t_v_evac)
                    e.dma_start(out=dstv,
                                in_=vsb[p].rearrange("p (j c) -> p j c", j=NBLK)
                                ).then_inc(splv[p].h, 16)
                P.g(sy_vspill)
                state[f"splv_done_{gi}"] = splv[p].inc(16)

    gi = 0
    for i in range(nt_full):
        emit_tile(i, gi, x_full, "kv")
        gi += 1
    for i in range(nt_own):
        emit_tile(i, gi, x_own, "q")
        gi += 1
    t_a1_tensor_end = tch.n
    t_a1_scalar_end = sch.n

    t_normq_last = state["t_o0_free"]
    t_mk_last = state["t_nk_red"]

    def v_mk2(e):
        e.wait_ge(vch.h, max(t_normq_last, t_mk_last))
        e.tensor_scalar_max(out=mkrun[:], in0=mkrun[:], scalar1=1e-19)
        e.tensor_scalar_max(out=normq[0:1, 0:T_own], in0=normq[0:1, 0:T_own],
                            scalar1=1e-19).then_inc(vch.h, 1)
    P.v(v_mk2)
    t_mk2 = vch.inc(1)

    def s_mk3(e):
        e.wait_ge(vch.h, t_mk2)
        e.activation(out=mkrun[:], in_=mkrun[:], func=AF.Ln).then_inc(sch.h, 1)
    P.s(s_mk3)
    t_mk3 = sch.inc(1)

    def v_mk4(e):
        e.wait_ge(sch.h, t_mk3)
        e.tensor_scalar_mul(out=mksc[:], in0=mkrun[:], scalar1=0.5).then_inc(vch.h, 1)
    P.v(v_mk4)
    t_mk4 = vch.inc(1)

    def s_mk5(e):
        e.wait_ge(vch.h, t_mk4)
        e.activation(out=normq[0:1, 0:T_own], in_=normq[0:1, 0:T_own],
                     func=AF.Ln).then_inc(sch.h, 1)
    P.s(s_mk5)
    t_mk5 = sch.inc(1)

    def s_mk6(e):
        e.wait_ge(sch.h, t_mk5)
        e.activation(out=m_q[0:1, 0:T_own], in_=normq[0:1, 0:T_own], func=AF.Exp,
                     scale=0.5, bias=mksc[:]).then_inc(sch.h, 1)
    P.s(s_mk6)
    t_mq = sch.inc(1)

    abf.reset()
    af32.reset()
    qg_sb = abf.carve(NCH * QG)
    kg = [abf.carve(NCH * KSC) for _ in range(2)]
    vg = [abf.carve((KSC // 128) * 1025) for _ in range(2)]
    p_buf = abf.carve(NCHU_F * QG)
    o_acc = af32.carve(NSUB * 1024)
    den_sb = af32.carve(512)
    o_n = af32.carve(NSUB * D)
    xres = af32.carve(NSUB * D)

    spl_targets = [(splp[0].h, splp[0].n), (splp[1].h, splp[1].n),
                   (splv[0].h, splv[0].n), (splv[1].h, splv[1].n)]

    qg_free = 0
    pbuf_free = 0
    s1_stored = 0
    o_bank_free = {0: state.get("t_o0_free", 0), 1: 0}
    den_bank_free = 0

    for g in range(n_groups):
        t_bankfree = bank_free[0]

        def t_mbc(e, g=g, t_bankfree=t_bankfree):
            e.wait_ge(sch.h, t_mq)
            wge(e, t_consts_h, t_consts)
            wge(e, vch.h, t_bankfree)
            e.matmul(out=psA[:, 0:QG], lhsT=ones_row[:],
                     rhs=m_q[0:1, g * QG:(g + 1) * QG], start=True,
                     stop=True).then_inc(tch.h, 1)
        P.t(t_mbc)
        t_mbcmm = tch.inc(1)

        def v_mbc(e, t_mbcmm=t_mbcmm):
            e.wait_ge(tch.h, t_mbcmm)
            e.tensor_copy(out=m_bc[:], in_=psA[:, 0:QG]).then_inc(vch.h, 1)
        P.v(v_mbc)
        bank_free[0] = vch.inc(1)
        t_mbc_ready = bank_free[0]

        t_qgfree = qg_free
        srcq = qfm_d.ap().transpose([1, 0, 2])[:, :, g * QG:(g + 1) * QG]

        def sy_qg(e, srcq=srcq, t_qgfree=t_qgfree, g=g):
            if g == 0:
                for (h, n) in spl_targets:
                    wge(e, h, n)
                wge(e, tch.h, t_a1_tensor_end)
                wge(e, sch.h, t_mq)
            wge(e, tch.h, t_qgfree)
            e.dma_start(out=qg_sb.rearrange("p (c t) -> p c t", c=NCH),
                        in_=srcq).then_inc(ldqg.h, 16)
        P.sy(sy_qg)
        t_qg = ldqg.inc(16)

        for ki in range(n_ksc):
            par = ki % 2
            if ki >= 2:
                t_used = state.get(f"kvused_{g}_{ki - 2}", 0)
            else:
                t_used = state.get(f"kvused_{g - 1}_{n_ksc + ki - 2}", 0)
            srck = kfm_d.ap().transpose([1, 0, 2])[:, :, ki * KSC:(ki + 1) * KSC]
            srcv = vtm_d.ap()[ki * KSC:(ki + 1) * KSC, :].rearrange(
                "(j p) c -> p j c", p=128)

            def sy_kv(e, par=par, srck=srck, srcv=srcv, t_used=t_used, g=g, ki=ki):
                if g == 0 and ki <= 1:
                    for (h, n) in spl_targets:
                        wge(e, h, n)
                    wge(e, tch.h, t_a1_tensor_end)
                    wge(e, sch.h, t_mq)
                wge(e, tch.h, t_used)
                e.dma_start(out=kg[par].rearrange("p (c t) -> p c t", c=NCH),
                            in_=srck).then_inc(ldkv[par].h, 16)
                e.dma_start(out=vg[par].rearrange("p (j c) -> p j c", j=KSC // 128),
                            in_=srcv).then_inc(ldkv[par].h, 16)
            P.sy(sy_kv)
            state[f"tkv_{ki % 2}"] = ldkv[par].inc(32)
            t_kv = state[f"tkv_{ki % 2}"]

            for kc in range(n_kc):
                chunk = ki * n_kc + kc
                bk = kc % 2
                t_bankfree = bank_free[bk]
                t_exp_prev = state.get(f"exp_{g}_{chunk - 1}", 0)

                def t_sc(e, g=g, par=par, ki=ki, kc=kc, bk=bk,
                         t_bankfree=t_bankfree, t_kv=t_kv, t_qg=t_qg):
                    e.wait_ge(ldkv[par].h, t_kv)
                    e.wait_ge(ldqg.h, t_qg)
                    wge(e, vch.h, t_bankfree)
                    for c in range(NCH):
                        ins = e.matmul(
                            out=psAB[bk][:, 0:QG],
                            lhsT=kg[par][:, c * KSC + kc * 128:c * KSC + (kc + 1) * 128],
                            rhs=qg_sb[:, c * QG:(c + 1) * QG],
                            start=(c == 0), stop=(c == NCH - 1))
                    ins.then_inc(tch.h, 1)
                P.t(t_sc)
                t_scmm = tch.inc(1)
                if ki == n_ksc - 1 and kc == n_kc - 1:
                    qg_free = t_scmm

                def v_stt(e, bk=bk, t_scmm=t_scmm, t_mb=t_mbc_ready,
                          t_exp_prev=t_exp_prev):
                    e.wait_ge(tch.h, t_scmm)
                    wge(e, vch.h, t_mb)
                    wge(e, sch.h, t_exp_prev)
                    e.scalar_tensor_tensor(out=tsb[bk][:], in0=psAB[bk][:, 0:QG],
                                           scalar=1.0, in1=m_bc[:],
                                           op0=ALU.mult, op1=ALU.subtract
                                           ).then_inc(vch.h, 1)
                P.v(v_stt)
                bank_free[bk] = vch.inc(1)
                t_stt = bank_free[bk]

                def s_exp(e, chunk=chunk, bk=bk, t_stt=t_stt, t_pb=pbuf_free):
                    e.wait_ge(vch.h, t_stt)
                    wge(e, tch.h, t_pb)
                    e.activation(out=p_buf[:, chunk * QG:(chunk + 1) * QG],
                                 in_=tsb[bk][:], func=AF.Exp).then_inc(sch.h, 1)
                P.s(s_exp)
                state[f"exp_{g}_{chunk}"] = sch.inc(1)
            t_exp_ksc = state[f"exp_{g}_{ki * n_kc + n_kc - 1}"]
            state[f"texp_{ki % 2}"] = t_exp_ksc

            if ki % 2 == 0:
                continue
            for sh in range((NSUB + 1) // 2):
                for sub in (2 * sh, 2 * sh + 1):
                    if sub >= NSUB:
                        continue
                    om = sub % 2
                    t_ofree = o_bank_free[om]

                    def t_pv(e, g=g, ki=ki, sub=sub, om=om,
                             t_ofree=t_ofree,
                             t_exp0=state["texp_0"], t_exp1=state["texp_1"]):
                        e.wait_ge(sch.h, max(t_exp0, t_exp1))
                        wge(e, vch.h, t_ofree)
                        for kj in (ki - 1, ki):
                            grp_start = (kj == ki - 1)
                            grp_stop = (kj == ki)
                            vb = vg[kj % 2]
                            for kc in range(n_kc):
                                chunk = kj * n_kc + kc
                                lhsT = p_buf[:, chunk * QG + sub * 128:
                                             chunk * QG + (sub + 1) * 128]
                                st0 = grp_start and kc == 0
                                sp0 = grp_stop and kc == n_kc - 1
                                e.matmul(out=psO[om][:, 0:512], lhsT=lhsT,
                                         rhs=vb[:, kc * 1025:kc * 1025 + 512],
                                         start=st0, stop=sp0)
                                ins = e.matmul(out=psO[om][:, 512:1024], lhsT=lhsT,
                                               rhs=vb[:, kc * 1025 + 512:kc * 1025 + 1024],
                                               start=st0, stop=sp0)
                        ins.then_inc(tch.h, 1)
                    P.t(t_pv)
                    t_pvmm = tch.inc(1)
                    if sub in (NSUB - 1, NSUB - 2):
                        pass
                    if sh == (NSUB + 1) // 2 - 1 and sub == min(2 * sh + 1, NSUB - 1):
                        state[f"kvused_{g}_{ki - 1}"] = t_pvmm
                        state[f"kvused_{g}_{ki}"] = t_pvmm
                        if ki == n_ksc - 1:
                            pbuf_free = t_pvmm

                    def v_oacc(e, ki=ki, sub=sub, om=om, t_pvmm=t_pvmm,
                               t_s1st=s1_stored):
                        e.wait_ge(tch.h, t_pvmm)
                        if ki == 1:
                            wge(e, st.h, t_s1st)
                            e.tensor_copy(out=o_acc[:, sub * 1024:(sub + 1) * 1024],
                                          in_=psO[om][:, 0:1024]).then_inc(vch.h, 1)
                        else:
                            e.tensor_tensor(out=o_acc[:, sub * 1024:(sub + 1) * 1024],
                                            in0=o_acc[:, sub * 1024:(sub + 1) * 1024],
                                            in1=psO[om][:, 0:1024],
                                            op=ALU.add).then_inc(vch.h, 1)
                    P.v(v_oacc)
                    state[f"oevac_{g}_{ki}_{sub}"] = vch.inc(1)
                    o_bank_free[om] = state[f"oevac_{g}_{ki}_{sub}"]

            def t_den(e, g=g, ki=ki, t_dfree=den_bank_free):
                if ki == 1:
                    wge(e, vch.h, t_dfree)
                for kj in (ki - 1, ki):
                    for kc in range(n_kc):
                        chunk = kj * n_kc + kc
                        ins = e.matmul(
                            out=psDen[0:1, 0:QG], lhsT=ones_bf[:],
                            rhs=p_buf[:, chunk * QG:(chunk + 1) * QG],
                            start=(ki == 1 and kj == 0 and kc == 0),
                            stop=(ki == n_ksc - 1 and kj == ki and kc == n_kc - 1))
                ins.then_inc(tch.h, 1)
            P.t(t_den)
            t_denmm = tch.inc(1)
            if ki == n_ksc - 1:
                pbuf_free = t_denmm
        t_oacc_all = state[f"oevac_{g}_{n_ksc - 1}_{NSUB - 1}"]

        def v_den(e, t_denmm=t_denmm):
            e.wait_ge(tch.h, t_denmm)
            e.tensor_copy(out=den_sb[0:1, 0:QG],
                          in_=psDen[0:1, 0:QG]).then_inc(vch.h, 1)
        P.v(v_den)
        den_bank_free = vch.inc(1)
        t_vden = den_bank_free

        for sub in range(NSUB):
            bk = sub % 2
            t_bankfree = bank_free[bk]

            def t_dtp(e, sub=sub, bk=bk, t_vden=t_vden, t_bankfree=t_bankfree):
                e.wait_ge(vch.h, max(t_vden, t_bankfree))
                e.transpose(out=psAB[bk][:, 0:1],
                            in_=den_sb[0:1, sub * 128:(sub + 1) * 128],
                            identity=ident_f32[0:1, 0:1]).then_inc(tch.h, 1)
            P.t(t_dtp)
            t_tpd = tch.inc(1)

            def v_dtp(e, sub=sub, bk=bk, t_tpd=t_tpd):
                e.wait_ge(tch.h, t_tpd)
                e.tensor_copy(out=rden[:, sub:sub + 1],
                              in_=psAB[bk][:, 0:1]).then_inc(vch.h, 1)
            P.v(v_dtp)
            bank_free[bk] = vch.inc(1)
        t_rden = bank_free[(NSUB - 1) % 2]

        def v_ep1(e, t_oacc_all=t_oacc_all, t_rden=t_rden):
            e.wait_ge(vch.h, max(t_oacc_all, t_rden))
            e.reciprocal(out=rden[:, 0:NSUB],
                         in_=rden[:, 0:NSUB]).then_inc(vch.h, 1)
        P.v(v_ep1)
        t_ep1 = vch.inc(1)

        def v_ep2(e, t_ep1=t_ep1, t_s1st=s1_stored):
            e.wait_ge(vch.h, t_ep1)
            wge(e, st.h, t_s1st)
            for sub in range(NSUB):
                ins = e.tensor_scalar_mul(out=o_n[:, sub * D:(sub + 1) * D],
                                          in0=o_acc[:, sub * 1024:sub * 1024 + D],
                                          scalar1=rden[:, sub:sub + 1])
            ins.then_inc(vch.h, 1)
        P.v(v_ep2)
        t_ep2 = vch.inc(1)

        def v_pre_ep(e, t_ep2=t_ep2):
            e.wait_ge(vch.h, t_ep2)
        P.v(v_pre_ep)
        t_epap = emit_ln(P, lnb, vch, sch, nblk=NSUB,
                         src=o_n, srcol=lambda j: j * D,
                         dst=o_n, dstcol=lambda j: j * D)

        srcx = x_own[g * QG:(g + 1) * QG, :].rearrange("(j p) c -> p j c", p=128)

        def sy_xres(e, srcx=srcx, t_s1st=s1_stored):
            wge(e, st.h, t_s1st)
            e.dma_start(out=xres.rearrange("p (j c) -> p j c", j=NSUB),
                        in_=srcx).then_inc(ldxr.h, 16)
        P.sy(sy_xres)
        t_xres = ldxr.inc(16)

        def v_ep_res(e, t_epap=t_epap, t_xres=t_xres):
            e.wait_ge(sch.h, t_epap)
            e.wait_ge(ldxr.h, t_xres)
            e.tensor_tensor(out=o_n[:], in0=o_n[:], in1=xres[:],
                            op=ALU.add).then_inc(vch.h, 1)
        P.v(v_ep_res)
        t_eprs = vch.inc(1)

        dsts1 = s1[g * QG:(g + 1) * QG, :].rearrange("(j p) c -> p j c", p=128)

        def sy_s1(e, dsts1=dsts1, t_eprs=t_eprs):
            e.wait_ge(vch.h, t_eprs)
            e.dma_start(out=dsts1,
                        in_=o_n.rearrange("p (j c) -> p j c", j=NSUB)
                        ).then_inc(st.h, 16)
        P.sy(sy_s1)
        s1_stored = st.inc(16)

    def sy_fin(e, t=s1_stored):
        e.wait_ge(st.h, t)
    P.sy(sy_fin)

    if not standalone:
        finals = [(vch.h, vch.n), (sch.h, sch.n), (tch.h, tch.n),
                  (gch.h, gch.n), (st.h, st.n),
                  (splp[0].h, splp[0].n), (splp[1].h, splp[1].n),
                  (splv[0].h, splv[0].n), (splv[1].h, splv[1].n),
                  (ldx[0].h, ldx[0].n), (ldx[1].h, ldx[1].n),
                  (ldqg.h, ldqg.n), (ldkv[0].h, ldkv[0].n),
                  (ldkv[1].h, ldkv[1].n), (ldxr.h, ldxr.n), (ldw.h, ldw.n)]
        return finals

    block = es.enter_context(nc.Block())
    P.emit(block)
    es.close()
    return nc



def build_l0ffn(T_own, ST=256, G=None, io=None):
    H = 4096
    NHC = H // 128
    nst = T_own // ST
    NB = ST // 128
    W = ST + 2

    standalone = G is None
    if standalone:
        nc = bass.Bass("TRN2", target_bir_lowering=False)
        pre = ""
    else:
        nc = G["nc"]
        pre = "b_"
    io = io or {}

    def dram(name, shape, dt, kind):
        if name in io:
            return io[name]
        return nc.dram_tensor(name, shape, dt, kind=kind).ap()

    s1e = dram("s1e", [T_own + 2, D], F32, "ExternalInput")
    w1t = dram("w1t", [D, H], F32, "ExternalInput")
    cw = dram("cw", [H, 3], F32, "ExternalInput")
    cb = dram("cb", [H], F32, "ExternalInput")
    w2t = dram("w2t", [H, D], F32, "ExternalInput")
    s2 = dram("s2", [T_own, D], F32, "ExternalOutput")

    if standalone:
        es = contextlib.ExitStack()
        P = Prog(nc)
    else:
        es, P = G["es"], G["P"]
        P.barrier_step(G["entry"])

    vch = Ctr(es, nc, pre + "vch")
    sch = Ctr(es, nc, pre + "sch")
    tch = Ctr(es, nc, pre + "tch")
    gch = Ctr(es, nc, pre + "gch")
    ldw = [Ctr(es, nc, f"{pre}ldw{p}") for p in range(2)]
    ldx = [Ctr(es, nc, f"{pre}ldx{p}") for p in range(2)]
    ldxr = Ctr(es, nc, pre + "ldxr")
    ldxr1 = Ctr(es, nc, pre + "ldxr1")
    ldc = Ctr(es, nc, pre + "ldc")
    st_c = Ctr(es, nc, pre + "st")

    if standalone:
        consts = build_consts(nc, es, P, gch)
        t_consts_h, t_consts = gch.h, consts["ready"]
    else:
        consts = G["consts"]
        t_consts_h, t_consts = G["consts_sem"], G["consts_ready"]
    ident_f32 = consts["ident_f32"]
    ident_bf = consts["ident_bf"]
    ones_row = consts["ones_row"]

    if standalone:
        abf = Arena(nc, es, "abf", BF16, 47600 + NHC * ST + 2 * ST)
        af32 = Arena(nc, es, "af32", F32, 19744)
    else:
        abf, af32 = G["abf"], G["af32"]
        abf.reset()
        af32.reset()

    wq1_bf = abf.carve(NCH * H // 2).bitcast(FP8)
    wq2_bf = abf.carve(NHC * D // 2).bitcast(FP8)
    hbig = abf.carve(NHC * ST)
    xqfm = [abf.carve(NCH * W) for _ in range(2)]
    hsb = [abf.carve(W) for _ in range(2)]
    xwin_bf = abf.carve((NB + 1) * D)

    stg = [af32.carve(H) for _ in range(2)]
    af32.reset()
    xwin = [af32.carve((NB + 1) * D) for _ in range(2)]
    cacc = [af32.carve(ST) for _ in range(2)]
    tmp32 = af32.carve(ST)
    sc2b = af32.carve(ST)
    out2 = af32.carve(NB * D)
    xres = af32.carve(2 * D)

    lnb = (G["lnb"] if not standalone else mk_ln_bufs(nc, es, NB + 1))
    cwsb = af32.carve_high(NHC * 3)
    cbsb = af32.carve_high(NHC)
    _b1 = af32.carve_high(160)
    ones_col32 = _b1[:, 0:1]
    ones128 = af32.carve_high(128)
    colsum = _b1[:, 1:2]
    asum = _b1[:, 2:3]
    wsum = _b1[0:1, 3:4]
    meanc = [_b1[0:1, 4 + i:5 + i] for i in range(2)]
    meanb = [_b1[:, 6 + i:7 + i] for i in range(2)]
    wsb = _b1[:, 8:9]
    magb = _b1[:, 9:10]
    amaxb = _b1[:, 16:16 + NB + 1]
    scb = _b1[:, 24:24 + NB + 1]
    fb = _b1[:, 32:32 + NB + 1]
    tmax = _b1[:, 40:40 + NB]
    sc2_tm = _b1[:, 48:48 + NB]
    g_tm = _b1[:, 56:56 + NB]
    cbstg0 = af32.carve_high(128)
    maxacc = af32.carve_high(ST)
    rep = af32.carve_high(128)

    if standalone:
        psA = es.enter_context(nc.psum_tensor("psA", [128, 512], F32))[:]
        psB = es.enter_context(nc.psum_tensor("psB", [128, 512], F32))[:]
        psH = [es.enter_context(nc.psum_tensor(f"psH{p}", [128, W], F32))[:]
               for p in range(2)]
        psM = es.enter_context(nc.psum_tensor("psM", [128, 1024], F32))[:]
    else:
        psA, psB = G["psA"], G["psB"]
        if W <= 512:
            psH = [G["psO0"][:, 0:W], G["psO0"][:, 512:512 + W]]
            psM = G["psO1"][:, 0:1024]
        else:
            psH = [G["psO0"][:, 0:W], G["psO1"][:, 0:W]]
            psM = G["psO0"][:, 512:1536]
    psAB = [psA, psB]

    def v_init(e):
        e.memset(ones_col32[:], 1.0)
        e.memset(lnb["veps"][:], 1.0)
        e.memset(amaxb[:], 1.0)
        e.memset(scb[:], 1.0)
        e.memset(fb[:], 1.0)
        e.memset(_b1[:, 64:120], 1.0)
        e.memset(magb[:], MAGIC)
        e.memset(ones128[:], 1.0).then_inc(vch.h, 1)
    P.v(v_init)
    t_vinit = vch.inc(1)

    cbstg = cbstg0[0:32, :]

    def sy_cw(e):
        e.dma_start(out=cwsb[:].rearrange("p (n m) -> p n m", n=NHC),
                    in_=cw.rearrange("(n p) m -> p n m", p=128)).then_inc(ldc.h, 16)
        e.dma_start(out=cbstg[:], in_=cb.rearrange("(n p) -> n p", p=128)
                    ).then_inc(ldc.h, 16)
    P.sy(sy_cw)
    t_ldc0 = ldc.inc(32)

    def t_cb(e):
        e.wait_ge(ldc.h, t_ldc0)
        wge(e, t_consts_h, t_consts)
        e.transpose(out=psB[:, 0:32], in_=cbstg[:],
                    identity=ident_f32[0:32, 0:32]).then_inc(tch.h, 1)
    P.t(t_cb)
    t_cbt = tch.inc(1)

    def v_cb(e):
        e.wait_ge(tch.h, t_cbt)
        e.tensor_copy(out=cbsb[:], in_=psB[:, 0:32]).then_inc(vch.h, 1)
    P.v(v_cb)
    t_cb_ev = vch.inc(1)
    t_ldc = t_ldc0

    state = {}

    def emit_wquant(widx, src_rows_ap, n_chunks, ccols, wq_dst, dstcol,
                    t_prev_done=0):
        numel = n_chunks * 128 * ccols
        for ch in range(n_chunks):
            par = ch % 2
            t_used = state.get(f"wq{widx}_used_{ch - 2}", t_prev_done)
            if ch < 2:
                t_used = max(t_used, t_prev_done)
            src = src_rows_ap[ch * 128:(ch + 1) * 128, :]

            def sy_ld(e, par=par, src=src, ccols=ccols, t_used=t_used):
                wge(e, vch.h, t_used)
                e.dma_start(out=stg[par][:, 0:ccols], in_=src).then_inc(ldw[par].h, 16)
            P.sy(sy_ld)
            t_ld = ldw[par].inc(16)

            t_prev_acc = state.get(f"wq{widx}_used_{ch - 1}", 0)

            def v_red(e, par=par, ccols=ccols, t_ld=t_ld, ch=ch,
                      t_prev_acc=t_prev_acc):
                e.wait_ge(ldw[par].h, t_ld)
                wge(e, vch.h, t_prev_acc)
                e.tensor_reduce(out=asum[:], in_=stg[par][:, 0:ccols], axis=AX.X,
                                op=ALU.add, apply_absolute_value=True
                                ).then_inc(vch.h, 1)
            P.v(v_red)
            t_red = vch.inc(1)

            def v_accum(e, ch=ch, t_red=t_red):
                e.wait_ge(vch.h, t_red)
                if ch == 0:
                    e.tensor_copy(out=colsum[:], in_=asum[:]).then_inc(vch.h, 1)
                else:
                    e.tensor_tensor(out=colsum[:], in0=colsum[:], in1=asum[:],
                                    op=ALU.add).then_inc(vch.h, 1)
            P.v(v_accum)
            state[f"wq{widx}_used_{ch}"] = vch.inc(1)
        t_sum = state[f"wq{widx}_used_{n_chunks - 1}"]

        def t_psum(e, t_sum=t_sum):
            e.wait_ge(vch.h, max(t_sum, t_vinit))
            e.matmul(out=psA[0:1, 0:1], lhsT=colsum[:], rhs=ones_col32[:],
                     start=True, stop=True).then_inc(tch.h, 1)
        P.t(t_psum)
        t_ps = tch.inc(1)

        def v_mean(e, t_ps=t_ps, numel=numel, widx=widx):
            e.wait_ge(tch.h, t_ps)
            e.tensor_scalar(out=meanc[widx][:], in0=psA[0:1, 0:1],
                            scalar1=1.0 / numel, scalar2=1e-5,
                            op0=ALU.mult, op1=ALU.max).then_inc(vch.h, 1)
        P.v(v_mean)
        t_mean = vch.inc(1)

        def v_ws(e, t_mean=t_mean, widx=widx):
            e.wait_ge(vch.h, t_mean)
            e.reciprocal(out=wsum[:], in_=meanc[widx][:]).then_inc(vch.h, 1)
        P.v(v_ws)
        t_wsv = vch.inc(1)

        def t_bc(e, t_wsv=t_wsv, widx=widx):
            e.wait_ge(vch.h, t_wsv)
            wge(e, t_consts_h, t_consts)
            e.matmul(out=psA[:, 1:2], lhsT=ones_row[:], rhs=wsum[:],
                     start=True, stop=True)
            e.matmul(out=psA[:, 2:3], lhsT=ones_row[:], rhs=meanc[widx][:],
                     start=True, stop=True).then_inc(tch.h, 1)
        P.t(t_bc)
        t_bcm = tch.inc(1)

        def v_bc(e, t_bcm=t_bcm, widx=widx):
            e.wait_ge(tch.h, t_bcm)
            e.tensor_copy(out=wsb[:], in_=psA[:, 1:2])
            e.tensor_copy(out=meanb[widx][:], in_=psA[:, 2:3]).then_inc(vch.h, 1)
        P.v(v_bc)
        t_wsb = vch.inc(1)

        for ch in range(n_chunks):
            par = ch % 2
            t_used = state.get(f"wq{widx}b_used_{ch - 2}", 0)
            src = src_rows_ap[ch * 128:(ch + 1) * 128, :]

            def sy_ld2(e, par=par, src=src, ccols=ccols, t_used=t_used,
                       t_wsb=t_wsb, ch=ch):
                if ch <= 1:
                    wge(e, vch.h, t_wsb)
                wge(e, vch.h, t_used)
                e.dma_start(out=stg[par][:, 0:ccols], in_=src).then_inc(ldw[par].h, 16)
            P.sy(sy_ld2)
            t_ld = ldw[par].inc(16)

            def v_q1(e, par=par, ccols=ccols, t_ld=t_ld, t_wsb=t_wsb):
                e.wait_ge(ldw[par].h, t_ld)
                wge(e, vch.h, t_wsb)
                e.tensor_scalar(out=stg[par][:, 0:ccols], in0=stg[par][:, 0:ccols],
                                scalar1=wsb[:], scalar2=MAGIC,
                                op0=ALU.mult, op1=ALU.add).then_inc(vch.h, 1)
            P.v(v_q1)
            t_q1 = vch.inc(1)

            def v_q2(e, par=par, ccols=ccols, t_q1=t_q1):
                e.wait_ge(vch.h, t_q1)
                e.tensor_scalar(out=stg[par][:, 0:ccols], in0=stg[par][:, 0:ccols],
                                scalar1=MAGIC, scalar2=1.0,
                                op0=ALU.subtract, op1=ALU.min).then_inc(vch.h, 1)
            P.v(v_q2)
            t_q2 = vch.inc(1)

            def v_q3(e, par=par, ccols=ccols, t_q2=t_q2, ch=ch):
                e.wait_ge(vch.h, t_q2)
                e.tensor_scalar_max(out=wq_dst[:, dstcol(ch):dstcol(ch) + ccols],
                                    in0=stg[par][:, 0:ccols],
                                    scalar1=-1.0).then_inc(vch.h, 1)
            P.v(v_q3)
            state[f"wq{widx}b_used_{ch}"] = vch.inc(1)
        return state[f"wq{widx}b_used_{n_chunks - 1}"]

    t_wq1 = emit_wquant(0, w1t, NCH, H, wq1_bf, lambda ch: ch * H)
    t_wq2 = emit_wquant(1, w2t, NHC, D, wq2_bf, lambda ch: ch * D,
                        t_prev_done=t_wq1)

    bank_free = {0: 0, 1: t_cb_ev}
    psh_free = {0: 0, 1: 0}
    psm_free = 0
    apply_done = {}
    s2_stored = 0

    def emit_front_a(si):
        par = si % 2
        t0 = si * ST
        blocks = [(j * D, j * D, 128) for j in range(NB)] + [(NB * D, NB * D, 2)]
        amaxb = _b1[:, 64 + 8 * par:64 + 8 * par + NB + 1]
        scb = _b1[:, 80 + 8 * par:80 + 8 * par + NB + 1]
        fb = _b1[:, 96 + 8 * par:96 + 8 * par + NB + 1]
        fbM = _b1[:, 112 + 8 * par:112 + 8 * par + NB + 1]
        t_xwfree = state.get(f"fbq2_{si - 2}", ("v", 0))[1]

        def sy_win(e, t0=t0, par=par, t_xwfree=t_xwfree, si=si):
            if si < 2:
                wge(e, vch.h, t_wq2)
            wge(e, vch.h, t_xwfree)
            e.dma_start(out=xwin[par].rearrange("p (j c) -> p j c", j=NB + 1)[:, 0:NB, :],
                        in_=s1e[t0:t0 + NB * 128, :].rearrange("(j p) c -> p j c", p=128)
                        ).then_inc(ldx[par].h, 16)
            e.dma_start(out=xwin[par][0:2, NB * D:(NB + 1) * D],
                        in_=s1e[t0 + NB * 128:t0 + W, :]).then_inc(ldx[par].h, 16)
        P.sy(sy_win)
        t_win = ldx[par].inc(32)

        def v_predma(e, par=par, t_win=t_win):
            e.wait_ge(ldx[par].h, t_win)
        P.v(v_predma)
        t_apply = emit_ln_blocks(P, lnb, vch, sch, src=xwin[par], dst=xwin[par],
                                 blocks=blocks, war_s=None)
        apply_done[si] = t_apply

        def v_amax(e, par=par, blocks=blocks, amaxb=amaxb, t_apply=t_apply):
            e.wait_ge(sch.h, t_apply)
            for j, (sc, dc, r) in enumerate(blocks):
                ins = e.tensor_reduce(out=amaxb[0:r, j:j + 1],
                                      in_=xwin[par][0:r, sc:sc + D], axis=AX.X,
                                      op=ALU.max, apply_absolute_value=True)
            ins.then_inc(vch.h, 1)
        P.v(v_amax)
        t_am = vch.inc(1)

        def v_clip(e, amaxb=amaxb, t_am=t_am):
            e.wait_ge(vch.h, t_am)
            e.tensor_scalar_max(out=amaxb[:, 0:NB + 1], in0=amaxb[:, 0:NB + 1],
                                scalar1=1e-5).then_inc(vch.h, 1)
        P.v(v_clip)
        t_cl = vch.inc(1)

        def v_rsc(e, amaxb=amaxb, scb=scb, t_cl=t_cl):
            e.wait_ge(vch.h, t_cl)
            e.reciprocal(out=scb[:, 0:NB + 1], in_=amaxb[:, 0:NB + 1]
                         ).then_inc(vch.h, 1)
        P.v(v_rsc)
        t_rs = vch.inc(1)

        def v_sc(e, amaxb=amaxb, scb=scb, fb=fb, t_rs=t_rs, t_w=t_wq1):
            e.wait_ge(vch.h, max(t_rs, t_w))
            e.tensor_scalar_mul(out=scb[:, 0:NB + 1], in0=scb[:, 0:NB + 1],
                                scalar1=127.0)
            e.tensor_scalar_mul(out=fb[:, 0:NB + 1], in0=amaxb[:, 0:NB + 1],
                                scalar1=1.0 / 127.0).then_inc(vch.h, 1)
        P.v(v_sc)
        t_sc = vch.inc(1)

        def v_f(e, fb=fb, t_sc=t_sc):
            e.wait_ge(vch.h, t_sc)
            for j in range(NB + 1):
                ins = e.tensor_tensor(out=fb[:, j:j + 1], in0=fb[:, j:j + 1],
                                      in1=meanb[0][:], op=ALU.mult)
            ins.then_inc(vch.h, 1)
        P.v(v_f)
        t_f0 = vch.inc(1)

        t_f = t_f0

        def s_bq1(e, par=par, blocks=blocks, scb=scb, t_f=t_f):
            wge(e, vch.h, t_f)
            for j, (sc, dc, r) in enumerate(blocks):
                ins = e.activation(out=xwin[par][0:r, sc:sc + D],
                                   in_=xwin[par][0:r, sc:sc + D],
                                   func=AF.Identity, scale=scb[0:r, j:j + 1],
                                   bias=magb[0:r, 0:1])
            ins.then_inc(sch.h, 1)
        P.s(s_bq1)
        t_b1 = sch.inc(1)

        t_tp_war = state.get(f"btp_{si - 1}", 0)

        def v_bq2(e, par=par, blocks=blocks, fb=fb, t_b1=t_b1,
                  t_tp_war=t_tp_war):
            e.wait_ge(sch.h, t_b1)
            wge(e, tch.h, t_tp_war)
            for j, (sc, dc, r) in enumerate(blocks):
                ins = e.tensor_scalar(out=xwin_bf[0:r, sc:sc + D],
                                      in0=xwin[par][0:r, sc:sc + D],
                                      scalar1=MAGIC, scalar2=fb[0:r, j:j + 1],
                                      op0=ALU.subtract, op1=ALU.mult)
            ins.then_inc(vch.h, 1)
        P.v(v_bq2)
        state[f"fbq2_{si}"] = ("v", vch.inc(1))

    def emit_front_b(si):
        par = si % 2
        t_b3 = state[f"fbq2_{si}"][1]
        t_xq_war = state.get(f"bmm1_{si - 2}", 0)
        for j in range(NB + 1):
            r = 128 if j < NB else 2
            bk = 0
            t_bankfree = bank_free[bk]

            def t_tp(e, j=j, r=r, bk=bk, t_bankfree=t_bankfree, t_b3=t_b3):
                e.wait_ge(vch.h, max(t_b3, t_bankfree))
                wge(e, t_consts_h, t_consts)
                pbf = psAB[bk].bitcast(BF16)
                for c in range(NCH):
                    ins = e.transpose(
                        out=pbf[:, c * 128:c * 128 + r],
                        in_=xwin_bf[0:r, j * D + c * 128:j * D + (c + 1) * 128],
                        identity=ident_bf[0:r, 0:r])
                ins.then_inc(tch.h, 1)
            P.t(t_tp)
            t_tpg = tch.inc(1)

            def v_evtp(e, par=par, j=j, r=r, bk=bk, t_tpg=t_tpg,
                       t_xq_war=t_xq_war):
                e.wait_ge(tch.h, max(t_tpg, t_xq_war))
                dst = xqfm[par].rearrange("p (c t) -> p c t", c=NCH)
                e.tensor_copy(out=dst[:, 0:NCH, j * 128:j * 128 + r],
                              in_=psAB[bk].bitcast(BF16)[:, 0:1024]
                              .rearrange("p (c t) -> p c t", c=NCH)[:, :, 0:r]
                              ).then_inc(vch.h, 1)
            P.v(v_evtp)
            bank_free[bk] = vch.inc(1)
        state[f"btp_{si}"] = tch.n
        state[f"fxq_{si}"] = vch.n

    emit_front_a(0)
    emit_front_b(0)
    if nst > 1:
        emit_front_a(1)

    for si in range(nst):
        par = si % 2
        t0 = si * ST
        t_xq_ready = state[f"fxq_{si}"]
        t_tpm_prev = state.get(f"tpm_{si - 1}", 0)

        def v_mx0(e, t_tpm_prev=t_tpm_prev):
            wge(e, tch.h, t_tpm_prev)
            e.memset(maxacc[:, 0:ST], 0.0).then_inc(vch.h, 1)
        P.v(v_mx0)
        t_mx0 = vch.inc(1)

        t_psm_cross = psm_free
        for hc in range(NHC):
            php = hc % 2
            t_phfree = psh_free[php]

            same_bank = (W * 4 <= 2048)

            def t_mm1(e, hc=hc, php=php, par=par, t_phfree=t_phfree,
                      t_xq=t_xq_ready,
                      t_w=t_wq1, t_psm_cross=t_psm_cross, same_bank=same_bank):
                e.wait_ge(vch.h, max(t_xq, t_w))
                if hc < 2:
                    wge(e, sch.h, t_psm_cross)
                if isinstance(t_phfree, tuple):
                    wge(e, t_phfree[0], t_phfree[1])
                else:
                    wge(e, vch.h, t_phfree)
                for c in range(NCH):
                    e.matmul(out=psH[php][:, 0:ST],
                             lhsT=wq1_bf[:, c * H + hc * 128:c * H + (hc + 1) * 128],
                             rhs=xqfm[par][:, c * W:c * W + ST],
                             start=(c == 0), stop=(False if same_bank else c == NCH - 1))
                    ins = e.matmul(out=psH[php][:, ST:W],
                                   lhsT=wq1_bf[:, c * H + hc * 128:c * H + (hc + 1) * 128],
                                   rhs=xqfm[par][:, c * W + ST:c * W + W],
                                   start=(False if same_bank else c == 0),
                                   stop=(c == NCH - 1))
                ins.then_inc(tch.h, 1)
            P.t(t_mm1)
            t_m1 = tch.inc(1)
            if hc == NHC - 1:
                state[f"bmm1_{si}"] = t_m1

            t_conv_prev = state.get(f"conv3_{hc - 2}", 0)

            def s_ehs(e, hc=hc, php=php, t_m1=t_m1, t_conv_prev=t_conv_prev):
                e.wait_ge(tch.h, t_m1)
                wge(e, vch.h, t_conv_prev)
                e.activation(out=hsb[php], in_=psH[php][:, 0:W],
                             func=AF.Identity).then_inc(sch.h, 1)
            P.s(s_ehs)
            t_eh = sch.inc(1)
            psh_free[php] = (sch.h, t_eh)

            t_cacc_free = state.get(f"conv3_{hc - 2}", 0)

            def s_c1(e, hc=hc, php=php, t_eh=t_eh, t_ldc=t_ldc,
                     t_cacc_free=t_cacc_free):
                e.wait_ge(sch.h, t_eh)
                e.wait_ge(vch.h, t_cacc_free)
                e.wait_ge(ldc.h, t_ldc)
                e.activation(out=cacc[php][:, 0:ST], in_=hsb[php][:, 0:ST],
                             func=AF.Identity,
                             scale=cwsb[:, hc * 3:hc * 3 + 1]).then_inc(sch.h, 1)
            P.s(s_c1)
            t_c1 = sch.inc(1)

            def v_c2(e, hc=hc, php=php, t_c1=t_c1):
                e.wait_ge(sch.h, t_c1)
                e.scalar_tensor_tensor(out=cacc[php][:, 0:ST],
                                       in0=hsb[php][:, 1:1 + ST],
                                       scalar=cwsb[:, hc * 3 + 1:hc * 3 + 2],
                                       in1=cacc[php][:, 0:ST],
                                       op0=ALU.mult, op1=ALU.add).then_inc(vch.h, 1)
            P.v(v_c2)
            state[f"conv1_{hc}"] = vch.inc(1)
            t_c2 = state[f"conv1_{hc}"]

            t_tmp_free = state.get(f"silu_{hc - 1}", 0)

            def v_c3(e, hc=hc, php=php, t_c2=t_c2, t_tmp_free=t_tmp_free):
                e.wait_ge(vch.h, t_c2)
                wge(e, sch.h, t_tmp_free)
                e.scalar_tensor_tensor(out=tmp32[:, 0:ST],
                                       in0=hsb[php][:, 2:2 + ST],
                                       scalar=cwsb[:, hc * 3 + 2:hc * 3 + 3],
                                       in1=cacc[php][:, 0:ST],
                                       op0=ALU.mult, op1=ALU.add).then_inc(vch.h, 1)
            P.v(v_c3)
            state[f"conv3_{hc}"] = vch.inc(1)
            t_c3 = state[f"conv3_{hc}"]

            t_q2done = state.get(f"q2_{hc}", None)
            t_q2done = state.get(f"bq2_{si - 1}_last", 0)

            def s_silu(e, hc=hc, t_c3=t_c3, t_q2done=t_q2done):
                e.wait_ge(vch.h, max(t_c3, t_cb_ev))
                wge(e, tch.h, t_q2done)
                e.activation(out=hbig[:, hc * ST:(hc + 1) * ST], in_=tmp32[:, 0:ST],
                             func=AF.Silu, bias=cbsb[:, hc:hc + 1]).then_inc(sch.h, 1)
            P.s(s_silu)
            state[f"silu_{hc}"] = sch.inc(1)
            t_sl = state[f"silu_{hc}"]

            def v_mx(e, hc=hc, t_sl=t_sl, t_mx0=t_mx0):
                e.wait_ge(sch.h, t_sl)
                e.wait_ge(vch.h, t_mx0)
                e.tensor_tensor(out=maxacc[:, 0:ST], in0=maxacc[:, 0:ST],
                                in1=hbig[:, hc * ST:(hc + 1) * ST],
                                op=ALU.max).then_inc(vch.h, 1)
            P.v(v_mx)
            state[f"mx_{hc}"] = vch.inc(1)
        if si + 1 < nst:
            emit_front_b(si + 1)
        if si + 2 < nst:
            emit_front_a(si + 2)

        t_mx_pre = state[f"mx_{NHC - 1}"]

        def v_mxf(e, t_mx_pre=t_mx_pre):
            e.wait_ge(vch.h, t_mx_pre)
            e.tensor_scalar_max(out=maxacc[:, 0:ST], in0=maxacc[:, 0:ST],
                                scalar1=0.27847).then_inc(vch.h, 1)
        P.v(v_mxf)
        t_mx_all = vch.inc(1)

        for j in range(NB):
            bk = 1
            t_bankfree = bank_free[bk]

            def t_tpm(e, j=j, bk=bk, t_mx_all=t_mx_all, t_bankfree=t_bankfree):
                e.wait_ge(vch.h, max(t_mx_all, t_bankfree))
                e.transpose(out=psAB[bk][:, 0:128],
                            in_=maxacc[:, j * 128:(j + 1) * 128],
                            identity=ident_f32[:]).then_inc(tch.h, 1)
            P.t(t_tpm)
            t_tj = tch.inc(1)

            def v_redm(e, j=j, bk=bk, t_tj=t_tj):
                e.wait_ge(tch.h, t_tj)
                e.tensor_reduce(out=tmax[:, j:j + 1], in_=psAB[bk][:, 0:128],
                                axis=AX.X, op=ALU.max).then_inc(vch.h, 1)
            P.v(v_redm)
            bank_free[bk] = vch.inc(1)
        t_tmax = bank_free[1]
        state[f"tpm_{si}"] = tch.n

        def v_sc2(e, t_tmax=t_tmax, t_w2=t_wq2):
            e.wait_ge(vch.h, max(t_tmax, t_w2))
            e.tensor_scalar_max(out=tmax[:, 0:NB], in0=tmax[:, 0:NB],
                                scalar1=1e-5).then_inc(vch.h, 1)
        P.v(v_sc2)
        t_s2pre = vch.inc(1)

        def v_sc2r(e, t_s2pre=t_s2pre):
            e.wait_ge(vch.h, t_s2pre)
            e.reciprocal(out=sc2_tm[:, 0:NB], in_=tmax[:, 0:NB]).then_inc(vch.h, 1)
        P.v(v_sc2r)
        t_s2a = vch.inc(1)

        def v_sc2b(e, t_s2a=t_s2a):
            e.wait_ge(vch.h, t_s2a)
            e.tensor_scalar_mul(out=sc2_tm[:, 0:NB], in0=sc2_tm[:, 0:NB],
                                scalar1=127.0)
            e.tensor_scalar_mul(out=g_tm[:, 0:NB], in0=tmax[:, 0:NB],
                                scalar1=1.0 / 127.0).then_inc(vch.h, 1)
        P.v(v_sc2b)
        t_s2b = vch.inc(1)

        def v_g(e, t_s2b=t_s2b):
            e.wait_ge(vch.h, t_s2b)
            for j in range(NB):
                ins = e.tensor_tensor(out=g_tm[:, j:j + 1], in0=g_tm[:, j:j + 1],
                                      in1=meanb[1][:], op=ALU.mult)
            ins.then_inc(vch.h, 1)
        P.v(v_g)
        t_g = vch.inc(1)

        for j in range(NB):
            bk = 1
            t_bankfree = bank_free[bk]

            def v_rep(e, j=j, t_s2b=t_s2b, t_vinit=t_vinit):
                e.wait_ge(vch.h, max(t_s2b, t_vinit))
                e.tensor_scalar_mul(out=rep[:], in0=ones128[:],
                                    scalar1=sc2_tm[:, j:j + 1]).then_inc(vch.h, 1)
            P.v(v_rep)
            t_rep = vch.inc(1)

            def t_tpr(e, j=j, bk=bk, t_rep=t_rep, t_bankfree=t_bankfree):
                e.wait_ge(vch.h, max(t_rep, t_bankfree))
                e.transpose(out=psAB[bk][:, 0:128], in_=rep[:],
                            identity=ident_f32[:]).then_inc(tch.h, 1)
            P.t(t_tpr)
            t_tpj = tch.inc(1)

            def v_esc(e, j=j, bk=bk, t_tpj=t_tpj):
                e.wait_ge(tch.h, t_tpj)
                e.tensor_copy(out=sc2b[:, j * 128:(j + 1) * 128],
                              in_=psAB[bk][:, 0:128]).then_inc(vch.h, 1)
            P.v(v_esc)
            bank_free[bk] = vch.inc(1)
        t_sc2b_ready = bank_free[1]

        for hc in range(NHC):
            t_qh_prev = state.get(f"qh_{hc - 1}", 0)

            def v_qh1(e, hc=hc, t_sc2b_ready=t_sc2b_ready, t_qh_prev=t_qh_prev):
                e.wait_ge(vch.h, max(t_sc2b_ready, t_qh_prev))
                e.tensor_tensor(out=tmp32[:, 0:ST], in0=hbig[:, hc * ST:(hc + 1) * ST],
                                in1=sc2b[:, 0:ST], op=ALU.mult).then_inc(vch.h, 1)
            P.v(v_qh1)
            t_h1 = vch.inc(1)

            def v_qh2(e, hc=hc, t_h1=t_h1):
                e.wait_ge(vch.h, t_h1)
                e.tensor_scalar(out=hbig[:, hc * ST:(hc + 1) * ST],
                                in0=tmp32[:, 0:ST],
                                scalar1=MAGIC, scalar2=MAGIC,
                                op0=ALU.add, op1=ALU.subtract).then_inc(vch.h, 1)
            P.v(v_qh2)
            state[f"qh_{hc}"] = vch.inc(1)
        t_q_all = state[f"qh_{NHC - 1}"]

        for j in range(NB):
            def t_mm2(e, j=j, t_q_all=t_q_all, t_w2=t_wq2, t_psm=psm_free):
                e.wait_ge(vch.h, max(t_q_all, t_w2))
                wge(e, sch.h, t_psm)
                for hc in range(NHC):
                    e.matmul(out=psM[:, 0:512],
                             lhsT=hbig[:, hc * ST + j * 128:hc * ST + (j + 1) * 128],
                             rhs=wq2_bf[:, hc * D:hc * D + 512],
                             start=(hc == 0), stop=(hc == NHC - 1))
                    ins = e.matmul(out=psM[:, 512:1024],
                                   lhsT=hbig[:, hc * ST + j * 128:hc * ST + (j + 1) * 128],
                                   rhs=wq2_bf[:, hc * D + 512:hc * D + 1024],
                                   start=(hc == 0), stop=(hc == NHC - 1))
                ins.then_inc(tch.h, 1)
            P.t(t_mm2)
            t_m2 = tch.inc(1)
            if j == NB - 1:
                state[f"bq2_{si}_last"] = t_m2

            def s_em2(e, j=j, t_m2=t_m2, t_g=t_g, t_s2st=s2_stored):
                e.wait_ge(tch.h, t_m2)
                wge(e, vch.h, t_g)
                wge(e, st_c.h, t_s2st)
                e.activation(out=out2[:, j * D:(j + 1) * D], in_=psM[:, 0:1024],
                             func=AF.Identity,
                             scale=g_tm[:, j:j + 1]).then_inc(sch.h, 1)
            P.s(s_em2)
            psm_free = sch.inc(1)
        t_out2 = psm_free

        def v_pre(e, t_out2=t_out2):
            e.wait_ge(sch.h, t_out2)
        P.v(v_pre)
        t_ap2 = emit_ln(P, lnb, vch, sch, nblk=NB,
                        src=out2, srcol=lambda j: j * D,
                        dst=out2, dstcol=lambda j: j * D)

        for j in range(NB):
            px = j % 2
            lx = ldxr if px == 0 else ldxr1
            t_pfree = state.get(f"bres_{si}_{j - 2}", 0)

            def sy_xres(e, t0=t0, j=j, px=px, lx=lx, t_s2st=s2_stored,
                        t_pfree=t_pfree):
                wge(e, st_c.h, t_s2st)
                wge(e, vch.h, t_pfree)
                e.dma_start(out=xres[:, px * D:(px + 1) * D],
                            in_=s1e[t0 + 1 + j * 128:t0 + 1 + (j + 1) * 128, :]
                            ).then_inc(lx.h, 16)
            P.sy(sy_xres)
            t_xr = lx.inc(16)

            def v_res(e, j=j, px=px, lx=lx, t_ap2=t_ap2, t_xr=t_xr,
                      t_s2st=s2_stored):
                if j == 0:
                    e.wait_ge(sch.h, t_ap2)
                    wge(e, st_c.h, t_s2st)
                e.wait_ge(lx.h, t_xr)
                e.tensor_tensor(out=out2[:, j * D:(j + 1) * D],
                                in0=out2[:, j * D:(j + 1) * D],
                                in1=xres[:, px * D:(px + 1) * D],
                                op=ALU.add).then_inc(vch.h, 1)
            P.v(v_res)
            state[f"bres_{si}_{j}"] = vch.inc(1)
        t_resv = state[f"bres_{si}_{NB - 1}"]

        def sy_s2(e, t0=t0, t_resv=t_resv):
            e.wait_ge(vch.h, t_resv)
            e.dma_start(out=s2[t0:t0 + ST, :].rearrange("(j p) c -> p j c", p=128),
                        in_=out2.rearrange("p (j c) -> p j c", j=NB)
                        ).then_inc(st_c.h, 16)
        P.sy(sy_s2)
        s2_stored = st_c.inc(16)

    def sy_fin(e, t=s2_stored):
        e.wait_ge(st_c.h, t)
    P.sy(sy_fin)

    if not standalone:
        finals = [(vch.h, vch.n), (sch.h, sch.n), (tch.h, tch.n),
                  (gch.h, gch.n), (st_c.h, st_c.n), (ldc.h, ldc.n),
                  (ldw[0].h, ldw[0].n), (ldw[1].h, ldw[1].n),
                  (ldx[0].h, ldx[0].n), (ldx[1].h, ldx[1].n),
                  (ldxr.h, ldxr.n)]
        return finals

    block = es.enter_context(nc.Block())
    P.emit(block)
    es.close()
    return nc



def l1_proj_tiles(ctx, si, t0, *, do_q):
    (nc, P, es, state, vch, sch, tch, gch) = (ctx[k] for k in
        ("nc", "P", "es", "state", "vch", "sch", "tch", "gch"))
    psAB = ctx["psAB"]
    bank_free = ctx["bank_free"]
    ident_f32 = ctx["ident_f32"]
    ident_bf = ctx["ident_bf"]
    t_consts = ctx["t_consts"]
    t_consts_h = ctx["t_consts_h"]
    lnb = ctx["lnb"]
    ldx = ctx["ldx"]
    s2src = ctx["s2src"]
    xst = ctx["xst"]
    xfm = ctx["xfm"]
    TILE = 512
    NBLK = 4
    nbuf = ctx["nbuf"]

    p = si % nbuf
    xbf = ctx["xbf"]
    t_apfree = state.get(f"l1ap_{si - nbuf}", 0)
    t_tp_prev = state.get(f"l1tp_{si - 1}", 0)
    src = s2src[t0:t0 + TILE, :].rearrange("(j p) c -> p j c", p=128)

    def sy_load(e, src=src, p=p, t_apfree=t_apfree, t_tp_prev=t_tp_prev, si=si):
        if si < nbuf:
            wge(e, vch.h, ctx.get("t_wdone", 0))
        wge(e, sch.h, t_apfree)
        wge(e, tch.h, t_tp_prev)
        e.dma_start(out=xst[p].rearrange("p (j c) -> p j c", j=NBLK),
                    in_=src).then_inc(ldx[p].h, 16)
    P.sy(sy_load)
    t_ldx = ldx[p].inc(16)

    def v_predma(e, p=p, t_ldx=t_ldx):
        e.wait_ge(ldx[p].h, t_ldx)
    P.v(v_predma)
    t_apply = emit_ln(P, lnb, vch, sch, nblk=NBLK,
                      src=xst[p], srcol=lambda j: j * D,
                      dst=xbf, dstcol=lambda j: j * D,
                      war_s=(tch.h, t_tp_prev))
    state[f"l1ap_{si}"] = t_apply

    for j in range(NBLK):
        for h in range(2):
            bk = (j * 2 + h) % 2
            t_bankfree = bank_free[bk]

            def t_tp(e, p=p, j=j, h=h, bk=bk, t_bankfree=t_bankfree,
                     t_ap=t_apply):
                e.wait_ge(sch.h, t_ap)
                wge(e, t_consts_h, t_consts)
                wge(e, vch.h, t_bankfree)
                pbf = psAB[bk].bitcast(BF16)
                for cc in range(4):
                    c = 4 * h + cc
                    ins = e.transpose(
                        out=pbf[:, cc * 128:(cc + 1) * 128],
                        in_=xbf[:, j * D + c * 128:j * D + (c + 1) * 128],
                        identity=ident_bf[:])
                ins.then_inc(tch.h, 1)
            P.t(t_tp)
            t_tpg = tch.inc(1)

            def v_evac_tp(e, p=p, j=j, h=h, bk=bk, t_tpg=t_tpg):
                e.wait_ge(tch.h, t_tpg)
                dst = xfm[p].rearrange("p (c t) -> p c t", c=NCH)
                e.tensor_copy(out=dst[:, 4 * h:4 * h + 4, j * 128:(j + 1) * 128],
                              in_=psAB[bk].bitcast(BF16)[:, 0:512]
                              .rearrange("p (c t) -> p c t", c=4)
                              ).then_inc(vch.h, 1)
            P.v(v_evac_tp)
            bank_free[bk] = vch.inc(1)
    state[f"l1tp_{si}"] = tch.n
    t_xfm_ready = vch.n

    plist = [("k", True), ("v", False)] + ([("q", True)] if do_q else [])
    for (nm, phi) in plist:
        wbf, t_w = ctx[f"w_{nm}"]
        bcol = ctx["bias_fm"]
        boff = {"k": 0, "v": NCH, "q": 2 * NCH}[nm]
        dstb = ctx[f"p_{nm}"][p]
        kb = ctx["kb"]
        mn = ctx["mn"]
        emb = ctx["emb"]
        for e_ch in range(NCH):
            bk = e_ch % 2
            t_bankfree = bank_free[bk]

            def t_proj(e, p=p, e_ch=e_ch, bk=bk, wbf=wbf, t_w=t_w,
                       t_bankfree=t_bankfree, t_x=t_xfm_ready):
                e.wait_ge(vch.h, max(t_x, t_w, t_bankfree))
                for c in range(NCH):
                    ins = e.matmul(
                        out=psAB[bk][:, 0:TILE],
                        lhsT=wbf[:, c * D + e_ch * 128:c * D + (e_ch + 1) * 128],
                        rhs=xfm[p][:, c * TILE:(c + 1) * TILE],
                        start=(c == 0), stop=(c == NCH - 1))
                ins.then_inc(tch.h, 1)
            P.t(t_proj)
            t_mm = tch.inc(1)

            if not phi:
                def s_ev(e, e_ch=e_ch, bk=bk, dstb=dstb, boff=boff, t_mm=t_mm,
                         t_b=ctx["t_bias"]):
                    e.wait_ge(tch.h, t_mm)
                    wge(e, vch.h, t_b)
                    e.activation(out=dstb[:, e_ch * TILE:(e_ch + 1) * TILE],
                                 in_=psAB[bk][:, 0:TILE], func=AF.Identity,
                                 bias=bcol[:, boff + e_ch:boff + e_ch + 1]
                                 ).then_inc(sch.h, 1)
                P.s(s_ev)
                t_sev = sch.inc(1)

                def v_done(e, t_sev=t_sev):
                    e.wait_ge(sch.h, t_sev)
                    e.engine_nop().then_inc(vch.h, 1)
                P.v(v_done)
                bank_free[bk] = vch.inc(1)
            else:
                boff1 = {"k": 3 * NCH, "q": 4 * NCH}[nm]
                t_em_free = state.get("phi_last", 0)

                def s_em(e, e_ch=e_ch, bk=bk, boff=boff, t_mm=t_mm,
                         t_b=ctx["t_bias"], t_em_free=t_em_free):
                    e.wait_ge(tch.h, t_mm)
                    wge(e, vch.h, max(t_b, t_em_free))
                    e.activation(out=emb[:, 0:TILE], in_=psAB[bk][:, 0:TILE],
                                 func=AF.Exp,
                                 bias=bcol[:, boff + e_ch:boff + e_ch + 1]
                                 ).then_inc(sch.h, 1)
                P.s(s_em)
                t_em = sch.inc(1)

                def v_mn(e, t_em=t_em):
                    e.wait_ge(sch.h, t_em)
                    e.tensor_scalar_min(out=emb[:, 0:TILE], in0=emb[:, 0:TILE],
                                        scalar1=1.0).then_inc(vch.h, 1)
                P.v(v_mn)
                t_mn = vch.inc(1)

                def v_phi(e, e_ch=e_ch, bk=bk, dstb=dstb, boff1=boff1,
                          t_mn=t_mn):
                    e.wait_ge(vch.h, t_mn)
                    e.scalar_tensor_tensor(
                        out=dstb[:, e_ch * TILE:(e_ch + 1) * TILE],
                        in0=psAB[bk][:, 0:TILE],
                        scalar=bcol[:, boff1 + e_ch:boff1 + e_ch + 1],
                        in1=emb[:, 0:TILE],
                        op0=ALU.add, op1=ALU.max).then_inc(vch.h, 1)
                P.v(v_phi)
                state["phi_last"] = vch.inc(1)
                bank_free[bk] = state["phi_last"]
    t_proj_all = vch.n
    t_proj_all_s = sch.n

    kt_tm = ctx["kt_tm"][p]
    va_tm = ctx["va_tm"][p]
    npsT = ctx["npsT"]
    for cch in range(4):
        for pr in range(NCH):
            bk = (cch * NCH + pr) % npsT
            t_bankfree = state.get(f"ptp_ev_{bk}", 0)

            def t_tpp(e, p=p, cch=cch, pr=pr, bk=bk, t_bankfree=t_bankfree,
                      t_pa=t_proj_all, t_ps=t_proj_all_s):
                e.wait_ge(vch.h, max(t_pa, t_bankfree))
                e.wait_ge(sch.h, t_ps)
                pk = ctx["p_k"][p]
                pv = ctx["p_v"][p]
                psT = ctx["psT"]
                e.transpose(out=psT[bk][:, 0:128],
                            in_=pk[:, pr * 512 + cch * 128:pr * 512 + (cch + 1) * 128],
                            identity=ident_bf[:])
                e.transpose(out=psT[bk][:, 128:256],
                            in_=pv[:, pr * 512 + cch * 128:pr * 512 + (cch + 1) * 128],
                            identity=ident_bf[:]).then_inc(tch.h, 1)
            P.t(t_tpp)
            t_tpg = tch.inc(1)

            def v_evpp(e, p=p, cch=cch, pr=pr, bk=bk, t_tpg=t_tpg,
                       t_vo=ctx["t_vones"]):
                e.wait_ge(tch.h, t_tpg)
                wge(e, vch.h, t_vo)
                psT = ctx["psT"]
                e.tensor_copy(out=kt_tm[:, cch * 1024 + pr * 128:cch * 1024 + (pr + 1) * 128],
                              in_=psT[bk][:, 0:128])
                dst = va_tm[:, cch * 1040 + pr * 130:]
                e.tensor_copy(
                    out=bass.AP(va_tm.tensor, va_tm.offset + cch * 1040 + pr * 130,
                                [[va_tm.ap[0][0], 128], [65, 2], [1, 64]])
                    if False else
                    va_tm.rearrange("p (cc hh x) -> p cc hh x", cc=4, hh=16)[:, cch, 2 * pr:2 * pr + 2, 0:64],
                    in_=psT[bk][:, 128:256].rearrange("p (hh x) -> p hh x", hh=2)
                    ).then_inc(vch.h, 1)
            P.v(v_evpp)
            state[f"ptp_ev_{bk}"] = vch.inc(1)
    state[f"l1ptp_{si}"] = tch.n
    return vch.n


def _l1_common(nc, es, P, T_own, with_q, with_wo, nbuf=2, abf_sz=59000,
               af_sz=13200, G=None, pre=""):
    ctx = {}
    vch = Ctr(es, nc, pre + "vch"); sch = Ctr(es, nc, pre + "sch")
    tch = Ctr(es, nc, pre + "tch"); gch = Ctr(es, nc, pre + "gch")
    ldw = Ctr(es, nc, pre + "ldwq")
    ldx = [Ctr(es, nc, f"{pre}ldx{p}") for p in range(2)]
    ldb = Ctr(es, nc, pre + "ldb")
    st_c = Ctr(es, nc, pre + "st")
    ctx.update(nc=nc, es=es, P=P, vch=vch, sch=sch, tch=tch, gch=gch,
               ldx=ldx, ldb=ldb, st=st_c, state={}, nbuf=nbuf)

    if G is None:
        consts = build_consts(nc, es, P, gch)
        ctx["t_consts_h"], ctx["t_consts"] = gch.h, consts["ready"]
    else:
        consts = G["consts"]
        ctx["t_consts_h"], ctx["t_consts"] = G["consts_sem"], G["consts_ready"]
    ctx["ident_f32"] = consts["ident_f32"]
    ctx["ident_bf"] = consts["ident_bf"]
    ctx["ones_row"] = consts["ones_row"]

    if G is None:
        abf = Arena(nc, es, "abf", BF16, abf_sz)
        af32 = Arena(nc, es, "af32", F32, af_sz)
    else:
        abf, af32 = G["abf"], G["af32"]
        abf.reset()
        af32.reset()
    ctx["abf"] = abf; ctx["af32"] = af32

    names = ["k", "v"] + (["q"] if with_q else []) + (["o"] if with_wo else [])
    wtens = {}
    for nm in names:
        wtens[nm] = abf.carve(NCH * D)
    wstage = af32.carve(NCH * D // 2)
    af32.reset()
    xst = [af32.carve(4 * D) for _ in range(nbuf)]
    ctx["xst"] = xst
    ctx["xbf"] = af32.carve(2 * D).bitcast(BF16)
    ctx["xfm"] = [abf.carve(NCH * 512) for _ in range(nbuf)]
    ctx["p_k"] = [abf.carve(NCH * 512) for _ in range(nbuf)]
    ctx["p_v"] = [abf.carve(NCH * 512) for _ in range(nbuf)]
    if with_q:
        ctx["p_q"] = [abf.carve(NCH * 512) for _ in range(nbuf)]
    ctx["kt_tm"] = [abf.carve(4 * 1024) for _ in range(nbuf)]
    ctx["va_tm"] = [abf.carve(4 * 1040) for _ in range(nbuf)]
    ctx["kb"] = af32.carve(512)
    ctx["mn"] = af32.carve(512)
    ctx["emb"] = af32.carve(512)

    ctx["lnb"] = (G["lnb"] if G is not None else mk_ln_bufs(nc, es, 4))
    ctx["bias_fm"] = af32.carve_high(5 * NCH)
    ctx["bstg"] = ctx["mn"][0:NCH, 0:128]
    ctx["S_sb"] = af32.carve_high(NCH * 65)
    ctx["S_bf"] = abf.carve_high(NCH * 65)

    if G is None:
        psA = es.enter_context(nc.psum_tensor("psA", [128, 512], F32))[:]
        psB = es.enter_context(nc.psum_tensor("psB", [128, 512], F32))[:]
        psT_single = es.enter_context(nc.psum_tensor("psT0", [128, 256], BF16))[:]
        psS_single = es.enter_context(nc.psum_tensor("psS0", [128, 65], F32))[:]
        ctx["psT"] = [psT_single, psT_single]
        ctx["psS"] = [psS_single, psS_single]
        ctx["npsT"] = 1
        ctx["npsS"] = 1
    else:
        psA, psB = G["psA"], G["psB"]
        if with_wo:
            ctx["psT"] = [G["psO1"].bitcast(BF16)[:, 2048:2304]] * 2
            ctx["npsT"] = 1
        else:
            ctx["psT"] = [G["psO1"].bitcast(BF16)[:, 0:256],
                          G["psO1"].bitcast(BF16)[:, 2048:2304]]
            ctx["npsT"] = 2
        ctx["psS"] = [G["psO0"][:, 0:65], G["psO0"][:, 512:577]]
        ctx["npsS"] = 2
    ctx["psAB"] = [psA, psB]
    ctx["bank_free"] = {0: 0, 1: 0}

    wsrc = ctx["wsrc"] = {}
    return ctx, wtens, wstage, ldw


def build_l1state(T_own, G=None, io=None):
    standalone = G is None
    if standalone:
        nc = bass.Bass("TRN2", target_bir_lowering=False)
        pre = ""
    else:
        nc = G["nc"]
        pre = "c_"
    io = io or {}

    def dram(name, shape, dt, kind):
        if name in io:
            return io[name]
        return nc.dram_tensor(name, shape, dt, kind=kind).ap()

    s2 = dram("s2", [T_own, D], F32, "ExternalInput")
    wkT = dram("wkT", [D, D], F32, "ExternalInput")
    wvT = dram("wvT", [D, D], F32, "ExternalInput")
    bk = dram("bk", [D], F32, "ExternalInput")
    bv = dram("bv", [D], F32, "ExternalInput")
    stout = dram("state_loc", [D, 65], F32, "ExternalOutput")

    if standalone:
        es = contextlib.ExitStack()
        P = Prog(nc)
    else:
        es, P = G["es"], G["P"]
        P.barrier_step(G["entry"])
    ctx, wtens, wstage, ldw = _l1_common(nc, es, P, T_own, with_q=False,
                                         with_wo=False, G=G, pre=pre)
    vch, sch, tch, gch = ctx["vch"], ctx["sch"], ctx["tch"], ctx["gch"]
    state = ctx["state"]
    st_c = ctx["st"]
    ldb = ctx["ldb"]
    ctx["s2src"] = s2

    tw = load_weights_bf(P, wstage, [(wtens["k"], wkT, D), (wtens["v"], wvT, D)],
                         ldw, vch)
    ctx["w_k"] = (wtens["k"], tw[0])
    ctx["w_v"] = (wtens["v"], tw[1])
    ctx["t_wdone"] = tw[-1]

    def sy_b(e):
        e.dma_start(out=ctx["bstg"][:], in_=bk.rearrange("(n p) -> n p", p=128)
                    ).then_inc(ldb.h, 16)
    P.sy(sy_b)
    t_b1 = ldb.inc(16)

    def t_bt(e):
        e.wait_ge(ldb.h, t_b1)
        wge(e, ctx["t_consts_h"], ctx["t_consts"])
        e.transpose(out=ctx["psAB"][0][:, 0:NCH], in_=ctx["bstg"][:],
                    identity=ctx["ident_f32"][0:NCH, 0:NCH]).then_inc(tch.h, 1)
    P.t(t_bt)
    t_b2 = tch.inc(1)

    def v_bt(e):
        e.wait_ge(tch.h, t_b2)
        e.tensor_copy(out=ctx["bias_fm"][:, 0:NCH],
                      in_=ctx["psAB"][0][:, 0:NCH]).then_inc(vch.h, 1)
    P.v(v_bt)
    t_b3 = vch.inc(1)

    def sy_b2(e):
        e.wait_ge(vch.h, t_b3)
        e.dma_start(out=ctx["bstg"][:], in_=bv.rearrange("(n p) -> n p", p=128)
                    ).then_inc(ldb.h, 16)
    P.sy(sy_b2)
    t_b4 = ldb.inc(16)

    def t_bt2(e):
        e.wait_ge(ldb.h, t_b4)
        e.transpose(out=ctx["psAB"][0][:, 0:NCH], in_=ctx["bstg"][:],
                    identity=ctx["ident_f32"][0:NCH, 0:NCH]).then_inc(tch.h, 1)
    P.t(t_bt2)
    t_b5 = tch.inc(1)

    def v_bt2(e):
        e.wait_ge(tch.h, t_b5)
        e.tensor_copy(out=ctx["bias_fm"][:, NCH:2 * NCH],
                      in_=ctx["psAB"][0][:, 0:NCH]).then_inc(vch.h, 1)
    P.v(v_bt2)
    t_bv2 = vch.inc(1)

    def v_b1c(e, t_bv2=t_bv2):
        e.wait_ge(vch.h, t_bv2)
        e.tensor_scalar_add(out=ctx["bias_fm"][:, 3 * NCH:4 * NCH],
                            in0=ctx["bias_fm"][:, 0:NCH],
                            scalar1=1.0).then_inc(vch.h, 1)
    P.v(v_b1c)
    ctx["t_bias"] = vch.inc(1)
    ctx["bank_free"][0] = ctx["t_bias"]

    def v_init(e):
        for p in range(2):
            va = ctx["va_tm"][p]
            e.memset(va.rearrange("p (cc hh x) -> p cc hh x", cc=4, hh=16)[:, :, :, 64:65], 1.0)
        e.memset(ctx["S_sb"][:], 0.0).then_inc(vch.h, 1)
    P.v(v_init)
    ctx["t_vones"] = vch.inc(1)

    nt = T_own // 512
    psS = ctx["psS"]
    for si in range(nt):
        t_ready = l1_proj_tiles(ctx, si, si * 512, do_q=False)
        p = si % 2
        kt_tm = ctx["kt_tm"][p]
        va_tm = ctx["va_tm"][p]
        for pr in range(NCH):
            pp = pr % ctx["npsS"]
            t_psfree = state.get(f"sevac_any_{(si * NCH + pr) - ctx['npsS']}", 0)

            def t_sc(e, pr=pr, pp=pp, kt_tm=kt_tm, va_tm=va_tm,
                     t_ready=t_ready, t_psfree=t_psfree):
                e.wait_ge(vch.h, max(t_ready, t_psfree))
                for hh in range(2):
                    for cch in range(4):
                        lhsT = kt_tm[:, cch * 1024 + pr * 128 + hh * 64:
                                     cch * 1024 + pr * 128 + (hh + 1) * 64]
                        rhs = va_tm[:, cch * 1040 + pr * 130 + hh * 65:
                                    cch * 1040 + pr * 130 + (hh + 1) * 65]
                        ins = e.matmul(out=psS[pp][hh * 64:(hh + 1) * 64, 0:65],
                                       lhsT=lhsT, rhs=rhs,
                                       start=(cch == 0), stop=(cch == 3))
                ins.then_inc(tch.h, 1)
            P.t(t_sc)
            t_scm = tch.inc(1)

            def v_sacc(e, si=si, pr=pr, pp=pp, t_scm=t_scm):
                e.wait_ge(tch.h, t_scm)
                if si == 0:
                    e.tensor_copy(out=ctx["S_sb"][:, pr * 65:(pr + 1) * 65],
                                  in_=psS[pp][:, 0:65]).then_inc(vch.h, 1)
                else:
                    e.tensor_tensor(out=ctx["S_sb"][:, pr * 65:(pr + 1) * 65],
                                    in0=ctx["S_sb"][:, pr * 65:(pr + 1) * 65],
                                    in1=psS[pp][:, 0:65],
                                    op=ALU.add).then_inc(vch.h, 1)
            P.v(v_sacc)
            state[f"sevac_any_{si * NCH + pr}"] = vch.inc(1)
    t_all = state[f"sevac_any_{nt * NCH - 1}"]

    t_out_target = st_c.inc(16)

    def sy_out(e, t_all=t_all, t_out_target=t_out_target):
        e.wait_ge(vch.h, t_all)
        e.dma_start(out=stout.rearrange("(n p) m -> p n m", p=128),
                    in_=ctx["S_sb"][:].rearrange("p (n m) -> p n m", n=NCH)
                    ).then_inc(st_c.h, 16)
        e.wait_ge(st_c.h, t_out_target)
    P.sy(sy_out)

    if not standalone:
        finals = [(vch.h, vch.n), (sch.h, sch.n), (tch.h, tch.n),
                  (gch.h, gch.n), (st_c.h, st_c.n), (ldb.h, ldb.n),
                  (ldw.h, ldw.n),
                  (ctx["ldx"][0].h, ctx["ldx"][0].n),
                  (ctx["ldx"][1].h, ctx["ldx"][1].n)]
        return finals

    block = es.enter_context(nc.Block())
    P.emit(block)
    es.close()
    return nc


def build_l1attn(T_own, G=None, io=None):
    EPS = 1e-6
    standalone = G is None
    if standalone:
        nc = bass.Bass("TRN2", target_bir_lowering=False)
        pre = ""
    else:
        nc = G["nc"]
        pre = "d_"
    io = io or {}

    def dram(name, shape, dt, kind):
        if name in io:
            return io[name]
        return nc.dram_tensor(name, shape, dt, kind=kind).ap()

    s2 = dram("s2", [T_own, D], F32, "ExternalInput")
    wqT = dram("wqT", [D, D], F32, "ExternalInput")
    wkT = dram("wkT", [D, D], F32, "ExternalInput")
    wvT = dram("wvT", [D, D], F32, "ExternalInput")
    woT = dram("woT", [D, D], F32, "ExternalInput")
    bq = dram("bq", [D], F32, "ExternalInput")
    bk = dram("bk", [D], F32, "ExternalInput")
    bv = dram("bv", [D], F32, "ExternalInput")
    bo = dram("bo", [D], F32, "ExternalInput")
    if "state_full" not in io:
        stv = dram("state_v", [64, 1024], F32, "ExternalInput")
        stk = dram("state_k", [64, 16], F32, "ExternalInput")
    else:
        stv = stk = None
    s3 = dram("s3", [T_own, D], F32, "ExternalOutput")

    if standalone:
        es = contextlib.ExitStack()
        P = Prog(nc)
    else:
        es, P = G["es"], G["P"]
        P.barrier_step(G["entry"])
    ctx, wtens, wstage, ldw = _l1_common(nc, es, P, T_own, with_q=True,
                                         with_wo=True, nbuf=1, abf_sz=66000,
                                         af_sz=19744, G=G, pre=pre)
    vch, sch, tch, gch = ctx["vch"], ctx["sch"], ctx["tch"], ctx["gch"]
    state = ctx["state"]
    st_c = ctx["st"]
    ldb = ctx["ldb"]
    ldxr = Ctr(es, nc, pre + "ldxr")
    ldxr1 = Ctr(es, nc, pre + "ldxr1")
    ldsi = Ctr(es, nc, pre + "ldsi")
    ldsh = Ctr(es, nc, pre + "ldsh")
    ctx["s2src"] = s2
    abf, af32 = ctx["abf"], ctx["af32"]
    psAB = ctx["psAB"]
    psT = ctx["psT"]
    bank_free = ctx["bank_free"]
    lnb = ctx["lnb"]

    tw = load_weights_bf(P, wstage, [(wtens["k"], wkT, D), (wtens["v"], wvT, D),
                                     (wtens["q"], wqT, D), (wtens["o"], woT, D)],
                         ldw, vch)
    ctx["w_k"] = (wtens["k"], tw[0])
    ctx["w_v"] = (wtens["v"], tw[1])
    ctx["w_q"] = (wtens["q"], tw[2])
    wo_bf, t_wo = wtens["o"], tw[3]
    ctx["t_wdone"] = tw[-1]

    attn_buf = ctx["xfm"][0]
    afm = abf.carve(NCH * 512)
    q_od = afm
    pt_sb = [abf.carve(512) for _ in range(2)]
    mask4 = abf.carve(512)
    out2 = af32.carve(4 * D)
    xres = af32.carve(2 * D)
    och = af32.carve(272)
    bo_bc = abf.carve(D)
    k_od = af32.carve(2048).bitcast(BF16)
    rden4 = och[:, 264:268]
    borow = xres[0:1, 0:D]
    S_v = af32.carve_high(1024)[0:64, :]
    S_k = af32.carve_high(16)[0:64, 0:16]
    S_vb = abf.carve_high(1024)[0:64, :]
    S_kb = abf.carve_high(32)[0:64, 0:16]
    if standalone:
        psO4 = [es.enter_context(nc.psum_tensor(f"psO4_{i}", [128, 260], F32))[:]
                for i in range(2)]
        _psSv = es.enter_context(nc.psum_tensor("psSv", [64, 512], F32))[:]
        psSv2 = [_psSv, _psSv]
        psSk = es.enter_context(nc.psum_tensor("psSk", [64, 16], F32))[:]
    else:
        psO4 = [G["psO0"][:, 0:260], G["psO0"][:, 512:772]]
        psSv2 = [G["psO1"][0:64, 0:512], G["psO1"][0:64, 512:1024]]
        psSk = G["psO0"][0:64, 1024:1040]
    ones_col_bf = abf.carve_high(32)[:, 0:1]

    bsrc = {"k": bk, "v": bv, "q": bq}
    boffm = {"k": 0, "v": NCH, "q": 2 * NCH}
    t_prev = 0
    for nm in ("k", "v", "q"):
        t_l = ldb.inc(16)

        def sy_b(e, nm=nm, t_prev=t_prev):
            wge(e, vch.h, t_prev)
            e.dma_start(out=ctx["bstg"][:],
                        in_=bsrc[nm].rearrange("(n p) -> n p", p=128)
                        ).then_inc(ldb.h, 16)
        P.sy(sy_b)

        def t_bt(e, t_l=t_l):
            e.wait_ge(ldb.h, t_l)
            wge(e, ctx["t_consts_h"], ctx["t_consts"])
            e.transpose(out=psAB[0][:, 0:NCH], in_=ctx["bstg"][:],
                        identity=ctx["ident_f32"][0:NCH, 0:NCH]).then_inc(tch.h, 1)
        P.t(t_bt)
        t_t = tch.inc(1)

        def v_bt(e, nm=nm, t_t=t_t):
            e.wait_ge(tch.h, t_t)
            e.tensor_copy(out=ctx["bias_fm"][:, boffm[nm]:boffm[nm] + NCH],
                          in_=psAB[0][:, 0:NCH]).then_inc(vch.h, 1)
        P.v(v_bt)
        t_prev = vch.inc(1)

    def v_b1d(e, t_prev=t_prev):
        e.wait_ge(vch.h, t_prev)
        e.tensor_scalar_add(out=ctx["bias_fm"][:, 3 * NCH:4 * NCH],
                            in0=ctx["bias_fm"][:, 0:NCH], scalar1=1.0)
        e.tensor_scalar_add(out=ctx["bias_fm"][:, 4 * NCH:5 * NCH],
                            in0=ctx["bias_fm"][:, 2 * NCH:3 * NCH],
                            scalar1=1.0).then_inc(vch.h, 1)
    P.v(v_b1d)
    ctx["t_bias"] = vch.inc(1)
    bank_free[0] = ctx["t_bias"]

    t_bo = ldb.inc(16)

    def sy_bo(e, t_bo_prev=t_prev):
        wge(e, vch.h, t_bo_prev)
        e.dma_start(out=borow[:], in_=bo.rearrange("(a d) -> a d", a=1)
                    ).then_inc(ldb.h, 16)
    P.sy(sy_bo)

    def t_bobc(e, t_bo=t_bo, t_bf=bank_free[0]):
        e.wait_ge(ldb.h, t_bo)
        wge(e, ctx["t_consts_h"], ctx["t_consts"])
        wge(e, vch.h, t_bf)
        e.matmul(out=psAB[0][:, 0:512], lhsT=ctx["ones_row"][:],
                 rhs=borow[0:1, 0:512], start=True, stop=True)
        e.matmul(out=psAB[1][:, 0:512], lhsT=ctx["ones_row"][:],
                 rhs=borow[0:1, 512:1024], start=True, stop=True).then_inc(tch.h, 1)
    P.t(t_bobc)
    t_bob = tch.inc(1)

    def v_bobc(e, t_bob=t_bob):
        e.wait_ge(tch.h, t_bob)
        e.tensor_copy(out=bo_bc[:, 0:512], in_=psAB[0][:, 0:512])
        e.tensor_copy(out=bo_bc[:, 512:1024], in_=psAB[1][:, 0:512]).then_inc(vch.h, 1)
    P.v(v_bobc)
    t_bobc_v = vch.inc(1)
    bank_free[0] = t_bobc_v
    bank_free[1] = t_bobc_v

    state_full = io.get("state_full")
    if state_full is not None:
        s_tmp = af32.carve_high(1056)[0:64, 0:1040]

        def sy_si(e):
            e.dma_start(out=s_tmp.rearrange("p (h m) -> p h m", h=16),
                        in_=state_full.rearrange("(h p) m -> p h m", p=64)
                        ).then_inc(ldsi.h, 16)
        P.sy(sy_si)
        t_si0 = ldsi.inc(16)

        def v_split(e):
            e.wait_ge(ldsi.h, t_si0)
            e.tensor_copy(out=S_v[:].rearrange("p (h m) -> p h m", h=16),
                          in_=s_tmp.rearrange("p (h m) -> p h m", h=16)[:, :, 0:64])
            e.tensor_copy(out=S_k[:],
                          in_=s_tmp[:, 64:1040:65]).then_inc(ldsi.h, 16)
        P.v(v_split)
        t_si = ldsi.inc(16)
    else:
        stv3 = stv.rearrange("p (h m) -> p h m", h=16)
        stk3 = stk.rearrange("p (h m) -> p h m", m=1)

        def sy_si(e):
            e.dma_start(out=S_v[:].rearrange("p (h m) -> p h m", h=16),
                        in_=stv3).then_inc(ldsi.h, 16)
            e.dma_start(out=S_k[:].rearrange("p (h m) -> p h m", m=1),
                        in_=stk3).then_inc(ldsi.h, 16)
        P.sy(sy_si)
        t_si = ldsi.inc(32)

    def v_init(e):
        e.wait_ge(ldsi.h, t_si)
        for p in range(ctx["nbuf"]):
            va = ctx["va_tm"][p]
            e.memset(va.rearrange("p (cc hh x) -> p cc hh x", cc=4, hh=16)[:, :, :, 64:65], 1.0)
        e.memset(ones_col_bf[:], 1.0)
        e.tensor_copy(out=S_vb[:], in_=S_v[:])
        e.tensor_copy(out=S_kb[:], in_=S_k[:])
        e.memset(mask4, 1.0).then_inc(vch.h, 1)
    P.v(v_init)
    ctx["t_vones"] = vch.inc(1)
    t_vinit = ctx["t_vones"]

    def g_mask(e, t_vinit=t_vinit):
        e.wait_ge(vch.h, t_vinit)
        e.affine_select(out=mask4, in_=mask4, pattern=[[0, 4], [1, 128]],
                        compare_op=ALU.is_ge, fill=0.0,
                        base=0, channel_multiplier=-1).then_inc(gch.h, 1)
    P.g(g_mask)
    t_mask = gch.inc(1)

    nt = T_own // 512
    s3_stored = 0
    t_supd_prev = t_vinit
    for si in range(nt):
        t_ready = l1_proj_tiles(ctx, si, si * 512, do_q=True)
        p = 0
        kt_tm = ctx["kt_tm"][p]
        va_tm = ctx["va_tm"][p]
        p_q = ctx["p_q"][p]
        p_k = ctx["p_k"][p]

        t_qod_used = state.get(f"afm_used_{si - 1}", 0)
        t_kod_used = state.get(f"kod_used_{si - 1}", 0)

        def sy_shift(e, t_ready=t_ready, t_qod_used=t_qod_used,
                     t_kod_used=t_kod_used):
            e.wait_ge(vch.h, t_ready)
            wge(e, tch.h, max(t_qod_used, t_kod_used))
            e.dma_start(out=q_od[0:64, :], in_=p_q[64:128, :]).then_inc(ldsh.h, 16)
            e.dma_start(out=k_od[0:64, :], in_=p_k[64:128, :]).then_inc(ldsh.h, 16)
        P.sy(sy_shift)
        t_shift = ldsh.inc(32)

        def v_shift(e, t_shift=t_shift):
            e.wait_ge(ldsh.h, t_shift)
            e.engine_nop().then_inc(vch.h, 1)
        P.v(v_shift)
        t_shift_v = vch.inc(1)

        def ksl(h, cch):
            src = p_k if h % 2 == 0 else k_od
            return src[0:64, (h // 2) * 512 + cch * 128:(h // 2) * 512 + (cch + 1) * 128]

        def qsl(h, cch):
            src = p_q if h % 2 == 0 else q_od
            return src[0:64, (h // 2) * 512 + cch * 128:(h // 2) * 512 + (cch + 1) * 128]

        for cch in range(4):
            for g4 in range(4):
                bk2 = g4 % 2
                t_bankfree = bank_free[bk2]
                t_pt_free = state.get(f"pvmm_{si}_{cch}_{g4 - 2}", 0) if g4 >= 2 \
                    else state.get(f"pvmm_{si}_{cch - 1}_{g4 + 2}", 0)

                def t_scor(e, cch=cch, g4=g4, bk2=bk2, t_bankfree=t_bankfree,
                           t_sh=t_shift_v):
                    e.wait_ge(vch.h, max(t_sh, t_bankfree))
                    for hh4 in range(4):
                        h = g4 * 4 + hh4
                        ins = e.matmul(out=psAB[bk2][:, hh4 * 128:(hh4 + 1) * 128],
                                       lhsT=ksl(h, cch), rhs=qsl(h, cch),
                                       start=True, stop=True)
                    ins.then_inc(tch.h, 1)
                P.t(t_scor)
                t_sg = tch.inc(1)

                def v_mask(e, g4=g4, bk2=bk2, t_sg=t_sg, t_mask=t_mask,
                           t_pt_free=t_pt_free):
                    e.wait_ge(tch.h, t_sg)
                    wge(e, gch.h, t_mask)
                    wge(e, tch.h, t_pt_free)
                    e.scalar_tensor_tensor(out=pt_sb[g4 % 2], in0=psAB[bk2][:, 0:512],
                                           scalar=1.0, in1=mask4,
                                           op0=ALU.mult, op1=ALU.mult
                                           ).then_inc(vch.h, 1)
                P.v(v_mask)
                bank_free[bk2] = vch.inc(1)
                t_pt = bank_free[bk2]

                om = g4 % 2
                t_o4free = state.get(f"oevac_{si}_{cch}_{g4 - 2}", 0) if g4 >= 2 \
                    else state.get(f"oevac_{si}_{cch - 1}_{g4 + 2}", 0)

                def t_pv(e, cch=cch, g4=g4, om=om, t_pt=t_pt, t_o4free=t_o4free,
                         t_supd=t_supd_prev):
                    e.wait_ge(vch.h, max(t_pt, t_o4free, t_supd))
                    for hh4 in range(4):
                        h = g4 * 4 + hh4
                        pr = h // 2
                        e.matmul(out=psO4[om][:, hh4 * 65:(hh4 + 1) * 65],
                                 lhsT=pt_sb[g4 % 2][:, hh4 * 128:(hh4 + 1) * 128],
                                 rhs=va_tm[:, cch * 1040 + pr * 130 + (h % 2) * 65:
                                           cch * 1040 + pr * 130 + (h % 2 + 1) * 65],
                                 start=True, stop=False)
                        e.matmul(out=psO4[om][:, hh4 * 65:hh4 * 65 + 64],
                                 lhsT=qsl(h, cch),
                                 rhs=S_vb[0:64, h * 64:(h + 1) * 64],
                                 start=False, stop=False)
                        ins = e.matmul(out=psO4[om][:, hh4 * 65 + 64:hh4 * 65 + 65],
                                       lhsT=qsl(h, cch),
                                       rhs=S_kb[0:64, h:h + 1],
                                       start=False, stop=True)
                    ins.then_inc(tch.h, 1)
                P.t(t_pv)
                state[f"pvmm_{si}_{cch}_{g4}"] = tch.inc(1)
                t_pvm = state[f"pvmm_{si}_{cch}_{g4}"]

                t_och_free = state.get("t_och", 0)

                def v_oe(e, g4=g4, om=om, t_pvm=t_pvm, t_och_free=t_och_free):
                    e.wait_ge(tch.h, t_pvm)
                    wge(e, vch.h, t_och_free)
                    e.tensor_copy(out=och[:, 0:260], in_=psO4[om][:, 0:260]
                                  ).then_inc(vch.h, 1)
                P.v(v_oe)
                state[f"oevac_{si}_{cch}_{g4}"] = vch.inc(1)
                t_oe = state[f"oevac_{si}_{cch}_{g4}"]

                def v_eps(e, t_oe=t_oe):
                    e.wait_ge(vch.h, t_oe)
                    e.tensor_scalar_add(out=och[:, 64:260:65], in0=och[:, 64:260:65],
                                        scalar1=EPS).then_inc(vch.h, 1)
                P.v(v_eps)
                t_ep = vch.inc(1)

                def v_rec(e, t_ep=t_ep):
                    e.wait_ge(vch.h, t_ep)
                    e.reciprocal(out=rden4[:, 0:4], in_=och[:, 64:260:65]
                                 ).then_inc(vch.h, 1)
                P.v(v_rec)
                t_rc = vch.inc(1)

                t_abuf_used = state.get("abuf_used", 0)

                def v_div(e, cch=cch, g4=g4, t_rc=t_rc, t_abuf_used=t_abuf_used):
                    e.wait_ge(vch.h, t_rc)
                    wge(e, tch.h, t_abuf_used)
                    for hh4 in range(4):
                        h = g4 * 4 + hh4
                        ins = e.tensor_scalar_mul(
                            out=attn_buf[:, cch * 1024 + h * 64:cch * 1024 + (h + 1) * 64],
                            in0=och[:, hh4 * 65:hh4 * 65 + 64],
                            scalar1=rden4[:, hh4:hh4 + 1])
                    ins.then_inc(vch.h, 1)
                P.v(v_div)
                state["t_och"] = vch.inc(1)
            t_pv_all = state[f"pvmm_{si}_{cch}_{3}"]

            nsv = 2 if psSv2[0] is not psSv2[1] else 1
            for half in range(2):
                svb = half % nsv
                t_svfree = state.get(f"svev_{(si * 4 + cch) * 2 + half - nsv}", 0)

                def t_sv(e, cch=cch, half=half, svb=svb, t_ready=t_ready,
                         t_svfree=t_svfree, t_pv_all=t_pv_all):
                    e.wait_ge(vch.h, max(t_ready, t_svfree))
                    wge(e, tch.h, t_pv_all)
                    psSv = psSv2[svb]
                    for hh in range(8):
                        h = half * 8 + hh
                        pr = h // 2
                        ins = e.matmul(
                            out=psSv[:, hh * 64:(hh + 1) * 64],
                            lhsT=kt_tm[:, cch * 1024 + pr * 128 + (h % 2) * 64:
                                       cch * 1024 + pr * 128 + (h % 2 + 1) * 64],
                            rhs=va_tm[:, cch * 1040 + pr * 130 + (h % 2) * 65:
                                      cch * 1040 + pr * 130 + (h % 2) * 65 + 64],
                            start=True, stop=True)
                    ins.then_inc(tch.h, 1)
                P.t(t_sv)
                t_svm = tch.inc(1)

                def v_svacc(e, half=half, svb=svb, t_svm=t_svm):
                    e.wait_ge(tch.h, t_svm)
                    e.tensor_tensor(out=S_v[:, half * 512:(half + 1) * 512],
                                    in0=S_v[:, half * 512:(half + 1) * 512],
                                    in1=psSv2[svb][:, 0:512], op=ALU.add
                                    ).then_inc(vch.h, 1)
                P.v(v_svacc)
                state[f"svev_{(si * 4 + cch) * 2 + half}"] = vch.inc(1)
            t_sv_done = state[f"svev_{(si * 4 + cch) * 2 + 1}"]

            t_skfree = state.get(f"skev_{si * 4 + cch - 1}", 0)

            def t_sk(e, cch=cch, t_ready=t_ready, t_skfree=t_skfree,
                     t_pv_all=t_pv_all, t_vo=t_vinit):
                e.wait_ge(vch.h, max(t_ready, t_skfree, t_vo))
                wge(e, tch.h, t_pv_all)
                for h in range(16):
                    pr = h // 2
                    ins = e.matmul(
                        out=psSk[:, h:h + 1],
                        lhsT=kt_tm[:, cch * 1024 + pr * 128 + (h % 2) * 64:
                                   cch * 1024 + pr * 128 + (h % 2 + 1) * 64],
                        rhs=ones_col_bf[:], start=True, stop=True)
                ins.then_inc(tch.h, 1)
            P.t(t_sk)
            t_skm = tch.inc(1)

            def v_skacc(e, t_skm=t_skm):
                e.wait_ge(tch.h, t_skm)
                e.tensor_tensor(out=S_k[:], in0=S_k[:], in1=psSk[:, 0:16],
                                op=ALU.add).then_inc(vch.h, 1)
            P.v(v_skacc)
            state[f"skev_{si * 4 + cch}"] = vch.inc(1)
            t_sk_done = state[f"skev_{si * 4 + cch}"]

            def v_sbf(e, t_sv_done=t_sv_done, t_sk_done=t_sk_done):
                e.wait_ge(vch.h, max(t_sv_done, t_sk_done))
                e.tensor_copy(out=S_vb[:], in_=S_v[:])
                e.tensor_copy(out=S_kb[:], in_=S_k[:]).then_inc(vch.h, 1)
            P.v(v_sbf)
            t_supd_prev = vch.inc(1)

        t_attn_all = state["t_och"]
        for cch in range(4):
            for ech in range(NCH):
                bk2 = 0
                t_tfree = state.get(f"afm_ev_{(cch * NCH + ech) - 1}", 0)

                def t_at(e, cch=cch, ech=ech, bk2=bk2, t_attn_all=t_attn_all,
                         t_tfree=t_tfree, si=si):
                    e.wait_ge(vch.h, max(t_attn_all, t_tfree))
                    e.transpose(out=psT[bk2][:, 0:128],
                                in_=attn_buf[:, cch * 1024 + ech * 128:
                                             cch * 1024 + (ech + 1) * 128],
                                identity=ctx["ident_bf"][:]).then_inc(tch.h, 1)
                P.t(t_at)
                t_tg = tch.inc(1)

                def v_ae(e, cch=cch, ech=ech, bk2=bk2, t_tg=t_tg):
                    e.wait_ge(tch.h, t_tg)
                    e.tensor_copy(out=afm[:, ech * 512 + cch * 128:
                                          ech * 512 + (cch + 1) * 128],
                                  in_=psT[bk2][:, 0:128]).then_inc(vch.h, 1)
                P.v(v_ae)
                state[f"afm_ev_{cch * NCH + ech}"] = vch.inc(1)
        t_afm = state[f"afm_ev_{4 * NCH - 1}"]
        state["abuf_used"] = tch.n

        for tb in range(4):
            t_bf0 = bank_free[0]
            t_bf1 = bank_free[1]

            def t_om(e, tb=tb, t_afm=t_afm, t_wo=t_wo, t_bf0=t_bf0, t_bf1=t_bf1):
                e.wait_ge(vch.h, max(t_afm, t_wo, t_bf0, t_bf1))
                for ech in range(NCH):
                    lh = afm[:, ech * 512 + tb * 128:ech * 512 + (tb + 1) * 128]
                    e.matmul(out=psAB[0][:, 0:512], lhsT=lh,
                             rhs=wo_bf[:, ech * D:ech * D + 512],
                             start=(ech == 0), stop=(ech == NCH - 1))
                    ins = e.matmul(out=psAB[1][:, 0:512], lhsT=lh,
                                   rhs=wo_bf[:, ech * D + 512:ech * D + 1024],
                                   start=(ech == 0), stop=(ech == NCH - 1))
                ins.then_inc(tch.h, 1)
            P.t(t_om)
            t_omm = tch.inc(1)
            if tb == 3:
                state[f"afm_used_{si}"] = t_omm
                state[f"kod_used_{si}"] = state[f"pvmm_{si}_{3}_{3}"]

            def v_oev(e, tb=tb, t_omm=t_omm, t_bo=t_bobc_v, t_s3=s3_stored):
                e.wait_ge(tch.h, t_omm)
                wge(e, vch.h, t_bo)
                wge(e, st_c.h, t_s3)
                e.tensor_tensor(out=out2[:, tb * D:tb * D + 512],
                                in0=psAB[0][:, 0:512], in1=bo_bc[:, 0:512],
                                op=ALU.add)
                e.tensor_tensor(out=out2[:, tb * D + 512:(tb + 1) * D],
                                in0=psAB[1][:, 0:512], in1=bo_bc[:, 512:1024],
                                op=ALU.add).then_inc(vch.h, 1)
            P.v(v_oev)
            bank_free[0] = vch.inc(1)
            bank_free[1] = bank_free[0]
        t_out2 = bank_free[0]

        def v_pre(e, t_out2=t_out2):
            e.wait_ge(vch.h, t_out2)
        P.v(v_pre)
        t_ap2 = emit_ln(P, lnb, vch, sch, nblk=4,
                        src=out2, srcol=lambda j: j * D,
                        dst=out2, dstcol=lambda j: j * D)

        for j in range(4):
            px = j % 2
            lx = ldxr if px == 0 else ldxr1
            t_pfree = state.get(f"dres_{si}_{j - 2}", 0)

            def sy_xr(e, si=si, j=j, px=px, lx=lx, t_s3=s3_stored,
                      t_pfree=t_pfree):
                wge(e, st_c.h, t_s3)
                wge(e, vch.h, t_pfree)
                if si == 0 and j == 0:
                    wge(e, vch.h, t_bobc_v)
                e.dma_start(out=xres[:, px * D:(px + 1) * D],
                            in_=s2[si * 512 + j * 128:si * 512 + (j + 1) * 128, :]
                            ).then_inc(lx.h, 16)
            P.sy(sy_xr)
            t_xr = lx.inc(16)

            def v_res(e, j=j, px=px, lx=lx, t_ap2=t_ap2, t_xr=t_xr,
                      t_s3=s3_stored):
                if j == 0:
                    e.wait_ge(sch.h, t_ap2)
                    wge(e, st_c.h, t_s3)
                e.wait_ge(lx.h, t_xr)
                e.tensor_tensor(out=out2[:, j * D:(j + 1) * D],
                                in0=out2[:, j * D:(j + 1) * D],
                                in1=xres[:, px * D:(px + 1) * D],
                                op=ALU.add).then_inc(vch.h, 1)
            P.v(v_res)
            state[f"dres_{si}_{j}"] = vch.inc(1)
        t_rv = state[f"dres_{si}_{3}"]

        def sy_s3(e, si=si, t_rv=t_rv):
            e.wait_ge(vch.h, t_rv)
            e.dma_start(out=s3[si * 512:(si + 1) * 512, :].rearrange(
                "(j p) c -> p j c", p=128),
                in_=out2.rearrange("p (j c) -> p j c", j=4)).then_inc(st_c.h, 16)
        P.sy(sy_s3)
        s3_stored = st_c.inc(16)

    t_fin = s3_stored

    def sy_fin(e, t=t_fin):
        e.wait_ge(st_c.h, t)
    P.sy(sy_fin)

    if not standalone:
        finals = [(vch.h, vch.n), (sch.h, sch.n), (tch.h, tch.n),
                  (gch.h, gch.n), (st_c.h, st_c.n), (ldb.h, ldb.n),
                  (ldw.h, ldw.n), (ldxr.h, ldxr.n), (ldxr1.h, ldxr1.n),
                  (ldsi.h, ldsi.n), (ldsh.h, ldsh.n),
                  (ctx["ldx"][0].h, ctx["ldx"][0].n),
                  (ctx["ldx"][1].h, ctx["ldx"][1].n)]
        return finals

    block = es.enter_context(nc.Block())
    P.emit(block)
    es.close()
    return nc




def build_l1ffn(T_own, G=None, io=None):
    standalone = G is None
    if standalone:
        nc = bass.Bass("TRN2", target_bir_lowering=False)
        pre = ""
    else:
        nc = G["nc"]
        pre = "e_"
    io = io or {}

    def dram(name, shape, dt, kind):
        if name in io:
            return io[name]
        return nc.dram_tensor(name, shape, dt, kind=kind).ap()

    s3e = dram("s3e", [T_own + 1, D], F32, "ExternalInput")
    wrT = dram("wrT", [D, D], F32, "ExternalInput")
    wkT = dram("wkT", [D, D], F32, "ExternalInput")
    wvT = dram("wvT", [D, D], F32, "ExternalInput")
    mu_r = dram("mu_r", [D], F32, "ExternalInput")
    mu_k = dram("mu_k", [D], F32, "ExternalInput")
    s4 = dram("s4", [T_own, D], F32, "ExternalOutput")

    if standalone:
        es = contextlib.ExitStack()
        P = Prog(nc)
    else:
        es, P = G["es"], G["P"]
        P.barrier_step(G["entry"])
    vch = Ctr(es, nc, pre + "vch"); sch = Ctr(es, nc, pre + "sch")
    tch = Ctr(es, nc, pre + "tch"); gch = Ctr(es, nc, pre + "gch")
    ldw = Ctr(es, nc, pre + "ldw")
    ldx = [Ctr(es, nc, f"{pre}ldx{p}") for p in range(2)]
    ldb = Ctr(es, nc, pre + "ldb")
    ldxr = Ctr(es, nc, pre + "ldxr")
    st_c = Ctr(es, nc, pre + "st")

    if standalone:
        consts = build_consts(nc, es, P, gch)
        t_consts_h, t_consts = gch.h, consts["ready"]
    else:
        consts = G["consts"]
        t_consts_h, t_consts = G["consts_sem"], G["consts_ready"]
    ident_f32 = consts["ident_f32"]

    TP1 = T_own + 1
    WIN = 513
    if standalone:
        abf = Arena(nc, es, "abf", BF16, 46600)
    else:
        abf = G["abf"]
        abf.reset()

    wr_bf = abf.carve(NCH * D)
    wk_bf = abf.carve(NCH * D)
    wv_bf = abf.carve(NCH * D)
    ln4 = abf.carve(NCH * WIN)
    xr_fm = abf.carve(NCH * 512)
    xk_fm = abf.carve(NCH * 512)
    k2_fm = abf.carve(NCH * 512)
    r_tm = abf.carve(4 * 1024)

    if standalone:
        af32b = Arena(nc, es, "af32b", F32, 14200)
    else:
        af32b = G["af32"]
        af32b.reset()
    wstage = af32b.carve(NCH * D // 2)
    af32b.reset()
    xst1 = af32b.carve(5 * D)
    xst = [xst1, xst1]
    out2 = af32b.carve(4 * D)
    xres = af32b.carve(4 * D)
    tmpE = af32b.carve(512)

    lnb = (G["lnb"] if G is not None else mk_ln_bufs(nc, es, 5))
    mub = af32b.carve_high(2 * NCH)
    bstg = af32b.carve_high(128)[0:NCH, :]
    dbuf = abf.carve_high(512)

    if standalone:
        def v_veps_init(e):
            e.memset(lnb["veps"][:], 1.0).then_inc(vch.h, 1)
        P.v(v_veps_init)
        vch.inc(1)

    if standalone:
        psA = es.enter_context(nc.psum_tensor("psA", [128, 512], F32))[:]
        psB = es.enter_context(nc.psum_tensor("psB", [128, 512], F32))[:]
        psC = es.enter_context(nc.psum_tensor("psC", [128, 512], F32))[:]
        psD = es.enter_context(nc.psum_tensor("psD", [128, 512], F32))[:]
    else:
        psA, psB = G["psA"], G["psB"]
        psC = G["psO0"][:, 0:512]
        psD = G["psO0"][:, 512:1024]
    psAB = [psA, psB]
    psCD = [psC, psD]

    tw = load_weights_bf(P, wstage, [(wr_bf, wrT, D), (wk_bf, wkT, D),
                                     (wv_bf, wvT, D)], ldw, vch)
    t_wr, t_wk, t_wv = tw
    t_wdone = tw[-1]

    state = {}
    bank_free = {0: 0, 1: 0}
    t_prev = 0
    for (i, mu) in enumerate((mu_r, mu_k)):
        t_l = ldb.inc(16)

        def sy_mu(e, mu=mu, t_prev=t_prev):
            wge(e, vch.h, t_prev)
            e.dma_start(out=bstg[:], in_=mu.rearrange("(n p) -> n p", p=128)
                        ).then_inc(ldb.h, 16)
        P.sy(sy_mu)

        def t_mu(e, t_l=t_l):
            e.wait_ge(ldb.h, t_l)
            wge(e, t_consts_h, t_consts)
            e.transpose(out=psA[:, 0:NCH], in_=bstg[:],
                        identity=ident_f32[0:NCH, 0:NCH]).then_inc(tch.h, 1)
        P.t(t_mu)
        t_t = tch.inc(1)

        def v_mu(e, i=i, t_t=t_t):
            e.wait_ge(tch.h, t_t)
            e.tensor_copy(out=mub[:, i * NCH:(i + 1) * NCH],
                          in_=psA[:, 0:NCH]).then_inc(vch.h, 1)
        P.v(v_mu)
        t_prev = vch.inc(1)
    t_mu_done = t_prev
    bank_free[0] = t_mu_done

    s4_stored = 0
    apply_done = {}
    for si in range(T_own // 512):
        t0 = si * 512
        p = 0
        blocks = [(j * D, j * D, 128) for j in range(4)] + [(4 * D, 4 * D, 1)]
        t_apfree = apply_done.get(si - 1, 0)
        t_tp_prev = state.get(f"etp_{si - 1}", 0)

        def sy_ld(e, t0=t0, t_apfree=t_apfree, t_tp_prev=t_tp_prev, si=si):
            if si == 0:
                wge(e, vch.h, t_wdone)
            wge(e, sch.h, t_apfree)
            wge(e, tch.h, t_tp_prev)
            e.dma_start(out=xst[0].rearrange("p (j c) -> p j c", j=5)[:, 0:4, :],
                        in_=s3e[t0:t0 + 512, :].rearrange(
                            "(j p) c -> p j c", p=128)).then_inc(ldx[0].h, 16)
            e.dma_start(out=xst[0][0:1, 4 * D:5 * D],
                        in_=s3e[t0 + 512:t0 + 513, :]).then_inc(ldx[0].h, 16)
        P.sy(sy_ld)
        t_ldx = ldx[0].inc(32)

        def v_pre0(e, t_ldx=t_ldx):
            e.wait_ge(ldx[0].h, t_ldx)
        P.v(v_pre0)
        t_apply = emit_ln_blocks(P, lnb, vch, sch, src=xst[0], dst=xst[0],
                                 blocks=blocks, war_s=(tch.h, t_tp_prev))
        apply_done[si] = t_apply

        t_mix_used_prev = state.get(f"emix_used_{si - 1}", 0)
        for j in range(5):
            rr = 128 if j < 4 else 1
            for h in range(2):
                bk = (j * 2 + h) % 2
                t_bankfree = bank_free[bk]

                def t_tp(e, j=j, h=h, rr=rr, bk=bk, t_bankfree=t_bankfree,
                         t_ap=t_apply):
                    e.wait_ge(sch.h, t_ap)
                    wge(e, t_consts_h, t_consts)
                    wge(e, vch.h, t_bankfree)
                    for cc in range(4):
                        c = 4 * h + cc
                        ins = e.transpose(
                            out=psAB[bk][:, cc * 128:cc * 128 + rr],
                            in_=xst[0][0:rr, j * D + c * 128:j * D + (c + 1) * 128],
                            identity=ident_f32[0:rr, 0:rr])
                    ins.then_inc(tch.h, 1)
                P.t(t_tp)
                t_tpg = tch.inc(1)

                def v_ev(e, j=j, h=h, rr=rr, bk=bk, t_tpg=t_tpg,
                         t_mu2=t_mix_used_prev):
                    e.wait_ge(tch.h, t_tpg)
                    wge(e, tch.h, t_mu2)
                    dst = ln4.rearrange("p (c t) -> p c t", c=NCH)
                    e.tensor_copy(out=dst[:, 4 * h:4 * h + 4, j * 128:j * 128 + rr],
                                  in_=psAB[bk][:].rearrange("p (c t) -> p c t", c=4)[:, :, 0:rr]
                                  ).then_inc(vch.h, 1)
                P.v(v_ev)
                bank_free[bk] = vch.inc(1)
        state[f"etp_{si}"] = tch.n
        t_ln4 = vch.n

        t_xr_prev = state.get(f"emix_used_{si - 1}", 0)
        for c in range(NCH):
            t_dprev = state.get(f"emix_{si}_{c - 1}", 0)

            def v_d(e, c=c, t_ln4=t_ln4, t_xr_prev=t_xr_prev, t_dprev=t_dprev):
                e.wait_ge(vch.h, max(t_ln4, t_dprev))
                wge(e, tch.h, t_xr_prev)
                e.tensor_tensor(out=dbuf[:],
                                in0=ln4[:, c * WIN + 1:c * WIN + 513],
                                in1=ln4[:, c * WIN:c * WIN + 512],
                                op=ALU.subtract).then_inc(vch.h, 1)
            P.v(v_d)
            t_d = vch.inc(1)

            def v_xr(e, c=c, t_d=t_d, t_mu=t_mu_done):
                e.wait_ge(vch.h, max(t_d, t_mu))
                e.scalar_tensor_tensor(out=xr_fm[:, c * 512:(c + 1) * 512],
                                       in0=dbuf[:], scalar=mub[:, c:c + 1],
                                       in1=ln4[:, c * WIN:c * WIN + 512],
                                       op0=ALU.mult, op1=ALU.add)
                e.scalar_tensor_tensor(out=xk_fm[:, c * 512:(c + 1) * 512],
                                       in0=dbuf[:], scalar=mub[:, NCH + c:NCH + c + 1],
                                       in1=ln4[:, c * WIN:c * WIN + 512],
                                       op0=ALU.mult, op1=ALU.add).then_inc(vch.h, 1)
            P.v(v_xr)
            state[f"emix_{si}_{c}"] = vch.inc(1)
        t_mix = state[f"emix_{si}_{NCH - 1}"]

        for tb in range(4):
            bk = tb % 2
            t_bankfree = bank_free[bk]
            t_rtm_used = state.get(f"ekv_ev_{si - 1}_{tb}", 0)

            def t_wr_mm(e, tb=tb, bk=bk, t_mix=t_mix, t_w=t_wr,
                        t_bankfree=t_bankfree):
                e.wait_ge(vch.h, max(t_mix, t_w, t_bankfree))
                for c in range(NCH):
                    e.matmul(out=psAB[bk][:, 0:512],
                             lhsT=xr_fm[:, c * 512 + tb * 128:c * 512 + (tb + 1) * 128],
                             rhs=wr_bf[:, c * D:c * D + 512],
                             start=(c == 0), stop=(c == NCH - 1))
                    ins = e.matmul(out=psCD[bk][:, 0:512],
                                   lhsT=xr_fm[:, c * 512 + tb * 128:c * 512 + (tb + 1) * 128],
                                   rhs=wr_bf[:, c * D + 512:c * D + 1024],
                                   start=(c == 0), stop=(c == NCH - 1))
                ins.then_inc(tch.h, 1)
            P.t(t_wr_mm)
            t_mm = tch.inc(1)

            def s_sig(e, tb=tb, bk=bk, t_mm=t_mm, t_rtm_used=t_rtm_used):
                e.wait_ge(tch.h, t_mm)
                wge(e, vch.h, t_rtm_used)
                e.activation(out=r_tm[:, tb * D:tb * D + 512], in_=psAB[bk][:, 0:512],
                             func=AF.Sigmoid)
                e.activation(out=r_tm[:, tb * D + 512:(tb + 1) * D],
                             in_=psCD[bk][:, 0:512],
                             func=AF.Sigmoid).then_inc(sch.h, 1)
            P.s(s_sig)
            t_sg = sch.inc(1)

            def v_free(e, t_sg=t_sg):
                e.wait_ge(sch.h, t_sg)
                e.engine_nop().then_inc(vch.h, 1)
            P.v(v_free)
            bank_free[bk] = vch.inc(1)
            state[f"er_{si}_{tb}"] = bank_free[bk]
        t_r_all_s = sch.n

        for ech in range(NCH):
            bk = ech % 2
            t_bankfree = bank_free[bk]
            t_tmp_used = state.get("ek2_last", 0)

            def t_wk_mm(e, ech=ech, bk=bk, t_mix=t_mix, t_w=t_wk,
                        t_bankfree=t_bankfree):
                e.wait_ge(vch.h, max(t_mix, t_w, t_bankfree))
                for c in range(NCH):
                    ins = e.matmul(out=psAB[bk][:, 0:512],
                                   lhsT=wk_bf[:, c * D + ech * 128:c * D + (ech + 1) * 128],
                                   rhs=xk_fm[:, c * 512:(c + 1) * 512],
                                   start=(c == 0), stop=(c == NCH - 1))
                ins.then_inc(tch.h, 1)
            P.t(t_wk_mm)
            t_mm = tch.inc(1)

            def v_relu(e, ech=ech, bk=bk, t_mm=t_mm, t_tmp_used=t_tmp_used):
                e.wait_ge(tch.h, t_mm)
                wge(e, vch.h, t_tmp_used)
                e.tensor_scalar_max(out=tmpE[:, 0:512], in0=psAB[bk][:, 0:512],
                                    scalar1=0.0).then_inc(vch.h, 1)
            P.v(v_relu)
            t_rl = vch.inc(1)
            bank_free[bk] = t_rl

            def v_sq(e, ech=ech, t_rl=t_rl):
                e.wait_ge(vch.h, t_rl)
                e.tensor_tensor(out=k2_fm[:, ech * 512:(ech + 1) * 512],
                                in0=tmpE[:, 0:512], in1=tmpE[:, 0:512],
                                op=ALU.mult).then_inc(vch.h, 1)
            P.v(v_sq)
            state["ek2_last"] = vch.inc(1)
        t_k2 = state["ek2_last"]

        for tb in range(4):
            bk = tb % 2
            t_bankfree = bank_free[bk]

            def t_wv_mm(e, tb=tb, bk=bk, t_k2=t_k2, t_w=t_wv,
                        t_bankfree=t_bankfree):
                e.wait_ge(vch.h, max(t_k2, t_w, t_bankfree))
                for ech in range(NCH):
                    e.matmul(out=psAB[bk][:, 0:512],
                             lhsT=k2_fm[:, ech * 512 + tb * 128:ech * 512 + (tb + 1) * 128],
                             rhs=wv_bf[:, ech * D:ech * D + 512],
                             start=(ech == 0), stop=(ech == NCH - 1))
                    ins = e.matmul(out=psCD[bk][:, 0:512],
                                   lhsT=k2_fm[:, ech * 512 + tb * 128:ech * 512 + (tb + 1) * 128],
                                   rhs=wv_bf[:, ech * D + 512:ech * D + 1024],
                                   start=(ech == 0), stop=(ech == NCH - 1))
                ins.then_inc(tch.h, 1)
            P.t(t_wv_mm)
            t_mm = tch.inc(1)
            state[f"ekv_ev_{si}_{tb}"] = t_mm

            def v_kv(e, tb=tb, bk=bk, t_mm=t_mm, t_rs=t_r_all_s,
                     t_s4=s4_stored):
                e.wait_ge(tch.h, t_mm)
                e.wait_ge(sch.h, t_rs)
                wge(e, st_c.h, t_s4)
                e.tensor_tensor(out=out2[:, tb * D:tb * D + 512],
                                in0=psAB[bk][:, 0:512],
                                in1=r_tm[:, tb * D:tb * D + 512], op=ALU.mult)
                e.tensor_tensor(out=out2[:, tb * D + 512:(tb + 1) * D],
                                in0=psCD[bk][:, 0:512],
                                in1=r_tm[:, tb * D + 512:(tb + 1) * D],
                                op=ALU.mult).then_inc(vch.h, 1)
            P.v(v_kv)
            bank_free[bk] = vch.inc(1)
        t_out2 = bank_free[(4 - 1) % 2]

        def v_pre2(e, t_out2=t_out2, t_o2=bank_free[0]):
            e.wait_ge(vch.h, max(t_out2, t_o2))
        P.v(v_pre2)
        t_ap2 = emit_ln(P, lnb, vch, sch, nblk=4,
                        src=out2, srcol=lambda j: j * D,
                        dst=out2, dstcol=lambda j: j * D)

        def sy_xr(e, t0=t0, t_s4=s4_stored):
            wge(e, st_c.h, t_s4)
            e.dma_start(out=xres.rearrange("p (j c) -> p j c", j=4),
                        in_=s3e[t0 + 1:t0 + 513, :].rearrange(
                            "(j p) c -> p j c", p=128)).then_inc(ldxr.h, 16)
        P.sy(sy_xr)
        t_xr = ldxr.inc(16)

        def v_res(e, t_ap2=t_ap2, t_xr=t_xr, t_s4=s4_stored):
            e.wait_ge(sch.h, t_ap2)
            e.wait_ge(ldxr.h, t_xr)
            wge(e, st_c.h, t_s4)
            e.tensor_tensor(out=out2[:, 0:4 * D], in0=out2[:, 0:4 * D],
                            in1=xres[:, 0:4 * D], op=ALU.add).then_inc(vch.h, 1)
        P.v(v_res)
        t_rv = vch.inc(1)

        def sy_s4(e, t0=t0, t_rv=t_rv):
            e.wait_ge(vch.h, t_rv)
            e.dma_start(out=s4[t0:t0 + 512, :].rearrange("(j p) c -> p j c", p=128),
                        in_=out2.rearrange("p (j c) -> p j c", j=4)
                        ).then_inc(st_c.h, 16)
        P.sy(sy_s4)
        s4_stored = st_c.inc(16)
        state[f"emix_used_{si}"] = state[f"ekv_ev_{si}_{3}"]

    def sy_fin(e, t=s4_stored):
        e.wait_ge(st_c.h, t)
    P.sy(sy_fin)

    if not standalone:
        finals = [(vch.h, vch.n), (sch.h, sch.n), (tch.h, tch.n),
                  (gch.h, gch.n), (st_c.h, st_c.n), (ldb.h, ldb.n),
                  (ldw.h, ldw.n), (ldxr.h, ldxr.n),
                  (ldx[0].h, ldx[0].n), (ldx[1].h, ldx[1].n)]
        return finals

    block = es.enter_context(nc.Block())
    P.emit(block)
    es.close()
    return nc



def _emit_exchange(G, tag, contribs, cc_in, cc_out, outs, entry):
    nc, es, P = G["nc"], G["es"], G["P"]
    P.barrier_step(entry)
    af32 = G["af32"]
    af32.reset()
    n = cc_in.shape[-1]
    nrow = len(contribs)
    stg = af32.carve(n)
    vch = Ctr(es, nc, f"{tag}_vch")
    ld = Ctr(es, nc, f"{tag}_ld")
    ccs = Ctr(es, nc, f"{tag}_ccs")
    wr = Ctr(es, nc, f"{tag}_wr")

    for i, (src, mask) in enumerate(contribs):
        t_l = ld.inc(16)
        t_wprev = wr.n
        t_vprev = vch.n

        def sy_l(e, src=src, t_wprev=t_wprev, t_vprev=t_vprev):
            wge(e, vch.h, t_vprev)
            wge(e, wr.h, t_wprev)
            e.dma_start(out=stg[0:1, 0:n], in_=src).then_inc(ld.h, 16)
        P.sy(sy_l)
        t_m = vch.inc(1)

        def v_m(e, mask=mask, t_l=t_l):
            e.wait_ge(ld.h, t_l)
            e.tensor_scalar_mul(out=stg[0:1, 0:n], in0=stg[0:1, 0:n],
                                scalar1=mask).then_inc(vch.h, 1)
        P.v(v_m)
        t_w = wr.inc(16)

        def sy_w(e, i=i, t_m=t_m):
            e.wait_ge(vch.h, t_m)
            e.dma_start(out=cc_in[i:i + 1, :], in_=stg[0:1, 0:n]).then_inc(wr.h, 16)
        P.sy(sy_w)

    def g_cc(e, t_w=wr.n):
        e.wait_ge(wr.h, t_w)
        e.collective_compute(
            "AllReduce", ALU.add,
            replica_groups=[[0, 1], [2, 3], [4, 5], [6, 7]],
            ins=[cc_in.opt()], outs=[cc_out.opt()]).then_inc(ccs.h, 1)
    P.g(g_cc)
    t_cc = ccs.inc(1)

    for (row, mask, dst) in outs:
        t_l = ld.inc(16)
        t_wprev = wr.n
        t_vprev = vch.n

        def sy_l2(e, row=row, t_cc=t_cc, t_vprev=t_vprev, t_wprev=t_wprev):
            e.wait_ge(ccs.h, t_cc)
            wge(e, vch.h, t_vprev)
            wge(e, wr.h, t_wprev)
            e.dma_start(out=stg[0:1, 0:n], in_=cc_out[row:row + 1, :]
                        ).then_inc(ld.h, 16)
        P.sy(sy_l2)
        t_m = vch.inc(1)

        def v_m2(e, mask=mask, t_l=t_l):
            e.wait_ge(ld.h, t_l)
            e.tensor_scalar_mul(out=stg[0:1, 0:n], in0=stg[0:1, 0:n],
                                scalar1=mask).then_inc(vch.h, 1)
        P.v(v_m2)
        t_w = wr.inc(16)

        def sy_w2(e, dst=dst, t_m=t_m):
            e.wait_ge(vch.h, t_m)
            e.dma_start(out=dst, in_=stg[0:1, 0:n]).then_inc(wr.h, 16)
        P.sy(sy_w2)

    def sy_waitw(e, t=wr.n):
        e.wait_ge(wr.h, t)
    P.sy(sy_waitw)
    return [(vch.h, vch.n), (ld.h, ld.n), (ccs.h, ccs.n), (wr.h, wr.n)]


def _emit_state_exchange(G, stloc, cc_in, cc_out, state_masked,
                         m_even_col, m_odd_col, entry):
    nc, es, P = G["nc"], G["es"], G["P"]
    P.barrier_step(entry)
    af32 = G["af32"]
    af32.reset()
    stg = af32.carve(520)
    vch = Ctr(es, nc, "x2_vch")
    ld = Ctr(es, nc, "x2_ld")
    ccs = Ctr(es, nc, "x2_ccs")
    wr = Ctr(es, nc, "x2_wr")

    t_l0 = ld.inc(16)

    def sy_l0(e):
        e.dma_start(out=stg[:, 0:520].rearrange("p (n m) -> p n m", n=NCH),
                    in_=stloc.rearrange("(n p) m -> p n m", p=128)
                    ).then_inc(ld.h, 16)
    P.sy(sy_l0)
    t_m = vch.inc(1)

    def v_m(e, t_l0=t_l0):
        e.wait_ge(ld.h, t_l0)
        e.tensor_scalar_mul(out=stg[:, 0:520], in0=stg[:, 0:520],
                            scalar1=m_even_col).then_inc(vch.h, 1)
    P.v(v_m)
    t_w1 = wr.inc(16)

    def sy_w(e, t_m=t_m):
        e.wait_ge(vch.h, t_m)
        e.dma_start(out=cc_in.rearrange("(n p) m -> p n m", p=128),
                    in_=stg[:, 0:520].rearrange("p (n m) -> p n m", n=NCH)
                    ).then_inc(wr.h, 16)
    P.sy(sy_w)

    def g_cc(e, t_w1=t_w1):
        e.wait_ge(wr.h, t_w1)
        e.collective_compute(
            "AllReduce", ALU.add,
            replica_groups=[[0, 1], [2, 3], [4, 5], [6, 7]],
            ins=[cc_in.opt()], outs=[cc_out.opt()]).then_inc(ccs.h, 1)
    P.g(g_cc)
    t_cc = ccs.inc(1)
    t_l = ld.inc(16)

    def sy_l(e, t_cc=t_cc):
        e.wait_ge(ccs.h, t_cc)
        e.dma_start(out=stg[:, 0:520].rearrange("p (n m) -> p n m", n=NCH),
                    in_=cc_out.rearrange("(n p) m -> p n m", p=128)
                    ).then_inc(ld.h, 16)
    P.sy(sy_l)
    t_m2 = vch.inc(1)

    def v_m2(e, t_l=t_l):
        e.wait_ge(ld.h, t_l)
        e.tensor_scalar_mul(out=stg[:, 0:520], in0=stg[:, 0:520],
                            scalar1=m_odd_col).then_inc(vch.h, 1)
    P.v(v_m2)
    t_w2 = wr.inc(16)

    def sy_w2(e, t_m2=t_m2):
        e.wait_ge(vch.h, t_m2)
        e.dma_start(out=state_masked.rearrange("(n p) m -> p n m", p=128),
                    in_=stg[:, 0:520].rearrange("p (n m) -> p n m", n=NCH)
                    ).then_inc(wr.h, 16)
        e.wait_ge(wr.h, t_w2)
    P.sy(sy_w2)
    return [(vch.h, vch.n), (ld.h, ld.n), (ccs.h, ccs.n), (wr.h, wr.n)]


def build_fused(T_own, T_full, QG=512, KSC=512, ST=512):
    nc = bass.Bass("TRN2", target_bir_lowering=False)
    es = contextlib.ExitStack()
    P = Prog(nc)

    def din(name, shape):
        return nc.dram_tensor(name, shape, F32, kind="ExternalInput").ap()

    x_full = din("x_full", [T_full, D])
    x_own = din("x_own", [T_own, D])
    wqT0 = din("wqT0", [D, D]); wkT0 = din("wkT0", [D, D]); wvT0 = din("wvT0", [D, D])
    w1t = din("w1t", [D, 4096]); cw = din("cw", [4096, 3]); cb = din("cb", [4096])
    w2t = din("w2t", [4096, D])
    wqT1 = din("wqT1", [D, D]); wkT1 = din("wkT1", [D, D]); wvT1 = din("wvT1", [D, D])
    woT1 = din("woT1", [D, D])
    bq1 = din("bq1", [D]); bk1 = din("bk1", [D]); bv1 = din("bv1", [D]); bo1 = din("bo1", [D])
    wrT2 = din("wrT2", [D, D]); wkT2 = din("wkT2", [D, D]); wvT2 = din("wvT2", [D, D])
    mu_r = din("mu_r", [D]); mu_k = din("mu_k", [D])
    m_even = din("m_even", [128, 1])
    m_odd = din("m_odd", [128, 1])
    s4 = nc.dram_tensor("s4", [T_own, D], F32, kind="ExternalOutput").ap()

    s1_int = nc.dram_tensor("s1_int", [T_own + 2, D], F32, kind="Internal").ap()
    s2_int = nc.dram_tensor("s2_int", [T_own, D], F32, kind="Internal").ap()
    s3_int = nc.dram_tensor("s3_int", [T_own + 1, D], F32, kind="Internal").ap()
    stloc = nc.dram_tensor("stloc", [D, 65], F32, kind="Internal").ap()
    stmask = nc.dram_tensor("stmask", [D, 65], F32, kind="Internal").ap()
    cc1_in = nc.dram_tensor("cc1_in", [2, D], F32, kind="Internal").ap()
    cc1_out = nc.dram_tensor("cc1_out", [2, D], F32, kind="Internal").ap()
    cc2_in = nc.dram_tensor("cc2_in", [D, 65], F32, kind="Internal").ap()
    cc2_out = nc.dram_tensor("cc2_out", [D, 65], F32, kind="Internal").ap()
    cc3_in = nc.dram_tensor("cc3_in", [1, D], F32, kind="Internal").ap()
    cc3_out = nc.dram_tensor("cc3_out", [1, D], F32, kind="Internal").ap()

    kgch = Ctr(es, nc, "k_gch")
    consts = build_consts(nc, es, P, kgch)
    abf = Arena(nc, es, "abf", BF16, 66000)
    af32 = Arena(nc, es, "af32", F32, 19744)
    psA = es.enter_context(nc.psum_tensor("psA", [128, 512], F32))[:]
    psB = es.enter_context(nc.psum_tensor("psB", [128, 512], F32))[:]
    psO0 = es.enter_context(nc.psum_tensor("psO0", [128, 1536], F32))[:]
    psO1 = es.enter_context(nc.psum_tensor("psO1", [128, 1536], F32))[:]
    lnb = mk_ln_bufs(nc, es, 5)
    msk_sb = es.enter_context(nc.sbuf_tensor("msk_sb", [128, 2], F32))
    ldm = Ctr(es, nc, "k_ldm")

    def sy_msk(e):
        e.dma_start(out=msk_sb[:, 0:1], in_=m_even).then_inc(ldm.h, 16)
        e.dma_start(out=msk_sb[:, 1:2], in_=m_odd).then_inc(ldm.h, 16)
    P.sy(sy_msk)
    t_msk = ldm.inc(32)

    def v_lninit(e):
        e.wait_ge(ldm.h, t_msk)
        e.memset(lnb["veps"][:], 1.0).then_inc(ldm.h, 16)
    P.v(v_lninit)
    t_lninit = ldm.inc(16)

    G = dict(nc=nc, es=es, P=P, consts=consts, consts_sem=kgch.h,
             consts_ready=kgch.n, abf=abf, af32=af32,
             psA=psA, psB=psB, psO0=psO0, psO1=psO1, lnb=lnb)

    G["entry"] = []
    fin_a = build_l0attn(T_own, T_full, QG=QG, KSC=KSC, G=G, io=dict(
        x_full=x_full, x_own=x_own, wqT=wqT0, wkT=wkT0, wvT=wvT0,
        s1=s1_int[1:T_own + 1, :]))

    mev1 = msk_sb[0:1, 0:1]
    mod1 = msk_sb[0:1, 1:2]
    fin_x1 = _emit_exchange(
        G, "x1",
        contribs=[(s1_int[T_own:T_own + 1, :], mev1),
                  (s1_int[1:2, :], mod1)],
        cc_in=cc1_in, cc_out=cc1_out,
        outs=[(0, mod1, s1_int[0:1, :]),
              (1, mev1, s1_int[T_own + 1:T_own + 2, :])],
        entry=fin_a + [(ldm.h, t_msk)])

    G["entry"] = fin_a + fin_x1 + [(ldm.h, t_lninit)]
    fin_b = build_l0ffn(T_own, ST=ST, G=G, io=dict(
        s1e=s1_int, w1t=w1t, cw=cw, cb=cb, w2t=w2t, s2=s2_int))

    G["entry"] = fin_b
    fin_c = build_l1state(T_own, G=G, io=dict(
        s2=s2_int, wkT=wkT1, wvT=wvT1, bk=bk1, bv=bv1, state_loc=stloc))

    fin_x2 = _emit_state_exchange(
        G, stloc, cc2_in, cc2_out, stmask, msk_sb[:, 0:1], msk_sb[:, 1:2],
        entry=fin_c + [(ldm.h, t_msk)])

    G["entry"] = fin_c + fin_x2
    fin_d = build_l1attn(T_own, G=G, io=dict(
        s2=s2_int, wqT=wqT1, wkT=wkT1, wvT=wvT1, woT=woT1,
        bq=bq1, bk=bk1, bv=bv1, bo=bo1,
        state_full=stmask,
        s3=s3_int[1:T_own + 1, :]))

    fin_x3 = _emit_exchange(
        G, "x3",
        contribs=[(s3_int[T_own:T_own + 1, :], mev1)],
        cc_in=cc3_in, cc_out=cc3_out,
        outs=[(0, mod1, s3_int[0:1, :])],
        entry=fin_d + [(ldm.h, t_msk)])

    G["entry"] = fin_d + fin_x3 + [(ldm.h, t_lninit)]
    fin_e = build_l1ffn(T_own, G=G, io=dict(
        s3e=s3_int, wrT=wrT2, wkT=wkT2, wvT=wvT2, mu_r=mu_r, mu_k=mu_k,
        s4=s4))

    block = es.enter_context(nc.Block())
    P.emit(block)
    es.close()
    return nc



_B, _T, _DM = 4, 8192, 1024
_TO = _T // 2

_cache = {}


def _fused_prog():
    if "F" not in _cache:
        _cache["F"] = build_fused(_TO, _T)
    return _cache["F"]


def _tr(w):
    return np.ascontiguousarray(np.asarray(w).T.astype(np.float32))


def kernel(L0_attn_x, L0_attn_Wq, L0_attn_Wk, L0_attn_Wv,
           L0_ffn_w1, L0_ffn_conv_w, L0_ffn_conv_b, L0_ffn_w2,
           L1_attn_Wq, L1_attn_bq, L1_attn_Wk, L1_attn_bk, L1_attn_Wv,
           L1_attn_bv, L1_attn_Wo, L1_attn_bo,
           L1_ffn_W_r, L1_ffn_W_k, L1_ffn_W_v, L1_ffn_mu_r, L1_ffn_mu_k):
    prog = _fused_prog()
    x = np.ascontiguousarray(np.asarray(L0_attn_x, dtype=np.float32))
    cores = list(range(8))
    f32 = lambda a: np.ascontiguousarray(np.asarray(a, dtype=np.float32))
    base = dict(
        wqT0=_tr(L0_attn_Wq), wkT0=_tr(L0_attn_Wk), wvT0=_tr(L0_attn_Wv),
        w1t=_tr(L0_ffn_w1),
        cw=f32(np.asarray(L0_ffn_conv_w).reshape(4096, 3)),
        cb=f32(L0_ffn_conv_b), w2t=_tr(L0_ffn_w2),
        wqT1=_tr(L1_attn_Wq), wkT1=_tr(L1_attn_Wk), wvT1=_tr(L1_attn_Wv),
        woT1=_tr(L1_attn_Wo),
        bq1=f32(L1_attn_bq), bk1=f32(L1_attn_bk), bv1=f32(L1_attn_bv),
        bo1=f32(L1_attn_bo),
        wrT2=_tr(L1_ffn_W_r), wkT2=_tr(L1_ffn_W_k), wvT2=_tr(L1_ffn_W_v),
        mu_r=f32(L1_ffn_mu_r), mu_k=f32(L1_ffn_mu_k),
    )
    in_maps = []
    for c in cores:
        b, h = c // 2, c % 2
        m = dict(base)
        m["x_full"] = x[b]
        m["x_own"] = x[b, h * _TO:(h + 1) * _TO]
        m["m_even"] = np.full((128, 1), 1.0 - h, np.float32)
        m["m_odd"] = np.full((128, 1), float(h), np.float32)
        in_maps.append(m)
    res = run_bass_kernel_spmd(prog, in_maps, core_ids=cores)
    out = np.empty((_B, _T, _DM), np.float32)
    for c in cores:
        b, h = c // 2, c % 2
        out[b, h * _TO:(h + 1) * _TO] = np.asarray(res.results[c]["s4"])
    return out
